# version 06
# baseline (speedup 1.0000x reference)
import numpy as np
import ml_dtypes
import concourse.bass as bass
import concourse.mybir as mybir
from concourse.bass_utils import run_bass_kernel_spmd
from concourse.bass import IndirectOffsetOnAxis

F32 = mybir.dt.float32
BF16 = mybir.dt.bfloat16
U32 = mybir.dt.uint32
U8 = mybir.dt.uint8
AF = mybir.ActivationFunctionType
ALU = mybir.AluOpType
AX = mybir.AxisListType
BF = ml_dtypes.bfloat16

NCORES = 8
BPC = 4
T = 2048
D = 1024
NT = T // 128
H0 = 16
CS = 16
K0 = 256
E0 = 8
HP = 128
E1 = 2048
S1 = 128
L1L = 512
DFF = 4096
CAP = 10240
LN_EPS = 1e-5

_LAST_PHASE = "ALL"
_DEBUG_TAPS = []



def _host_constants():
    c = {}
    c["ident_bf"] = np.eye(128, dtype=BF)
    c["ident_f"] = np.eye(128, dtype=np.float32)
    c["stril_bf"] = np.tril(np.ones((128, 128), dtype=np.float32), -1).astype(BF)
    c["ones1_f"] = np.ones((1, 128), dtype=np.float32)
    c["ones1_bf"] = np.ones((1, 128), dtype=BF)
    c["iota_p"] = np.arange(128, dtype=np.float32).reshape(128, 1)
    c["iota_r"] = np.broadcast_to(np.arange(K0, dtype=np.float32), (128, K0)).copy()
    c["iota8"] = np.broadcast_to(np.arange(8, dtype=np.float32), (128, 8)).copy()
    s8 = np.zeros((8, 24), dtype=np.float32)
    s8[:, 0:8] = np.triu(np.ones((8, 8)), 1)
    s8[:, 8:16] = np.eye(8)
    s8[:, 16] = 1.0
    c["stril8f"] = s8
    c["onecol_bf"] = np.ones((128, 1), dtype=BF)
    sm = np.zeros((128, 32), dtype=np.float32)
    for ch in range(16):
        sm[:, 2 * ch] = ch
        sm[:, 2 * ch + 1] = np.arange(128)
    c["smalls"] = sm.astype(BF)
    import jax
    import jax.numpy as jnp
    with jax.default_device(jax.devices("cpu")[0]):
        half = 32
        inv_freq = 1.0 / 10000.0 ** (jnp.arange(half, dtype=jnp.float32) * 2.0 / 64.0)
        pos = jnp.arange(T, dtype=jnp.float32)
        ang = np.asarray(pos[:, None] * inv_freq[None, :])
        half1 = 64
        inv1 = 10000.0 ** (jnp.arange(half1, dtype=jnp.float32) / half1)
        ang1 = np.asarray(pos[:, None] * inv1[None, :])
    c["postable"] = np.concatenate([
        np.sin(ang.astype(np.float64)),
        np.cos(ang.astype(np.float64)),
    ], axis=1).astype(np.float32)
    c["sinT1"] = np.ascontiguousarray(np.sin(ang1.astype(np.float64)).astype(np.float32).T)
    c["cosT1"] = np.ascontiguousarray(np.cos(ang1.astype(np.float64)).astype(np.float32).T)
    return c


def _host_prep(inputs):
    f32 = np.float32
    cst = _host_constants()
    x = np.ascontiguousarray(np.asarray(inputs["L0_attn_x"], f32))
    W_inT = np.ascontiguousarray(np.asarray(inputs["L0_attn_W_in"], f32).T)
    W_gateT = np.ascontiguousarray(np.asarray(inputs["L0_attn_W_gate"], f32).T)
    W_outT = np.ascontiguousarray(np.asarray(inputs["L0_attn_W_out"], f32).T)
    Wr0T = np.ascontiguousarray(np.asarray(inputs["L0_ffn_Wr"], f32).T)
    uv_wT = np.ascontiguousarray(np.asarray(inputs["L1_attn_uv_w"], f32).T)
    o_wT = np.ascontiguousarray(np.asarray(inputs["L1_attn_o_w"], f32).T)
    w1 = np.asarray(inputs["L1_attn_w"], f32)
    ii = np.arange(T)
    rel = ii[None, :] - ii[:, None] + (L1L - 1)
    eff = np.where(rel < 0, rel + (2 * L1L - 1), rel)
    eff = np.clip(eff, 0, 2 * L1L - 2)
    biasT = np.ascontiguousarray(w1[eff].T)

    shared = {
        "W_inT": W_inT, "W_gateT": W_gateT, "W_outT": W_outT, "Wr0T": Wr0T,
        "ln_g": np.asarray(inputs["L0_attn_ln_g"], f32).reshape(1, D),
        "ln_b": np.asarray(inputs["L0_attn_ln_b"], f32).reshape(1, D),
        "b_gate": np.asarray(inputs["L0_attn_b_gate"], f32).reshape(1, 32),
        "eig_raw": np.asarray(inputs["L0_attn_eig_raw"], f32).reshape(16, 1),
        "Wqkv": np.ascontiguousarray(np.asarray(inputs["L0_ffn_Wqkv"], f32)),
        "Wo0": np.ascontiguousarray(np.asarray(inputs["L0_ffn_Wo"], f32)),
        "ln1_w": np.asarray(inputs["L1_attn_ln_w"], f32).reshape(1, D),
        "ln1_b": np.asarray(inputs["L1_attn_ln_b"], f32).reshape(1, D),
        "uv_wT": uv_wT,
        "uv_b": np.asarray(inputs["L1_attn_uv_b"], f32).reshape(1, 4224),
        "gam": np.ascontiguousarray(np.asarray(inputs["L1_attn_gamma"], f32).reshape(2, S1)),
        "bet": np.ascontiguousarray(np.asarray(inputs["L1_attn_beta"], f32).reshape(2, S1)),
        "biasT": biasT, "o_wT": o_wT,
        "o_b": np.asarray(inputs["L1_attn_o_b"], f32).reshape(1, D),
        "Wr1": np.ascontiguousarray(np.asarray(inputs["L1_ffn_Wr"], f32)),
        **cst,
    }
    W1 = np.asarray(inputs["L1_ffn_W1"], f32)
    W2 = np.asarray(inputs["L1_ffn_W2"], f32)

    in_maps = []
    for c in range(NCORES):
        m = dict(shared)
        m["x4"] = np.ascontiguousarray(x[BPC * c:BPC * (c + 1)])
        m["W1e"] = np.ascontiguousarray(W1[c])
        m["W2e"] = np.ascontiguousarray(W2[c])
        cm = np.zeros((8, 1), f32)
        cm[:c] = 1.0
        m["cmask"] = cm
        m["cvec"] = np.array([[c * BPC * T, c * CAP, float(c)]], f32)
        in_maps.append(m)
    return in_maps



class Ev:

    def __init__(self, st, nc, name):
        self.sem = st.enter_context(nc.semaphore(name))
        self.n = 0

    def inc(self, instr, k=1):
        instr.then_inc(self.sem, k)
        self.n += k
        return instr

    def dma(self, instr):
        return self.inc(instr, 16)

    def wait(self, eng, val=None):
        eng.wait_ge(self.sem, self.n if val is None else val)


_DMA_OPS = {"dma_start", "dma_start_transpose", "indirect_dma_start"}


class Q:

    _CTRL = {"eng", "ev", "auto", "_deps", "after", "wait_ge", "raw"}

    def __init__(self, eng, ev, auto=True):
        self.eng = eng
        self.ev = ev
        self.auto = auto
        self._deps = []
        self._ev_override = None

    def after(self, *evs):
        self._deps.extend(evs)
        return self

    def to(self, ev):
        self._ev_override = ev
        return self

    def wait_ge(self, sem, val):
        self.eng.wait_ge(sem, val)

    @property
    def raw(self):
        return self.eng

    def __getattr__(self, name):
        f = getattr(object.__getattribute__(self, "eng"), name)
        if not callable(f):
            return f

        def call(*a, **kw):
            if self.auto:
                self.ev.wait(self.eng)
            for d in self._deps:
                if isinstance(d, tuple):
                    d[0].wait(self.eng, d[1])
                else:
                    d.wait(self.eng)
            del self._deps[:]
            ev = self._ev_override or self.ev
            self._ev_override = None
            ins = f(*a, **kw)
            return ev.inc(ins, 16 if name in _DMA_OPS else 1)
        return call


class Ctx:
    def ev(self, stack, name):
        return Ev(stack, self.nc, name)


def build_program():
    nc = bass.Bass("TRN2", target_bir_lowering=False)
    from contextlib import ExitStack
    st = ExitStack()

    g = Ctx()
    g.nc = nc
    g.st = st

    def din(name, shape, dt=F32):
        return nc.dram_tensor(name, shape, dt, kind="ExternalInput").ap()

    def dram(name, shape, dt=F32, out=False):
        kind = "ExternalOutput" if (out or name in _DEBUG_TAPS) else "Internal"
        return nc.dram_tensor(name, shape, dt, kind=kind).ap()

    g.x4 = din("x4", [BPC, T, D])
    g.W_inT = din("W_inT", [D, D]); g.ln_g = din("ln_g", [1, D]); g.ln_b = din("ln_b", [1, D])
    g.W_gateT = din("W_gateT", [D, 32]); g.b_gate = din("b_gate", [1, 32])
    g.eig_raw = din("eig_raw", [16, 1]); g.W_outT = din("W_outT", [D, D])
    g.Wr0T = din("Wr0T", [D, 8]); g.Wqkv = din("Wqkv", [E0, D, 3 * HP]); g.Wo0 = din("Wo0", [E0, HP, D])
    g.postable = din("postable", [T, 64])
    g.ln1_w = din("ln1_w", [1, D]); g.ln1_b = din("ln1_b", [1, D])
    g.uv_wT = din("uv_wT", [D, 4224]); g.uv_b = din("uv_b", [1, 4224])
    g.gam = din("gam", [2, S1]); g.bet = din("bet", [2, S1])
    g.biasT = din("biasT", [T, T]); g.o_wT = din("o_wT", [E1, D]); g.o_b = din("o_b", [1, D])
    g.sinT1 = din("sinT1", [64, T]); g.cosT1 = din("cosT1", [64, T])
    g.Wr1 = din("Wr1", [D, 8]); g.W1e = din("W1e", [D, DFF]); g.W2e = din("W2e", [DFF, D])
    g.ident_bf = din("ident_bf", [128, 128], BF16); g.ident_f = din("ident_f", [128, 128])
    g.stril_bf = din("stril_bf", [128, 128], BF16)
    g.ones1_f = din("ones1_f", [1, 128]); g.ones1_bf = din("ones1_bf", [1, 128], BF16)
    g.iota_p = din("iota_p", [128, 1]); g.iota_r = din("iota_r", [128, K0]); g.iota8 = din("iota8", [128, 8])
    g.smalls = din("smalls", [128, 32], BF16)
    g.stril8f = din("stril8f", [8, 24]); g.onecol_bf = din("onecol_bf", [128, 1], BF16)
    g.cmask = din("cmask", [8, 1]); g.cvec = din("cvec", [1, 3])

    g.xn0ext = dram("xn0ext", [BPC, T, 1096])
    g.xn0_tmb = dram("xn0_tmb", [BPC, T, D], BF16)
    g.x1p = dram("x1p", [BPC, T, D])
    g.stage16 = dram("stage16", [4, 16, T])
    g.idlist = dram("idlist", [32, K0])
    g.vstage2 = dram("vstage2", [32, T])
    g.uv_wTb = dram("uv_wTb", [D, 4224], BF16)
    g.o_wTb = dram("o_wTb", [E1, D], BF16)
    from concourse.replica_groups import maybe_share_collective_output_space
    g.rg = [list(range(NCORES))]
    _sp = maybe_share_collective_output_space("AllGather", g.rg, prefer_local=True)
    g.cc_cnt_in = nc.dram_tensor("cc_cnt_in", [1, 8], F32, kind="Internal").ap()
    g.cc_cnt_out = nc.dram_tensor("cc_cnt_out", [8, 8], F32, kind="Internal", addr_space=_sp).ap()
    g.lists_pub = nc.dram_tensor("lists_pub", [E0 * 8192, 1], U32, kind="Internal").ap()
    g.lists_all = nc.dram_tensor("lists_all", [8 * E0 * 8192, 1], U32, kind="Internal", addr_space=_sp).ap()
    g.xn1_all = nc.dram_tensor("xn1_all", [NCORES * BPC * T, D], BF16, kind="Internal", addr_space=_sp).ap()
    g.eout_pub = nc.dram_tensor("eout_pub", [CAP, D], BF16, kind="Internal").ap()
    g.eout_all = nc.dram_tensor("eout_all", [8 * CAP, D], BF16, kind="Internal", addr_space=_sp).ap()
    g.xn1_f32 = dram("xn1_f32", [BPC, T, D])
    g.xn1_bf = dram("xn1_bf", [BPC * T, D], BF16)
    g.xnattn_bf = dram("xnattn_bf", [BPC, T, D], BF16)
    g.x2p = dram("x2p", [BPC, T, D])
    g.out = dram("out", [BPC, T, D], out=True)

    g.LD = Ev(st, nc, "LD")
    g.LD2 = Ev(st, nc, "LD2")
    g.GD = Ev(st, nc, "GD")
    g.SC = Ev(st, nc, "SC")
    g.PE = Ev(st, nc, "PE")
    g.DV = Ev(st, nc, "DV")
    g.AC = Ev(st, nc, "AC")
    g.PL = Ev(st, nc, "PL")
    g.ST = Ev(st, nc, "ST")
    g.CC0 = Ev(st, nc, "CC0")
    g.CC1 = Ev(st, nc, "CC1")
    g.CC2 = Ev(st, nc, "CC2")
    g.CC3 = Ev(st, nc, "CC3")

    def barrier():
        for ev in (g.LD, g.LD2, g.GD, g.SC, g.ST):
            ev.wait(nc.sync)
        nc.all_engine_barrier()

    g.barrier = barrier
    g.SY = Q(nc.sync, g.LD, auto=False)
    g.SD = Q(nc.scalar, g.LD2, auto=False)
    g.GQ = Q(nc.gpsimd, g.GD, auto=False)
    g.V = Q(nc.vector, g.DV)
    g.A = Q(nc.scalar, g.AC)
    g.P = Q(nc.gpsimd, g.PL)
    g.TE = Q(nc.tensor, g.PE, auto=False)

    sb = lambda name, shape, dt=F32: st.enter_context(nc.sbuf_tensor(name, shape, dt))
    g.sb = sb
    g.c_ident_bf = sb("c_ident_bf", [128, 128], BF16)
    g.c_ident_f = sb("c_ident_f", [128, 128])
    g.c_stril = sb("c_stril", [128, 128], BF16)
    g.c_ones1f = sb("c_ones1f", [1, 128])
    g.c_ones1b = sb("c_ones1b", [1, 128], BF16)
    g.c_iotap = sb("c_iotap", [128, 1])
    g.c_iotar = sb("c_iotar", [128, K0])
    g.c_iota8 = sb("c_iota8", [128, 8])
    g.c_bgate = sb("c_bgate", [128, 32])
    g.c_eig = sb("c_eig", [16, 1])
    g.c_uvb = sb("c_uvb", [128, 33])
    g.c_gam = sb("c_gam", [128, 4])

    g.c_cmask = sb("c_cmask", [8, 1])
    g.c_cvec = sb("c_cvec", [1, 3])
    g.c_eps = sb("c_eps", [128, 1])
    g.c_smalls = sb("c_smalls", [128, 32], BF16)
    g.c_wr0 = sb("c_wr0", [128, 64], BF16)
    g.pmt = sb("pmt", [128, 16 * 32])
    g.c_wr1 = sb("c_wr1", [128, 64], BF16)
    g.c_s8 = sb("c_s8", [8, 24])
    g.c_onec = sb("c_onec", [128, 1], BF16)
    g.route = sb("route", [128, BPC * NT * 3])
    g.oh_all = sb("oh_all", [128, BPC * NT * 8], BF16)
    g.slotf = sb("slotf", [128, BPC * NT])
    g.keptf = sb("keptf", [128, BPC * NT])
    g.ttot = sb("ttot", [1, BPC * NT * 8])

    SY, SD, GQ, V, A, P, TE = g.SY, g.SD, g.GQ, g.V, g.A, g.P, g.TE
    SY.dma_start(out=g.c_ident_bf[:], in_=g.ident_bf)
    SY.dma_start(out=g.c_ident_f[:], in_=g.ident_f)
    SY.dma_start(out=g.c_stril[:], in_=g.stril_bf)
    SY.dma_start(out=g.c_ones1f[:], in_=g.ones1_f)
    SY.dma_start(out=g.c_ones1b[:], in_=g.ones1_bf)
    SY.dma_start(out=g.c_iotap[:], in_=g.iota_p)
    SY.dma_start(out=g.c_iotar[:], in_=g.iota_r)
    SY.dma_start(out=g.c_iota8[:], in_=g.iota8)
    SY.dma_start(out=g.c_bgate[:], in_=g.b_gate.broadcast_to([128, 32]))
    SY.dma_start(out=g.c_uvb[:], in_=g.uv_b.rearrange("o (c p) -> (o p) c", p=128), allow_slow_non_contiguous=True)

    SY.dma_start(out=g.c_cmask[:], in_=g.cmask)
    SY.dma_start(out=g.c_cvec[:], in_=g.cvec)
    SY.dma_start(out=g.c_smalls[:], in_=g.smalls)
    for j in range(8):
        GQ.dma_start(out=g.c_wr0[:, j * 8:(j + 1) * 8], in_=g.Wr0T[j * 128:(j + 1) * 128, :])
        GQ.dma_start(out=g.c_wr1[:, j * 8:(j + 1) * 8], in_=g.Wr1[j * 128:(j + 1) * 128, :])
    SY.dma_start(out=g.c_s8[:], in_=g.stril8f)
    SY.dma_start(out=g.c_onec[:], in_=g.onecol_bf)
    SY.dma_start(out=g.c_gam[:, 0:2], in_=g.gam.rearrange("c p -> p c"), allow_slow_non_contiguous=True)
    SY.dma_start(out=g.c_gam[:, 2:4], in_=g.bet.rearrange("c p -> p c"), allow_slow_non_contiguous=True)
    V.memset(g.c_eps[:], LN_EPS)
    SY.dma_start(out=g.c_eig[:], in_=g.eig_raw)
    A.after(g.LD).activation(out=g.c_eig[:], in_=g.c_eig[:], func=AF.Tanh)
    for b in range(BPC):
        GQ.dma_start(out=g.xn0ext[b, :, D:D + 64], in_=g.postable)
    with nc.sbuf_tensor("precast", [128, 4224], BF16) as pcast:
        for i in range(8):
            GQ.after(g.ST).dma_start(out=pcast[:], in_=g.uv_wT[i * 128:(i + 1) * 128, :])
            SY.to(g.ST).after(g.GD).dma_start(out=g.uv_wTb[i * 128:(i + 1) * 128, :], in_=pcast[:])
        for i in range(16):
            GQ.after(g.ST).dma_start(out=pcast[:, 0:D], in_=g.o_wT[i * 128:(i + 1) * 128, :])
            SY.to(g.ST).after(g.GD).dma_start(out=g.o_wTb[i * 128:(i + 1) * 128, :], in_=pcast[:, 0:D])

    barrier()

    def _scoped(name, fn, *a):
        with nc.named_scope(name):
            fn(*a)

    _scoped("A", phase_a, g)
    if _LAST_PHASE == "A":
        finish(g)
        return nc
    for b in range(BPC):
        _scoped(f"B{b}", phase_b, g, b)
        _scoped(f"C1{b}", phase_c1, g, b)
    if _LAST_PHASE == "B":
        finish(g)
        return nc
    _scoped("C2", phase_c2, g)
    _scoped("C3", phase_c3, g)
    if _LAST_PHASE == "C":
        finish(g)
        return nc
    _scoped("D", phase_d, g)
    if _LAST_PHASE == "D":
        finish(g)
        return nc
    for b in range(BPC):
        _scoped(f"E{b}", phase_e, g, b)
    _scoped("E2", phase_e2, g)
    if _LAST_PHASE == "F":
        for b in range(BPC):
            _scoped(f"F{b}", phase_f, g, b)
        finish(g)
        return nc
    for b in range(BPC):
        _scoped(f"F{b}", phase_f, g, b)
    _scoped("G", phase_g, g)
    if _LAST_PHASE == "G":
        finish(g)
        return nc
    _scoped("H", phase_h, g)
    finish(g)
    return nc


def phase_a(g):
    nc = g.nc
    SY, V, A, P = g.SY, g.V, g.A, g.P
    from contextlib import ExitStack
    with ExitStack() as ph:
        xt = [ph.enter_context(nc.sbuf_tensor(f"pa_x{i}", [128, D], F32)) for i in range(2)]
        xn = [ph.enter_context(nc.sbuf_tensor(f"pa_n{i}", [128, D], F32)) for i in range(2)]
        xnb = [ph.enter_context(nc.sbuf_tensor(f"pa_nb{i}", [128, D], BF16)) for i in range(2)]
        sq = ph.enter_context(nc.sbuf_tensor("pa_sq", [128, D], F32))
        col = ph.enter_context(nc.sbuf_tensor("pa_col", [128, 8], F32))

        ntile = BPC * NT
        evX = [g.ev(ph, f"pa_evX{i}") for i in range(2)]
        evS = [g.ev(ph, f"pa_evS{i}") for i in range(2)]
        hist = {}
        pl_h = {}
        for i in range(ntile):
            b, tt = divmod(i, NT)
            p = i % 2
            if i >= 2:
                SY.after((g.DV, hist[i - 2]))
            SY.to(evX[p]).dma_start(out=xt[p][:], in_=g.x4[b, tt * 128:(tt + 1) * 128, :])
            g.V.after(evX[p])
            if i >= 2:
                g.V.after(evS[p])
            ln_tile(g, xt[p], xn[p], sq, col,
                    adeps=(evX[p], (g.DV, getattr(g, "_ln_sqfree", 0))))
            hist[i] = g.DV.n
            if i >= 2:
                P.after((g.PL, pl_h[i - 2]))
            P.after((g.DV, hist[i])).tensor_copy(out=xnb[p][:], in_=xn[p][:])
            pl_h[i] = g.PL.n
            SY.to(evS[p]).after((g.DV, hist[i])).dma_start(
                out=g.xn0ext[b, tt * 128:(tt + 1) * 128, 0:D], in_=xn[p][:])
            SY.to(evS[p]).after((g.PL, pl_h[i])).dma_start(
                out=g.xn0_tmb[b, tt * 128:(tt + 1) * 128, :], in_=xnb[p][:])
    g.barrier()


def ln_tile(g, src, dst, sq, col, gamma=None, beta=None, adeps=()):
    V, A = g.V, g.A
    m = col[:, 0:1]; q = col[:, 1:2]; msq = col[:, 2:3]
    sd = col[:, 3:4]; rs = col[:, 4:5]; var = col[:, 5:6]
    if adeps:
        A.after(*adeps).activation(out=sq[:], in_=src[:], func=AF.Square)
    V.reduce_sum(m, src[:], axis=AX.X)
    if not adeps:
        A.after(g.DV).activation(out=sq[:], in_=src[:], func=AF.Square)
    V.after((g.AC, g.AC.n)).reduce_sum(q, sq[:], axis=AX.X)
    g._ln_sqfree = g.DV.n
    V.tensor_scalar(out=m, in0=m, scalar1=1.0 / D, scalar2=None, op0=ALU.mult)
    V.tensor_tensor(out=msq, in0=m, in1=m, op=ALU.mult)
    V.tensor_scalar(out=var, in0=q, scalar1=1.0 / D, scalar2=msq, op0=ALU.mult, op1=ALU.subtract)
    A.after((g.DV, g.DV.n)).activation(out=sd, in_=var, func=AF.Sqrt, bias=g.c_eps[:, 0:1])
    V.after((g.AC, g.AC.n)).reciprocal(rs, sd)
    V.tensor_scalar(out=dst[:], in0=src[:], scalar1=m, scalar2=rs,
                    op0=ALU.subtract, op1=ALU.mult)
    if gamma is not None:
        V.tensor_tensor(out=dst[:], in0=dst[:], in1=gamma[:], op=ALU.mult)
        V.tensor_tensor(out=dst[:], in0=dst[:], in1=beta[:], op=ALU.add)


def phase_b(g, b):
    nc = g.nc
    SY, SD, GQ, V, A, P, TE = g.SY, g.SD, g.GQ, g.V, g.A, g.P, g.TE
    from contextlib import ExitStack
    with ExitStack() as ph:
        en = ph.enter_context
        xpnfm = en(nc.sbuf_tensor(f"pb{b}_xpnfm", [128, 8 * T], F32))
        warena = [en(nc.sbuf_tensor(f"pb{b}_w{j}", [128, D], BF16)) for j in range(8)]
        hallb = en(nc.sbuf_tensor(f"pb{b}_hall", [128, 8 * T], BF16))
        s12 = ExitStack()
        al16 = s12.enter_context(nc.sbuf_tensor(f"pb{b}_al16", [16, T], F32))
        bet16 = s12.enter_context(nc.sbuf_tensor(f"pb{b}_bet16", [16, T], F32))

        def xpn_j(j, sl=slice(None)):
            return xpnfm[:, j * T:(j + 1) * T][:, sl]

        def hall_j(j, sl=slice(None)):
            return hallb[:, j * T:(j + 1) * T][:, sl]

        with ExitStack() as s1:
            en1 = s1.enter_context
            xn0fm = [en1(nc.sbuf_tensor(f"pb{b}_xn0fm{j}", [128, T], BF16)) for j in range(8)]
            wgate = en1(nc.sbuf_tensor(f"pb{b}_wg", [128, 8 * 32], F32))
            lngb = en1(nc.sbuf_tensor(f"pb{b}_lngb", [128, D], F32))
            lnbb = en1(nc.sbuf_tensor(f"pb{b}_lnbb", [128, D], F32))
            ps_xp = [en1(nc.psum_tensor(f"pb{b}_psxp{i}", [128, D], F32)) for i in range(2)]
            ps_tr = en1(nc.psum_tensor(f"pb{b}_pstr0", [128, 512], F32))
            ps_g = [en1(nc.psum_tensor(f"pb{b}_psg{i}", [128, 32], F32)) for i in range(2)]
            ps_gt = en1(nc.psum_tensor(f"pb{b}_psgt", [16, 256], F32))
            xpn_sb = [en1(nc.sbuf_tensor(f"pb{b}_xpn{i}", [128, D], F32)) for i in range(2)]
            sq = en1(nc.sbuf_tensor(f"pb{b}_sq", [128, D], F32))
            col = en1(nc.sbuf_tensor(f"pb{b}_col", [128, 8], F32))
            gsig = [en1(nc.sbuf_tensor(f"pb{b}_gsig{i}", [128, 32], F32)) for i in range(2)]
            SY.dma_start(out=lngb[:], in_=g.ln_g.broadcast_to([128, D]))
            SY.dma_start(out=lnbb[:], in_=g.ln_b.broadcast_to([128, D]))
            for j in range(8):
                SY.dma_start(out=xn0fm[j][:], in_=g.xn0_tmb[b, :, j * 128:(j + 1) * 128], transpose=True)
                GQ.dma_start(out=warena[j][:], in_=g.W_inT[j * 128:(j + 1) * 128, :])
                SD.dma_start(out=wgate[:, j * 32:(j + 1) * 32], in_=g.W_gateT[j * 128:(j + 1) * 128, :])
            ld0 = g.LD.n
            gd0 = g.GD.n
            ld20 = g.LD2.n

            dv_ln = {}
            dv_cp = {}
            dv_gs = {}
            for tt in range(NT):
                p = tt % 2
                ts_ = slice(tt * 128, (tt + 1) * 128)
                if tt == 0:
                    TE.after((g.LD, ld0), (g.GD, gd0), (g.LD2, ld20))
                if tt >= 2:
                    TE.after((g.DV, dv_ln[tt - 2]))
                for fh in range(2):
                    fs = slice(fh * 512, (fh + 1) * 512)
                    for j in range(8):
                        TE.matmul(out=ps_xp[p][:, fs], lhsT=xn0fm[j][:, ts_],
                                  rhs=warena[j][:, fs], start=(j == 0), stop=(j == 7))
                pe_xp = g.PE.n
                V.after((g.PE, pe_xp))
                ln_tile(g, ps_xp[p], xpn_sb[p], sq, col, gamma=lngb[:], beta=lnbb[:])
                dv_ln[tt] = g.DV.n
                TE.after((g.DV, dv_ln[tt]))
                for half in range(2):
                    if (tt, half) != (0, 0):
                        TE.after((g.DV, g.DV.n))
                    for jj in range(4):
                        j = half * 4 + jj
                        TE.transpose(out=ps_tr[:, jj * 128:(jj + 1) * 128],
                                     in_=xpn_sb[p][:, j * 128:(j + 1) * 128],
                                     identity=g.c_ident_f[:])
                    pe_tr = g.PE.n
                    V.after((g.PE, pe_tr)).tensor_copy(
                        out=xpnfm.rearrange("p (j t) -> p j t", j=8)[:, half * 4:half * 4 + 4, ts_],
                        in_=ps_tr[:].rearrange("p (j t) -> p j t", j=4))
                dv_cp[tt] = g.DV.n
                TE.after((g.DV, dv_cp[tt]))
                if tt >= 2:
                    TE.after((g.DV, dv_gs[tt - 2]))
                for j in range(8):
                    TE.matmul(out=ps_g[p][:], lhsT=xpn_j(j, ts_),
                              rhs=wgate[:, j * 32:(j + 1) * 32], start=(j == 0), stop=(j == 7))
                pe_g = g.PE.n
                V.after((g.PE, pe_g)).tensor_tensor(out=gsig[p][:], in0=ps_g[p][:],
                                                    in1=g.c_bgate[:, 0:32], op=ALU.add)
                dv_gg = g.DV.n
                A.after((g.DV, dv_gg)).activation(out=gsig[p][:], in_=gsig[p][:], func=AF.Sigmoid)
                ac_g = g.AC.n
                TE.after((g.AC, ac_g))
                if tt >= 1:
                    TE.after((g.DV, dv_gs[tt - 1]))
                TE.transpose(out=ps_gt[:, 0:128], in_=gsig[p][:, 0:16], identity=g.c_ident_f[:])
                TE.transpose(out=ps_gt[:, 128:256], in_=gsig[p][:, 16:32], identity=g.c_ident_f[:])
                pe_gt = g.PE.n
                V.after((g.PE, pe_gt)).tensor_copy(out=al16[:, ts_], in_=ps_gt[:, 0:128])
                V.tensor_copy(out=bet16[:, ts_], in_=ps_gt[:, 128:256])
                dv_gs[tt] = g.DV.n

        with ExitStack() as s2:
            en2 = s2.enter_context
            a16 = en2(nc.sbuf_tensor(f"pb{b}_a16", [16, T], F32))[:]
            atil = en2(nc.sbuf_tensor(f"pb{b}_atil", [16, T], F32))[:]
            btil = en2(nc.sbuf_tensor(f"pb{b}_btil", [16, T], F32))[:]
            cum = en2(nc.sbuf_tensor(f"pb{b}_cum", [16, T], F32))[:]
            z16 = en2(nc.sbuf_tensor(f"pb{b}_z16", [16, T], F32))[:]
            wp = en2(nc.sbuf_tensor(f"pb{b}_wp", [16, T], F32))[:]
            V.after(g.AC).tensor_scalar(out=a16, in0=al16[:], scalar1=g.c_eig[:], scalar2=None, op0=ALU.mult)
            V.tensor_copy(out=atil, in_=a16)
            V.memset(atil[:, 0::CS], 0.0)
            V.memset(btil, 0.0)
            V.tensor_copy(out=btil[:, 0::CS], in_=a16[:, 0::CS])
            V.tensor_tensor_scan(out=cum, data0=atil, data1=btil, initial=0.0,
                                 op0=ALU.mult, op1=ALU.add)
            A.after(g.DV).activation(out=z16[:, 1:], in_=cum[:, 0:T - 1], func=AF.Abs)
            V.after(g.AC).tensor_scalar(out=z16[:, 1:], in0=z16[:, 1:], scalar1=1e-8, scalar2=None,
                                        op0=ALU.is_gt)
            V.memset(z16[:, 0::CS], 1.0)
            V.tensor_tensor(out=wp, in0=a16, in1=z16, op=ALU.mult)
            V.tensor_tensor(out=wp, in0=wp, in1=bet16[:], op=ALU.mult)
            SY.to(g.ST).after(g.DV).dma_start(out=g.stage16[0], in_=atil)
            SY.to(g.ST).dma_start(out=g.stage16[1], in_=wp)
            SY.to(g.ST).dma_start(out=g.stage16[2], in_=bet16[:])
            SY.to(g.ST).dma_start(out=g.stage16[3], in_=cum)
            st_b2 = g.ST.n

        s12.close()
        with ExitStack() as s3:
            en3 = s3.enter_context
            exps = [[en3(nc.sbuf_tensor(f"pb{b}_e{i}{q}", [128, T], F32)) for q in range(4)]
                    for i in range(2)]
            s_p = en3(nc.sbuf_tensor(f"pb{b}_p", [128, T], F32))
            s_h = en3(nc.sbuf_tensor(f"pb{b}_h", [128, T], F32))
            hrun = en3(nc.sbuf_tensor(f"pb{b}_hrun", [128, T // CS], F32))
            dv_done = {}
            pl_done = {}
            evE = [g.ev(s3, f"pb{b}_evE{i}") for i in range(2)]
            SY.after((g.ST, st_b2), (g.DV, g.DV.n), (g.AC, g.AC.n))
            for c in range(2):
                for qi in range(4):
                    SY.to(evE[c]).dma_start(out=exps[c][qi][:], in_=g.stage16[qi, 2 * c:2 * c + 2, :]
                                            .unsqueeze(1).broadcast_to([2, 64, T]))
            ac_cp = {}
            for c in range(8):
                pp = c % 2
                ae, we, be, ce = exps[pp]
                V.after(evE[pp])
                if c >= 1:
                    V.after((g.AC, ac_cp[c - 1]))
                V.tensor_tensor(out=s_p[:], in0=we[:], in1=xpn_j(c), op=ALU.mult)
                V.tensor_tensor_scan(out=s_p[:], data0=ae[:], data1=s_p[:], initial=0.0,
                                     op0=ALU.mult, op1=ALU.add)
                P.after(evE[pp], (g.AC, g.AC.n)).tensor_tensor(
                    out=be[:], in0=be[:], in1=xpn_j(c), op=ALU.mult)
                pl_done[c] = g.PL.n
                V.memset(s_h[:, 0:1], 0.0)
                V.tensor_tensor(out=s_h[:, 1:], in0=ae[:, 1:], in1=s_p[:, 0:T - 1], op=ALU.mult)
                V.after((g.PL, pl_done[c])).tensor_tensor(out=s_h[:], in0=s_h[:], in1=be[:], op=ALU.add)
                V.tensor_tensor_scan(out=hrun[:], data0=ce[:, CS - 1::CS], data1=s_h[:, CS - 1::CS],
                                     initial=0.0, op0=ALU.mult, op1=ALU.add)
                nch = T // CS
                ce3 = ce[:, CS:].rearrange("p (c i) -> p c i", i=CS)
                hr3 = hrun[:, 0:nch - 1].unsqueeze(2).broadcast_to([128, nch - 1, CS])
                V.tensor_tensor(out=ae[:, CS:].rearrange("p (c i) -> p c i", i=CS),
                                in0=ce3, in1=hr3, op=ALU.mult)
                V.tensor_tensor(out=s_h[:, CS:], in0=s_h[:, CS:], in1=ae[:, CS:], op=ALU.add)
                dv_done[c] = g.DV.n
                A.after((g.DV, dv_done[c])).copy(out=hall_j(c), in_=s_h[:])
                ac_cp[c] = g.AC.n
                if c + 2 < 8:
                    SY.after((g.DV, dv_done[c]))
                    for qi in range(4):
                        SY.to(evE[pp]).dma_start(out=exps[pp][qi][:],
                                                 in_=g.stage16[qi, 2 * (c + 2):2 * (c + 2) + 2, :]
                                                 .unsqueeze(1).broadcast_to([2, 64, T]))
            ac_hall = g.AC.n

        with ExitStack() as s4:
            en4 = s4.enter_context
            ps_o = [en4(nc.psum_tensor(f"pb{b}_pso{i}", [128, D], F32)) for i in range(2)]
            xt = [en4(nc.sbuf_tensor(f"pb{b}_xt{i}", [128, D], F32)) for i in range(2)]
            x1t = [en4(nc.sbuf_tensor(f"pb{b}_x1t{i}", [128, D], F32)) for i in range(2)]
            GQ.after((g.PE, g.PE.n))
            for j in range(8):
                GQ.dma_start(out=warena[j][:], in_=g.W_outT[j * 128:(j + 1) * 128, :])
            gd_wo = g.GD.n
            histo = {}
            evX = [g.ev(s4, f"pb{b}_evX{i}") for i in range(2)]
            for tt in range(NT):
                p = tt % 2
                ts_ = slice(tt * 128, (tt + 1) * 128)
                if tt == 0:
                    SY.after((g.DV, g.DV.n), (g.AC, ac_hall))
                if tt >= 2:
                    SY.after((g.DV, histo[tt - 2]))
                SY.to(evX[p]).dma_start(out=xt[p][:], in_=g.x4[b, tt * 128:(tt + 1) * 128, :])
                if tt == 0:
                    TE.after((g.AC, ac_hall), (g.GD, gd_wo))
                if tt >= 2:
                    TE.after((g.DV, histo[tt - 2]))
                for fh in range(2):
                    fs = slice(fh * 512, (fh + 1) * 512)
                    for j in range(8):
                        TE.matmul(out=ps_o[p][:, fs], lhsT=hall_j(j, ts_),
                                  rhs=warena[j][:, fs], start=(j == 0), stop=(j == 7))
                pe_now = g.PE.n
                V.after((g.PE, pe_now), evX[p], g.ST).tensor_tensor(
                    out=x1t[p][:], in0=ps_o[p][:], in1=xt[p][:], op=ALU.add)
                histo[tt] = g.DV.n
                SY.to(g.ST).after((g.DV, histo[tt])).dma_start(
                    out=g.x1p[b, tt * 128:(tt + 1) * 128, :], in_=x1t[p][:])
    g.barrier()


def phase_c1(g, b):
    nc = g.nc
    SY, GQ, V, A, TE = g.SY, g.GQ, g.V, g.A, g.TE
    from contextlib import ExitStack
    with ExitStack() as ph:
        en = ph.enter_context
        xn0fm = [en(nc.sbuf_tensor(f"pc1{b}_xn0fm{j}", [128, T], BF16)) for j in range(8)]
        vsig = [en(nc.sbuf_tensor(f"pc1{b}_vs{i}", [128, 8], F32)) for i in range(2)]
        vfm_b = en(nc.sbuf_tensor(f"pc1{b}_vfm", [8, T], F32))
        ps_l = en(nc.psum_tensor(f"pc1{b}_psl", [128, 8], F32))
        ps_t = en(nc.psum_tensor(f"pc1{b}_pst", [8, 128], F32))
        for j in range(8):
            SY.dma_start(out=xn0fm[j][:], in_=g.xn0_tmb[b, :, j * 128:(j + 1) * 128], transpose=True)
        for tt in range(NT):
            p = tt % 2
            ts_ = slice(tt * 128, (tt + 1) * 128)
            if tt == 0:
                TE.after(g.LD, g.GD)
            if tt >= 2:
                TE.after((g.AC, g.AC.n))
                TE.after((g.DV, g.DV.n))
            for j in range(8):
                TE.matmul(out=ps_l[:], lhsT=xn0fm[j][:, ts_], rhs=g.c_wr0[:, j * 8:(j + 1) * 8],
                          start=(j == 0), stop=(j == 7))
            A.after((g.PE, g.PE.n), g.ST).activation(out=vsig[p][:], in_=ps_l[:], func=AF.Sigmoid)
            SY.to(g.ST).after(g.AC).dma_start(out=g.xn0ext[b, ts_, D + 64:D + 72], in_=vsig[p][:])
            TE.after((g.AC, g.AC.n), (g.DV, g.DV.n)).transpose(out=ps_t[:], in_=vsig[p][:], identity=g.c_ident_f[:])
            V.after((g.PE, g.PE.n)).tensor_copy(out=vfm_b[:, ts_], in_=ps_t[:])
        SY.to(g.ST).after(g.DV).dma_start(out=g.vstage2[8 * b:8 * b + 8, :], in_=vfm_b[:])
    g.barrier()


def phase_c2(g):
    nc = g.nc
    SY, V, A, TE = g.SY, g.V, g.A, g.TE
    from contextlib import ExitStack
    with ExitStack() as ph:
        en = ph.enter_context
        vfm = en(nc.sbuf_tensor("pc2_vfm", [32, T], F32))[:]
        junk = en(nc.sbuf_tensor("pc2_junk", [32, T], F32))[:]
        eq = en(nc.sbuf_tensor("pc2_eq", [32, T], F32))[:]
        pp0 = en(nc.sbuf_tensor("pc2_pp0", [32, T], F32))[:]
        pp1 = en(nc.sbuf_tensor("pc2_pp1", [32, T], F32))[:]
        sel = en(nc.sbuf_tensor("pc2_sel", [32, T], F32))[:]
        cfill = en(nc.sbuf_tensor("pc2_cf", [32, T], F32))[:]
        cols = en(nc.sbuf_tensor("pc2_cols", [32, 16], F32))
        ps_tr = en(nc.psum_tensor("pc2_pstr", [128, 32], F32))
        lo = cols[:, 0:1]; hi = cols[:, 1:2]; mid = cols[:, 2:3]
        cnt = cols[:, 3:4]; th = cols[:, 4:5]; m = cols[:, 5:6]; km = cols[:, 6:7]
        pf = cols[:, 7:8]; tv = cols[:, 8:9]

        SY.after(g.ST).dma_start(out=vfm, in_=g.vstage2)
        V.after(g.LD).memset(lo, 0.0)
        V.memset(hi, 1.0)
        for it in range(45):
            V.tensor_tensor(out=mid, in0=lo, in1=hi, op=ALU.add)
            V.tensor_scalar(out=mid, in0=mid, scalar1=0.5, scalar2=None, op0=ALU.mult)
            V.tensor_scalar(out=junk, in0=vfm, scalar1=mid, scalar2=0.0,
                            op0=ALU.is_gt, op1=ALU.add, accum_out=cnt)
            V.tensor_scalar(out=pf, in0=cnt, scalar1=float(K0), scalar2=None, op0=ALU.is_ge)
            V.tensor_tensor(out=tv, in0=pf, in1=mid, op=ALU.mult)
            V.tensor_tensor(out=lo, in0=lo, in1=tv, op=ALU.max)
            V.tensor_tensor(out=tv, in0=mid, in1=pf, op=ALU.add)
            V.tensor_tensor(out=hi, in0=hi, in1=tv, op=ALU.min)
        V.tensor_scalar(out=eq, in0=vfm, scalar1=hi, scalar2=None, op0=ALU.is_le)
        V.tensor_tensor(out=junk, in0=vfm, in1=eq, op=ALU.mult)
        V.tensor_tensor(out=junk, in0=junk, in1=eq, op=ALU.add)
        V.tensor_scalar(out=junk, in0=junk, scalar1=-1.0, scalar2=None, op0=ALU.add)
        V.reduce_max(th, junk, axis=AX.X)
        V.tensor_scalar(out=junk, in0=vfm, scalar1=th, scalar2=0.0,
                        op0=ALU.is_gt, op1=ALU.add, accum_out=m)
        V.tensor_scalar(out=eq, in0=vfm, scalar1=th, scalar2=None, op0=ALU.is_equal)
        _prefix(g, eq, pp0, pp1)
        V.tensor_tensor(out=pp0, in0=pp0, in1=eq, op=ALU.subtract)
        V.tensor_scalar(out=km, in0=m, scalar1=-1.0, scalar2=float(K0), op0=ALU.mult, op1=ALU.add)
        V.tensor_scalar(out=cfill, in0=pp0, scalar1=km, scalar2=None, op0=ALU.is_lt)
        V.tensor_tensor(out=cfill, in0=cfill, in1=eq, op=ALU.mult)
        V.tensor_scalar(out=sel, in0=vfm, scalar1=th, scalar2=None, op0=ALU.is_gt)
        V.tensor_tensor(out=sel, in0=sel, in1=cfill, op=ALU.add)
        _prefix(g, sel, pp0, pp1)
        V.tensor_tensor(out=pp0, in0=pp0, in1=sel, op=ALU.subtract)
        V.tensor_tensor(out=pp0, in0=pp0, in1=sel, op=ALU.mult)
        V.tensor_scalar(out=junk, in0=sel, scalar1=-1.0e6, scalar2=1.0e6, op0=ALU.mult, op1=ALU.add)
        V.tensor_tensor(out=pp0, in0=pp0, in1=junk, op=ALU.add)
        for ch in range(NT):
            TE.after((g.DV, g.DV.n)).transpose(out=ps_tr[:], in_=pp0[:, ch * 128:(ch + 1) * 128],
                                               identity=g.c_ident_f[0:32, 0:32])
            V.after((g.PE, g.PE.n)).tensor_copy(out=g.pmt[:, ch * 32:(ch + 1) * 32], in_=ps_tr[:])
    g.barrier()


def _prefix(g, srcv, out0, out1):
    V = g.V
    V.tensor_copy(out=out0, in_=srcv)
    cur, oth = out0, out1
    s = 1
    while s < T:
        V.tensor_copy(out=oth[:, 0:s], in_=cur[:, 0:s])
        V.tensor_tensor(out=oth[:, s:], in0=cur[:, s:], in1=cur[:, 0:T - s], op=ALU.add)
        cur, oth = oth, cur
        s *= 2
    if cur is not out0:
        V.tensor_copy(out=out0, in_=cur)


def phase_c3(g):
    nc = g.nc
    SY, SD, GQ, V, A, P, TE = g.SY, g.SD, g.GQ, g.V, g.A, g.P, g.TE
    from contextlib import ExitStack
    with ExitStack() as ph:
        en = ph.enter_context
        L = {}
        L["wq"] = [en(nc.sbuf_tensor(f"pc3_wq{j}", [128, 3 * HP], BF16)) for j in range(8)]
        L["woe"] = en(nc.sbuf_tensor("pc3_woe", [128, D], BF16))
        L["Gt16"] = en(nc.sbuf_tensor("pc3_G16", [128, NT * K0], BF16))
        L["idf"] = en(nc.sbuf_tensor("pc3_idf", [128, 4], F32))
        L["idu"] = en(nc.sbuf_tensor("pc3_idu", [128, 2], U32))
        L["vall"] = en(nc.sbuf_tensor("pc3_vall", [128, 2], F32))
        L["xg"] = [en(nc.sbuf_tensor(f"pc3_xg{i}", [128, 1096], F32)) for i in range(2)]
        L["xgb"] = en(nc.sbuf_tensor("pc3_xgb", [128, 1088], BF16))
        L["xgfm"] = [en(nc.sbuf_tensor(f"pc3_xgfm{j}", [128, K0], BF16)) for j in range(8)]
        L["ptab"] = en(nc.sbuf_tensor("pc3_ptab", [64, K0], F32))
        L["sn0"] = en(nc.sbuf_tensor("pc3_sn0", [32, K0], F32))
        L["cs0"] = en(nc.sbuf_tensor("pc3_cs0", [32, K0], F32))
        L["qf"] = en(nc.sbuf_tensor("pc3_qf", [128, K0], F32))
        L["kf"] = en(nc.sbuf_tensor("pc3_kf", [128, K0], F32))
        L["vf"] = en(nc.sbuf_tensor("pc3_vf", [128, K0], F32))
        L["r1"] = en(nc.sbuf_tensor("pc3_r1", [32, K0], F32))
        L["r2"] = en(nc.sbuf_tensor("pc3_r2", [32, K0], F32))
        L["r3"] = en(nc.sbuf_tensor("pc3_r3", [32, K0], F32))
        L["qfb"] = en(nc.sbuf_tensor("pc3_qfb", [128, K0], BF16))
        L["kfb"] = en(nc.sbuf_tensor("pc3_kfb", [128, K0], BF16))
        L["vfb"] = en(nc.sbuf_tensor("pc3_vfb", [128, K0], BF16))
        L["sct"] = en(nc.sbuf_tensor("pc3_sct", [128, K0], F32))
        L["et"] = en(nc.sbuf_tensor("pc3_et", [128, K0], F32))
        L["mkf"] = en(nc.sbuf_tensor("pc3_mkf", [128, K0], F32))
        L["idbc"] = en(nc.sbuf_tensor("pc3_idbc", [128, K0], F32))
        L["ab"] = en(nc.sbuf_tensor("pc3_ab", [128, K0], BF16))
        L["at"] = [en(nc.sbuf_tensor(f"pc3_at{i}", [128, K0], BF16)) for i in range(2)]
        L["vt"] = [en(nc.sbuf_tensor(f"pc3_vt{i}", [128, 128], BF16)) for i in range(2)]
        L["avb"] = en(nc.sbuf_tensor("pc3_avb", [128, K0], BF16))
        L["xps"] = [en(nc.sbuf_tensor(f"pc3_xps{i}", [128, D], F32)) for i in range(2)]
        L["cols"] = en(nc.sbuf_tensor("pc3_cols", [128, 8], F32))
        L["ps_id0"] = en(nc.psum_tensor("pc3_psid0", [128, 2], F32))
        L["ps_id1"] = en(nc.psum_tensor("pc3_psid1", [128, 2], F32))
        L["ps_trb"] = en(nc.psum_tensor("pc3_pstrb", [128, 128], BF16))
        L["ps_a"] = en(nc.psum_tensor("pc3_psa", [128, K0], F32))
        L["ps_b"] = en(nc.psum_tensor("pc3_psb", [128, K0], F32))
        L["ps_o"] = en(nc.psum_tensor("pc3_pso", [128, D], F32))

        for e in range(E0):
            GQ.after((g.PE, g.PE.n))
            for j in range(8):
                GQ.dma_start(out=L["wq"][j][:], in_=g.Wqkv[e, j * 128:(j + 1) * 128, :])
            GQ.dma_start(out=L["woe"][:], in_=g.Wo0[e])
            for b in range(BPC):
                _c3_pair(g, b, e, L)
    g.barrier()


def _c3_pair(g, b, e, L):
    nc = g.nc
    SY, SD, GQ, V, A, P, TE = g.SY, g.SD, g.GQ, g.V, g.A, g.P, g.TE
    wq = L["wq"]; woe = L["woe"]; Gt16 = L["Gt16"]; idf = L["idf"]; idu = L["idu"]
    vall = L["vall"]; xg = L["xg"]; xgb = L["xgb"]; xgfm = L["xgfm"]; ptab = L["ptab"]
    sn0 = L["sn0"]; cs0 = L["cs0"]; qf = L["qf"]; kf = L["kf"]; vf = L["vf"]
    r1 = L["r1"]; r2 = L["r2"]; r3 = L["r3"]; qfb = L["qfb"]; kfb = L["kfb"]; vfb = L["vfb"]
    sct = L["sct"]; et = L["et"]; mkf = L["mkf"]; idbc = L["idbc"]
    ab = L["ab"]; at = L["at"]; vt = L["vt"]; avb = L["avb"]; xps = L["xps"]; cols = L["cols"]
    ps_id0 = L["ps_id0"]; ps_id1 = L["ps_id1"]; ps_trb = L["ps_trb"]
    ps_a = L["ps_a"]; ps_b = L["ps_b"]; ps_o = L["ps_o"]
    row = b * 8 + e
    RT = 1.0 / float(np.sqrt(HP))

    V.after((g.PE, g.PE.n), g.GD, g.SC)
    for ch in range(NT):
        V.tensor_scalar(out=Gt16[:, ch * K0:(ch + 1) * K0], in0=g.c_iotar[:],
                        scalar1=g.pmt[:, ch * 32 + row:ch * 32 + row + 1],
                        scalar2=None, op0=ALU.is_equal)
    TE.after((g.DV, g.DV.n))
    for ch in range(NT):
        TE.matmul(out=ps_id0[:], lhsT=Gt16[:, ch * K0:ch * K0 + 128],
                  rhs=g.c_smalls[:, 2 * ch:2 * ch + 2], start=(ch == 0), stop=(ch == NT - 1))
    for ch in range(NT):
        TE.matmul(out=ps_id1[:], lhsT=Gt16[:, ch * K0 + 128:(ch + 1) * K0],
                  rhs=g.c_smalls[:, 2 * ch:2 * ch + 2], start=(ch == 0), stop=(ch == NT - 1))
    V.after((g.PE, g.PE.n)).tensor_copy(out=idf[:, 0:2], in_=ps_id0[:])
    V.tensor_copy(out=idf[:, 2:4], in_=ps_id1[:])
    V.scalar_tensor_tensor(out=cols[:, 0:1], in0=idf[:, 0:1], scalar=128.0, in1=idf[:, 1:2],
                           op0=ALU.mult, op1=ALU.add)
    V.scalar_tensor_tensor(out=cols[:, 1:2], in0=idf[:, 2:3], scalar=128.0, in1=idf[:, 3:4],
                           op0=ALU.mult, op1=ALU.add)
    V.tensor_copy(out=idu[:, 0:1], in_=cols[:, 0:1])
    V.tensor_copy(out=idu[:, 1:2], in_=cols[:, 1:2])
    SY.to(g.ST).after(g.DV).dma_start(out=g.idlist[row, 0:128], in_=cols[:, 0:1])
    SY.to(g.ST).dma_start(out=g.idlist[row, 128:256], in_=cols[:, 1:2])
    GQ.after(g.DV, g.AC)
    xnext_flat = g.xn0ext.rearrange("b t f -> (b t) f")
    GQ.indirect_dma_start(out=xg[0][:], out_offset=None, in_=xnext_flat,
                          in_offset=IndirectOffsetOnAxis(ap=idu[:, 0:1], axis=0),
                          element_offset=b * T * 1096)
    GQ.indirect_dma_start(out=xg[1][:], out_offset=None, in_=xnext_flat,
                          in_offset=IndirectOffsetOnAxis(ap=idu[:, 1:2], axis=0),
                          element_offset=b * T * 1096)
    SY.after(g.ST).dma_start(out=idbc[:], in_=g.idlist[row:row + 1, :].broadcast_to([128, K0]))
    for c in range(2):
        V.after(g.GD).tensor_copy(out=xgb[:], in_=xg[c][:, 0:1088])
        V.tensor_copy(out=vall[:, c:c + 1], in_=xg[c][:, 1088 + e:1089 + e])
        for j in range(9):
            w = 128 if j < 8 else 64
            TE.after((g.DV, g.DV.n)).transpose(out=ps_trb[0:w, :], in_=xgb[:, j * 128:j * 128 + w],
                                               identity=g.c_ident_bf[:])
            if j < 8:
                V.after((g.PE, g.PE.n)).tensor_copy(out=xgfm[j][:, c * 128:(c + 1) * 128],
                                                    in_=ps_trb[:])
            else:
                V.after((g.PE, g.PE.n)).tensor_copy(out=ptab[:, c * 128:(c + 1) * 128],
                                                    in_=ps_trb[0:64, :])
    outs = (qf, kf, vf)
    for jc in range(3):
        TE.after((g.DV, g.DV.n))
        for j in range(8):
            TE.matmul(out=ps_a[:], lhsT=wq[j][:, jc * 128:(jc + 1) * 128], rhs=xgfm[j][:],
                      start=(j == 0), stop=(j == 7))
        V.after((g.PE, g.PE.n)).tensor_copy(out=outs[jc][:], in_=ps_a[:])
    V.tensor_copy(out=sn0[:], in_=ptab[0:32, :])
    V.tensor_copy(out=cs0[:], in_=ptab[32:64, :])
    for X in (qf, kf):
        V.tensor_copy(out=r2[:], in_=X[32:64, :])
        V.tensor_tensor(out=r1[:], in0=X[0:32, :], in1=cs0[:], op=ALU.mult)
        V.tensor_tensor(out=r3[:], in0=r2[:], in1=sn0[:], op=ALU.mult)
        V.tensor_tensor(out=r1[:], in0=r1[:], in1=r3[:], op=ALU.subtract)
        V.tensor_tensor(out=r3[:], in0=r2[:], in1=cs0[:], op=ALU.mult)
        V.tensor_tensor(out=r2[:], in0=X[0:32, :], in1=sn0[:], op=ALU.mult)
        V.tensor_tensor(out=r3[:], in0=r3[:], in1=r2[:], op=ALU.add)
        V.tensor_copy(out=X[0:32, :], in_=r1[:])
        V.tensor_copy(out=X[32:64, :], in_=r3[:])
    A.after(g.DV).copy(out=qfb[:], in_=qf[:])
    A.copy(out=kfb[:], in_=kf[:])
    A.copy(out=vfb[:], in_=vf[:])
    for mc in range(2):
        TE.after(g.AC, (g.DV, g.DV.n)).transpose(out=ps_trb[:], in_=vfb[:, mc * 128:(mc + 1) * 128],
                                 identity=g.c_ident_bf[:])
        V.after((g.PE, g.PE.n)).tensor_copy(out=vt[mc][:], in_=ps_trb[:])
    for tc in range(2):
        TE.after(g.AC, (g.DV, g.DV.n), g.LD)
        TE.matmul(out=ps_b[:], lhsT=qfb[:, tc * 128:(tc + 1) * 128], rhs=kfb[:],
                  start=True, stop=True)
        V.after((g.PE, g.PE.n)).tensor_scalar(out=sct[:], in0=ps_b[:], scalar1=RT,
                                              scalar2=None, op0=ALU.mult)
        V.tensor_scalar(out=mkf[:], in0=idbc[:], scalar1=cols[:, tc:tc + 1],
                        scalar2=None, op0=ALU.is_le)
        V.tensor_tensor(out=sct[:], in0=sct[:], in1=mkf[:], op=ALU.mult)
        V.tensor_scalar(out=mkf[:], in0=mkf[:], scalar1=1.0e30, scalar2=-1.0e30,
                        op0=ALU.mult, op1=ALU.add)
        V.tensor_tensor(out=sct[:], in0=sct[:], in1=mkf[:], op=ALU.add)
        V.reduce_max(cols[:, 4:5], sct[:], axis=AX.X, negate=True)
        A.after(g.DV).activation(out=et[:], in_=sct[:], func=AF.Exp, bias=cols[:, 4:5])
        V.after(g.AC).reduce_sum(cols[:, 5:6], et[:], axis=AX.X)
        V.reciprocal(cols[:, 6:7], cols[:, 5:6])
        V.tensor_scalar(out=ab[:], in0=et[:], scalar1=cols[:, 6:7], scalar2=None, op0=ALU.mult)
        for mc in range(2):
            TE.after((g.DV, g.DV.n)).transpose(out=ps_trb[:], in_=ab[:, mc * 128:(mc + 1) * 128],
                                               identity=g.c_ident_bf[:])
            V.after((g.PE, g.PE.n)).tensor_copy(out=at[mc][:, tc * 128:(tc + 1) * 128],
                                                in_=ps_trb[:])
    TE.after((g.DV, g.DV.n))
    for mc in range(2):
        TE.matmul(out=ps_a[:], lhsT=vt[mc][:], rhs=at[mc][:], start=(mc == 0), stop=(mc == 1))
    V.after((g.PE, g.PE.n)).tensor_copy(out=avb[:], in_=ps_a[:])
    for tc in range(2):
        TE.after((g.DV, g.DV.n), g.SC)
        for fh in range(2):
            TE.matmul(out=ps_o[:, fh * 512:(fh + 1) * 512], lhsT=avb[:, tc * 128:(tc + 1) * 128],
                      rhs=woe[:, fh * 512:(fh + 1) * 512], start=True, stop=True)
        V.after((g.PE, g.PE.n), g.SC).tensor_scalar(out=xps[tc][:], in0=ps_o[:],
                                              scalar1=vall[:, tc:tc + 1], scalar2=None, op0=ALU.mult)
        GQ.to(g.SC).after(g.DV, g.SC).indirect_dma_start(
            out=g.x1p.rearrange("b t f -> (b t) f"),
            out_offset=IndirectOffsetOnAxis(ap=idu[:, tc:tc + 1], axis=0),
            in_=xps[tc][:], in_offset=None, compute_op=ALU.add, oob_is_err=True,
            element_offset=b * T * D)


def phase_d(g):
    nc = g.nc
    SY, V, A, P = g.SY, g.V, g.A, g.P
    from contextlib import ExitStack
    with ExitStack() as ph:
        en = ph.enter_context
        xt = [en(nc.sbuf_tensor(f"pd_x{i}", [128, D], F32)) for i in range(2)]
        xn = [en(nc.sbuf_tensor(f"pd_n{i}", [128, D], F32)) for i in range(2)]
        xa = [en(nc.sbuf_tensor(f"pd_a{i}", [128, D], F32)) for i in range(2)]
        xnb = [en(nc.sbuf_tensor(f"pd_nb{i}", [128, D], BF16)) for i in range(2)]
        xab = [en(nc.sbuf_tensor(f"pd_ab{i}", [128, D], BF16)) for i in range(2)]
        sq = en(nc.sbuf_tensor("pd_sq", [128, D], F32))
        col = en(nc.sbuf_tensor("pd_col", [128, 8], F32))
        ln1w = en(nc.sbuf_tensor("pd_ln1w", [128, D], F32))
        ln1b = en(nc.sbuf_tensor("pd_ln1b", [128, D], F32))
        SY.dma_start(out=ln1w[:], in_=g.ln1_w.broadcast_to([128, D]))
        SY.dma_start(out=ln1b[:], in_=g.ln1_b.broadcast_to([128, D]))
        evX = [g.ev(ph, f"pd_evX{i}") for i in range(2)]
        evS = [g.ev(ph, f"pd_evS{i}") for i in range(2)]
        hist = {}
        pl_h = {}
        for i in range(BPC * NT):
            b, tt = divmod(i, NT)
            p = i % 2
            if i >= 2:
                SY.after((g.DV, hist[i - 2]))
            SY.to(evX[p]).dma_start(out=xt[p][:], in_=g.x1p[b, tt * 128:(tt + 1) * 128, :])
            g.V.after(evX[p])
            if i >= 2:
                g.V.after(evS[p])
            ln_tile(g, xt[p], xn[p], sq, col,
                    adeps=(evX[p], (g.DV, getattr(g, "_ln_sqfree", 0))))
            ln_tile(g, xn[p], xa[p], sq, col, gamma=ln1w[:], beta=ln1b[:])
            hist[i] = g.DV.n
            if i >= 2:
                P.after((g.PL, pl_h[i - 2]))
            P.after((g.DV, hist[i])).tensor_copy(out=xnb[p][:], in_=xn[p][:])
            P.tensor_copy(out=xab[p][:], in_=xa[p][:])
            pl_h[i] = g.PL.n
            SY.to(evS[p]).after((g.DV, hist[i])).dma_start(
                out=g.xn1_f32[b, tt * 128:(tt + 1) * 128, :], in_=xn[p][:])
            SY.to(evS[p]).after((g.PL, pl_h[i])).dma_start(
                out=g.xn1_bf[(b * NT + tt) * 128:(b * NT + tt + 1) * 128, :], in_=xnb[p][:])
            SY.to(evS[p]).dma_start(out=g.xnattn_bf[b, tt * 128:(tt + 1) * 128, :], in_=xab[p][:])
    g.barrier()


def phase_e(g, b):
    nc = g.nc
    SY, GQ, V, A, TE = g.SY, g.GQ, g.V, g.A, g.TE
    from contextlib import ExitStack
    with ExitStack() as ph:
        en = ph.enter_context
        xn1fm = [en(nc.sbuf_tensor(f"pe{b}_fm{j}", [128, T], BF16)) for j in range(8)]
        prob = en(nc.sbuf_tensor(f"pe{b}_prob", [128, 8], F32))
        oh = en(nc.sbuf_tensor(f"pe{b}_oh", [128, 8], BF16))
        tmp8 = en(nc.sbuf_tensor(f"pe{b}_tmp8", [128, 8], F32))
        cols = en(nc.sbuf_tensor(f"pe{b}_cols", [128, 8], F32))
        ps_l = en(nc.psum_tensor(f"pe{b}_psl", [128, 8], F32))
        ps_c = en(nc.psum_tensor(f"pe{b}_psc", [128, 8], F32))
        ps_t = en(nc.psum_tensor(f"pe{b}_pst", [1, 8], F32))
        for j in range(8):
            SY.dma_start(out=xn1fm[j][:], in_=g.xn1_bf[b * T:(b + 1) * T, j * 128:(j + 1) * 128],
                         transpose=True)
        for tt in range(NT):
            gt = b * NT + tt
            ts_ = slice(tt * 128, (tt + 1) * 128)
            if tt == 0:
                TE.after(g.LD, g.GD)
            TE.after((g.DV, g.DV.n), (g.AC, g.AC.n))
            for j in range(8):
                TE.matmul(out=ps_l[:], lhsT=xn1fm[j][:, ts_], rhs=g.c_wr1[:, j * 8:(j + 1) * 8],
                          start=(j == 0), stop=(j == 7))
            pe_mm = g.PE.n
            V.after((g.PE, pe_mm)).reduce_max(cols[:, 0:1], ps_l[:], axis=AX.X, negate=True)
            A.after(g.DV).activation(out=prob[:], in_=ps_l[:], func=AF.Exp, bias=cols[:, 0:1])
            V.after(g.AC).reduce_sum(cols[:, 1:2], prob[:], axis=AX.X)
            V.reciprocal(cols[:, 2:3], cols[:, 1:2])
            V.tensor_scalar(out=prob[:], in0=prob[:], scalar1=cols[:, 2:3], scalar2=None, op0=ALU.mult)
            V.reduce_max(cols[:, 3:4], prob[:], axis=AX.X)
            V.tensor_scalar(out=tmp8[:], in0=prob[:], scalar1=cols[:, 3:4], scalar2=None, op0=ALU.is_equal)
            V.scalar_tensor_tensor(out=tmp8[:], in0=tmp8[:], scalar=-1000.0, in1=g.c_iota8[:],
                                   op0=ALU.mult, op1=ALU.add)
            V.tensor_reduce(cols[:, 4:5], tmp8[:], axis=AX.X, op=ALU.min)
            V.tensor_scalar(out=cols[:, 5:6], in0=cols[:, 4:5], scalar1=1000.0, scalar2=None,
                            op0=ALU.add)
            V.tensor_scalar(out=oh[:], in0=g.c_iota8[:], scalar1=cols[:, 5:6], scalar2=None, op0=ALU.is_equal)
            TE.after((g.DV, g.DV.n))
            TE.matmul(out=ps_c[:], lhsT=g.c_stril[:], rhs=oh[:], start=True, stop=True)
            TE.matmul(out=ps_t[:], lhsT=g.c_onec[:], rhs=oh[:], start=True, stop=True)
            pe_mm2 = g.PE.n
            V.after((g.PE, pe_mm2)).tensor_tensor(out=tmp8[:], in0=ps_c[:], in1=oh[:], op=ALU.mult)
            V.reduce_sum(g.route[:, gt * 3:gt * 3 + 1], tmp8[:], axis=AX.X)
            V.tensor_copy(out=g.route[:, gt * 3 + 1:gt * 3 + 2], in_=cols[:, 5:6])
            V.tensor_copy(out=g.route[:, gt * 3 + 2:gt * 3 + 3], in_=cols[:, 3:4])
            V.tensor_copy(out=g.oh_all[:, gt * 8:(gt + 1) * 8], in_=oh[:])
            V.tensor_copy(out=g.ttot[:, gt * 8:(gt + 1) * 8], in_=ps_t[:])
    g.barrier()


def phase_e2(g):
    nc = g.nc
    SY, GQ, V, A, TE = g.SY, g.GQ, g.V, g.A, g.TE
    NTILE = BPC * NT
    from contextlib import ExitStack
    with ExitStack() as ph:
        en = ph.enter_context
        pp0 = en(nc.sbuf_tensor("pe2_pp0", [1, NTILE * 8], F32))
        pp1 = en(nc.sbuf_tensor("pe2_pp1", [1, NTILE * 8], F32))
        cnts = en(nc.sbuf_tensor("pe2_cnts", [8, 8], F32))
        cmv = en(nc.sbuf_tensor("pe2_cmv", [8, 1], F32))
        offr = en(nc.sbuf_tensor("pe2_offr", [1, 8], F32))
        offb = en(nc.sbuf_tensor("pe2_offb", [128, 8], F32))
        tbb = en(nc.sbuf_tensor("pe2_tbb", [128, 8], F32))
        t8 = en(nc.sbuf_tensor("pe2_t8", [128, 8], F32))
        colz = en(nc.sbuf_tensor("pe2_colz", [128, 8], F32))
        gidu = en(nc.sbuf_tensor("pe2_gidu", [128, 1], U32))
        lidu = en(nc.sbuf_tensor("pe2_lidu", [128, 1], U32))
        cbcol = en(nc.sbuf_tensor("pe2_cbcol", [128, 1], F32))
        mk8 = en(nc.sbuf_tensor("pe2_mk8", [1, 8], F32))
        ps8 = en(nc.psum_tensor("pe2_ps8", [8, 8], F32))
        ps1 = en(nc.psum_tensor("pe2_ps1", [128, 8], F32))

        V.tensor_copy(out=pp0[:], in_=g.ttot[:])
        cur, oth = pp0, pp1
        s = 1
        while s < NTILE:
            V.tensor_copy(out=oth[:, 0:s * 8], in_=cur[:, 0:s * 8])
            V.tensor_tensor(out=oth[:, s * 8:], in0=cur[:, s * 8:],
                            in1=cur[:, 0:(NTILE - s) * 8], op=ALU.add)
            cur, oth = oth, cur
            s *= 2
        if cur is not pp0:
            V.tensor_copy(out=pp0[:], in_=cur[:])
        V.tensor_tensor(out=pp1[:], in0=pp0[:], in1=g.ttot[:], op=ALU.subtract)
        SY.to(g.ST).after(g.DV).dma_start(out=g.cc_cnt_in, in_=pp0[:, (NTILE - 1) * 8:NTILE * 8])
        g.ST.wait(nc.gpsimd)
        g.LD.wait(nc.gpsimd)
        g.CC0.inc(GQ.raw.collective_compute(
            "AllGather", ALU.bypass, replica_groups=g.rg,
            ins=[g.cc_cnt_in.opt()], outs=[g.cc_cnt_out.opt()]))
        g.CC1.inc(GQ.raw.collective_compute(
            "AllGather", ALU.bypass, replica_groups=g.rg,
            ins=[g.xn1_bf.opt()], outs=[g.xn1_all.opt()]))
        TE.after(g.LD).matmul(out=ps1[:, 0:1], lhsT=g.c_ones1f[:], rhs=g.c_cvec[0:1, 0:1],
                              start=True, stop=True)
        V.after((g.PE, g.PE.n)).tensor_copy(out=cbcol[:], in_=ps1[:, 0:1])
        for gt in range(NTILE):
            tbslice = pp1[:, gt * 8:(gt + 1) * 8]
            TE.after((g.DV, g.DV.n)).matmul(out=ps1[:], lhsT=g.c_ones1f[:], rhs=tbslice,
                                            start=True, stop=True)
            V.after((g.PE, g.PE.n), g.SC).tensor_copy(out=tbb[:], in_=ps1[:])
            ohsl = g.oh_all[:, gt * 8:(gt + 1) * 8]
            V.tensor_tensor(out=t8[:], in0=tbb[:], in1=ohsl, op=ALU.mult)
            V.reduce_sum(colz[:, 0:1], t8[:], axis=AX.X)
            V.tensor_tensor(out=colz[:, 1:2], in0=colz[:, 0:1],
                            in1=g.route[:, gt * 3:gt * 3 + 1], op=ALU.add)
            V.scalar_tensor_tensor(out=colz[:, 2:3], in0=g.route[:, gt * 3 + 1:gt * 3 + 2],
                                   scalar=8192.0, in1=colz[:, 1:2], op0=ALU.mult, op1=ALU.add)
            V.tensor_copy(out=lidu[:], in_=colz[:, 2:3])
            V.tensor_scalar(out=colz[:, 3:4], in0=g.c_iotap[:], scalar1=float(gt * 128),
                            scalar2=None, op0=ALU.add)
            V.tensor_tensor(out=colz[:, 3:4], in0=colz[:, 3:4], in1=cbcol[:], op=ALU.add)
            V.tensor_copy(out=gidu[:], in_=colz[:, 3:4])
            GQ.to(g.SC).after(g.DV, g.SC).indirect_dma_start(
                out=g.lists_pub, out_offset=IndirectOffsetOnAxis(ap=lidu[:], axis=0),
                in_=gidu[:], in_offset=None)
            V.tensor_copy(out=g.route[:, gt * 3:gt * 3 + 1], in_=colz[:, 1:2])
        g.SC.wait(nc.gpsimd)
        g.CC2.inc(GQ.raw.collective_compute(
            "AllGather", ALU.bypass, replica_groups=g.rg,
            ins=[g.lists_pub.opt()], outs=[g.lists_all.opt()]))
        SY.after(g.CC0)
        SY.dma_start(out=cnts[:], in_=g.cc_cnt_out)
        TE.after(g.LD).matmul(out=ps8[0:1, :], lhsT=g.c_cmask[:], rhs=cnts[:], start=True, stop=True)
        V.after((g.PE, g.PE.n)).tensor_copy(out=offr[:], in_=ps8[0:1, :])
        TE.after((g.DV, g.DV.n)).matmul(out=ps1[:], lhsT=g.c_ones1f[:], rhs=offr[:],
                                        start=True, stop=True)
        V.after((g.PE, g.PE.n)).tensor_copy(out=offb[:], in_=ps1[:])
        for gt in range(NTILE):
            ohsl = g.oh_all[:, gt * 8:(gt + 1) * 8]
            V.tensor_tensor(out=t8[:], in0=offb[:], in1=ohsl, op=ALU.mult)
            V.reduce_sum(colz[:, 0:1], t8[:], axis=AX.X)
            V.tensor_tensor(out=colz[:, 1:2], in0=colz[:, 0:1],
                            in1=g.route[:, gt * 3:gt * 3 + 1], op=ALU.add)
            V.tensor_scalar(out=g.keptf[:, gt:gt + 1], in0=colz[:, 1:2],
                            scalar1=float(CAP), scalar2=None, op0=ALU.is_lt)
            V.scalar_tensor_tensor(out=colz[:, 2:3], in0=g.route[:, gt * 3 + 1:gt * 3 + 2],
                                   scalar=float(CAP), in1=colz[:, 1:2], op0=ALU.mult, op1=ALU.add)
            V.tensor_tensor(out=colz[:, 2:3], in0=colz[:, 2:3], in1=g.keptf[:, gt:gt + 1], op=ALU.mult)
            V.tensor_scalar(out=colz[:, 3:4], in0=g.keptf[:, gt:gt + 1], scalar1=-1.0e6,
                            scalar2=1.0e6, op0=ALU.mult, op1=ALU.add)
            V.tensor_tensor(out=g.slotf[:, gt:gt + 1], in0=colz[:, 2:3], in1=colz[:, 3:4], op=ALU.add)
    g.barrier()


def phase_f(g, b):
    nc = g.nc
    SY, SD, GQ, V, A, P, TE = g.SY, g.SD, g.GQ, g.V, g.A, g.P, g.TE
    RT = 1.0 / float(np.sqrt(S1))
    NQ = 4
    QW = T // NQ
    from contextlib import ExitStack
    with ExitStack() as ph:
        en = ph.enter_context
        xafm = [en(nc.sbuf_tensor(f"pf{b}_xa{j}", [128, T], BF16)) for j in range(8)]
        vtm = [en(nc.sbuf_tensor(f"pf{b}_v{t}", [128, E1], BF16)) for t in range(NT)]
        kT = [en(nc.sbuf_tensor(f"pf{b}_kT{j}", [128, QW], BF16)) for j in range(NT)]
        Pq = [en(nc.sbuf_tensor(f"pf{b}_P{ec}", [128, QW], BF16)) for ec in range(16)]
        qb = en(nc.sbuf_tensor(f"pf{b}_qb", [128, T], BF16))
        kb = en(nc.sbuf_tensor(f"pf{b}_kb", [128, T], BF16))
        scr = en(nc.sbuf_tensor(f"pf{b}_scr", [128, QW], F32))
        scr2 = en(nc.sbuf_tensor(f"pf{b}_scr2", [128, QW], F32))
        uvbv = en(nc.sbuf_tensor(f"pf{b}_uvbv", [128, 2048], F32))
        ps1 = [en(nc.psum_tensor(f"pf{b}_ps1{i}", [128, 512], F32)) for i in range(2)]
        ps3 = [en(nc.psum_tensor(f"pf{b}_ps3{i}", [128, 512], F32)) for i in range(2)]
        ps5 = [en(nc.psum_tensor(f"pf{b}_ps5{i}", [128, 512], F32)) for i in range(2)]

        evXA = g.ev(ph, f"pf{b}_evXA")
        evUV = g.ev(ph, f"pf{b}_evUV")
        evSC = g.ev(ph, f"pf{b}_evSC")
        evWM = g.ev(ph, f"pf{b}_evWM")
        evWS = [g.ev(ph, f"pf{b}_evWS{i}") for i in range(2)]
        evB = [g.ev(ph, f"pf{b}_evB{i}") for i in range(2)]
        evU2 = [g.ev(ph, f"pf{b}_evU2{i}") for i in range(2)]
        evWF = g.ev(ph, f"pf{b}_evWF")
        evO = [g.ev(ph, f"pf{b}_evO{i}") for i in range(2)]
        evOB = g.ev(ph, f"pf{b}_evOB")
        for j in range(8):
            SY.to(evXA).dma_start(out=xafm[j][:], in_=g.xnattn_bf[b, :, j * 128:(j + 1) * 128], transpose=True)
        SY.to(evUV).dma_start(out=uvbv[:], in_=g.uv_b[0:1, E1:2 * E1].broadcast_to([128, 2048]))

        with ExitStack() as sb_:
            en2 = sb_.enter_context
            basef = en2(nc.sbuf_tensor(f"pf{b}_base", [128, T], F32))
            qkq = en2(nc.sbuf_tensor(f"pf{b}_qkq", [128, QW], F32))
            rr1 = en2(nc.sbuf_tensor(f"pf{b}_rr1", [64, QW], F32))
            rr2 = en2(nc.sbuf_tensor(f"pf{b}_rr2", [64, QW], F32))
            rr3 = en2(nc.sbuf_tensor(f"pf{b}_rr3", [64, QW], F32))
            sn = en2(nc.sbuf_tensor(f"pf{b}_sn", [64, T], F32))
            cs = en2(nc.sbuf_tensor(f"pf{b}_cs", [64, T], F32))
            wsm = [en2(nc.sbuf_tensor(f"pf{b}_wsm{j}", [128, 128], BF16)) for j in range(8)]
            SY.to(evSC).dma_start(out=sn[:], in_=g.sinT1)
            SY.to(evSC).dma_start(out=cs[:], in_=g.cosT1)
            for j in range(8):
                GQ.to(evWM).dma_start(out=wsm[j][:], in_=g.uv_wTb[j * 128:(j + 1) * 128, 2 * E1:2 * E1 + S1])
            ac_sil = {}
            for tq in range(NQ):
                qs = slice(tq * QW, (tq + 1) * QW)
                if tq == 0:
                    TE.after(evXA, evWM)
                if tq >= 2:
                    TE.after((g.AC, ac_sil[tq - 2]))
                for j in range(8):
                    TE.matmul(out=ps1[tq % 2][:], lhsT=wsm[j][:], rhs=xafm[j][:, qs],
                              start=(j == 0), stop=(j == 7))
                A.after((g.PE, g.PE.n)).activation(out=basef[:, qs], in_=ps1[tq % 2][:],
                                                   func=AF.Silu, bias=g.c_uvb[:, 32:33])
                ac_sil[tq] = g.AC.n
            for tq in range(NQ):
                qs = slice(tq * QW, (tq + 1) * QW)
                snq = sn[:, qs]
                csq = cs[:, qs]
                for gi, (dst, gcol, bcol) in enumerate(((qb, 0, 2), (kb, 1, 3))):
                    V.after((g.AC, ac_sil[tq]), evSC).tensor_scalar(
                        out=qkq[:], in0=basef[:, qs], scalar1=g.c_gam[:, gcol:gcol + 1],
                        scalar2=g.c_gam[:, bcol:bcol + 1], op0=ALU.mult, op1=ALU.add)
                    V.tensor_copy(out=rr2[:], in_=qkq[64:128, :])
                    V.tensor_tensor(out=rr1[:], in0=qkq[0:64, :], in1=csq, op=ALU.mult)
                    V.tensor_tensor(out=rr3[:], in0=rr2[:], in1=snq, op=ALU.mult)
                    V.tensor_tensor(out=rr1[:], in0=rr1[:], in1=rr3[:], op=ALU.subtract)
                    V.tensor_tensor(out=rr3[:], in0=rr2[:], in1=csq, op=ALU.mult)
                    V.tensor_tensor(out=rr2[:], in0=qkq[0:64, :], in1=snq, op=ALU.mult)
                    V.tensor_tensor(out=rr3[:], in0=rr3[:], in1=rr2[:], op=ALU.add)
                    V.tensor_copy(out=dst[0:64, qs], in_=rr1[:])
                    V.tensor_copy(out=dst[64:128, qs], in_=rr3[:])
            dv_rope = g.DV.n

        vstack = ExitStack()
        wsl = [[vstack.enter_context(nc.sbuf_tensor(f"pf{b}_wsl{i}_{j}", [128, 512], BF16)) for j in range(8)]
               for i in range(2)]
        gd_wsl = {}
        dv_v = {}
        GQ.after((g.DV, dv_rope))
        for ec4 in range(4):
            if ec4 < 2:
                for j in range(8):
                    GQ.to(evWS[ec4]).dma_start(out=wsl[ec4][j][:],
                                 in_=g.uv_wTb[j * 128:(j + 1) * 128, E1 + ec4 * 512:E1 + (ec4 + 1) * 512])
        for ec4 in range(4):
            es = slice(ec4 * 512, (ec4 + 1) * 512)
            wp = ec4 % 2
            if ec4 >= 2:
                pass
            for t in range(NT):
                i = ec4 * NT + t
                pp = i % 2
                if i == 0:
                    TE.after(evWS[ec4 % 2], evUV, (g.DV, dv_rope))
                elif t == 0:
                    TE.after(evWS[ec4 % 2])
                if i >= 2:
                    TE.after((g.DV, dv_v[i - 2]))
                for j in range(8):
                    TE.matmul(out=ps1[pp][:], lhsT=xafm[j][:, t * 128:(t + 1) * 128],
                              rhs=wsl[wp][j][:], start=(j == 0), stop=(j == 7))
                pe_now = g.PE.n
                V.after((g.PE, pe_now), evUV).tensor_tensor(out=(scr if pp == 0 else scr2)[:],
                                                      in0=ps1[pp][:], in1=uvbv[:, es], op=ALU.add)
                dv_v[i] = g.DV.n
                A.after((g.DV, dv_v[i])).activation(out=vtm[t][:, es],
                                                    in_=(scr if pp == 0 else scr2)[:], func=AF.Silu)
            if ec4 < 2:
                GQ.after((g.PE, g.PE.n))
                if ec4 + 2 < 4:
                    for j in range(8):
                        GQ.to(evWS[wp]).dma_start(out=wsl[wp][j][:],
                                     in_=g.uv_wTb[j * 128:(j + 1) * 128, E1 + (ec4 + 2) * 512:E1 + (ec4 + 3) * 512])
        ac_vlast = g.AC.n
        pe_vlast = g.PE.n
        vstack.close()

        bias2 = [en(nc.sbuf_tensor(f"pf{b}_bias{i}", [128, QW], F32)) for i in range(2)]
        wsm2 = [[en(nc.sbuf_tensor(f"pf{b}_wsm2{i}_{j}", [128, 128], BF16)) for j in range(8)]
                for i in range(2)]
        wof = [en(nc.sbuf_tensor(f"pf{b}_wof{ec}", [128, 512], BF16)) for ec in range(16)]
        obb = en(nc.sbuf_tensor(f"pf{b}_obb", [128, 512], F32))
        ub = [en(nc.sbuf_tensor(f"pf{b}_ub{i}", [128, QW], BF16)) for i in range(2)]
        x1t = [en(nc.sbuf_tensor(f"pf{b}_x1t{i}", [128, 512], F32)) for i in range(2)]
        o1 = [en(nc.sbuf_tensor(f"pf{b}_o1{i}", [128, 512], F32)) for i in range(2)]
        TE.after((g.AC, ac_vlast))
        dv_kt_last = None
        for tq in range(NQ):
            qs = slice(tq * QW, (tq + 1) * QW)
            pl_prev = g.PL.n
            ld_bias = {}
            dv_kt = {}
            pl_kt = {}
            if dv_kt_last is not None:
                SY.after((g.DV, dv_kt_last))
            else:
                SY.after((g.PE, pe_vlast))
            for j in range(2):
                SY.to(evB[j]).dma_start(out=bias2[j][:], in_=g.biasT[j * 128:(j + 1) * 128, qs])
            for j in range(NT):
                p = j % 2
                if j >= 2:
                    TE.after((g.DV, dv_kt[j - 2]))
                TE.matmul(out=ps3[p][:], lhsT=kb[:, j * 128:(j + 1) * 128], rhs=qb[:, qs],
                          start=True, stop=True)
                pe_s = g.PE.n
                if j < 2:
                    V.after((g.AC, ac_vlast), (g.PL, pl_prev))
                else:
                    V.after((g.PL, pl_kt[j - 2]))
                V.after((g.PE, pe_s), evB[p]).scalar_tensor_tensor(
                    out=(scr if p == 0 else scr2)[:], in0=ps3[p][:], scalar=RT, in1=bias2[p][:],
                    op0=ALU.mult, op1=ALU.add)
                dv_kt[j] = g.DV.n
                if j + 2 < NT:
                    SY.after((g.DV, dv_kt[j]))
                    SY.to(evB[p]).dma_start(out=bias2[p][:], in_=g.biasT[(j + 2) * 128:(j + 3) * 128, qs])
                P.after((g.DV, dv_kt[j])).tensor_scalar_max(
                    out=kT[j][:], in0=(scr if p == 0 else scr2)[:], scalar1=0.0)
                P.tensor_tensor(out=kT[j][:], in0=kT[j][:], in1=(scr if p == 0 else scr2)[:],
                                op=ALU.mult)
                pl_kt[j] = g.PL.n
                if j == NT - 1:
                    dv_kt_last = dv_kt[j]
            for ec in range(2):
                for j in range(8):
                    GQ.to(evU2[ec]).dma_start(out=wsm2[ec][j][:],
                                 in_=g.uv_wTb[j * 128:(j + 1) * 128, ec * 128:(ec + 1) * 128])
            ac_u = {}
            dv_p = {}
            for ec in range(16):
                p = ec % 2
                TE.after(evU2[p])
                if ec >= 2:
                    TE.after((g.AC, ac_u[ec - 2]))
                if ec == 0:
                    TE.after((g.PL, pl_kt[NT - 1]))
                for j in range(8):
                    TE.matmul(out=ps1[p][:], lhsT=wsm2[p][j][:], rhs=xafm[j][:, qs],
                              start=(j == 0), stop=(j == 7))
                pe_u = g.PE.n
                A.after((g.PE, pe_u)).activation(out=ub[p][:], in_=ps1[p][:],
                                                 func=AF.Silu, bias=g.c_uvb[:, ec:ec + 1])
                ac_u[ec] = g.AC.n
                if ec >= 2:
                    TE.after((g.DV, dv_p[ec - 2]))
                for j in range(NT):
                    TE.matmul(out=ps3[p][:], lhsT=vtm[j][:, ec * 128:(ec + 1) * 128], rhs=kT[j][:],
                              start=(j == 0), stop=(j == NT - 1))
                pe_o = g.PE.n
                V.after((g.PE, pe_o), (g.AC, ac_u[ec])).tensor_tensor(
                    out=Pq[ec][:], in0=ps3[p][:], in1=ub[p][:], op=ALU.mult)
                dv_p[ec] = g.DV.n
                if ec + 2 < 16:
                    GQ.after((g.PE, pe_u))
                    for j in range(8):
                        GQ.to(evU2[p]).dma_start(out=wsm2[p][j][:],
                                     in_=g.uv_wTb[j * 128:(j + 1) * 128, (ec + 2) * 128:(ec + 3) * 128])
            dv_o1 = {}
            for fh in range(2):
                GQ.after((g.PE, g.PE.n))
                for ec in range(16):
                    GQ.to(evWF).dma_start(out=wof[ec][:], in_=g.o_wTb[ec * 128:(ec + 1) * 128, fh * 512:(fh + 1) * 512])
                SY.after((g.DV, g.DV.n))
                SY.to(evOB).dma_start(out=obb[:], in_=g.o_b[0:1, fh * 512:(fh + 1) * 512].broadcast_to([128, 512]))
                for trel in range(4):
                    tt = tq * 4 + trel
                    p = trel % 2
                    ts_ = slice(tt * 128, (tt + 1) * 128)
                    prevkey = (fh, trel - 2) if trel >= 2 else ((0, trel + 2) if fh == 1 else None)
                    if prevkey is not None and prevkey in dv_o1:
                        SY.after((g.DV, dv_o1[prevkey]))
                    SY.to(evO[p]).dma_start(out=x1t[p][:], in_=g.x1p[b, ts_, fh * 512:(fh + 1) * 512])
                    SY.to(evO[p]).dma_start(out=o1[p][:], in_=g.xn1_f32[b, ts_, fh * 512:(fh + 1) * 512])
                    TE.after(evWF)
                    if prevkey is not None and prevkey in dv_o1:
                        TE.after((g.DV, dv_o1[prevkey]))
                    if fh == 0 and trel == 0:
                        TE.after((g.DV, dv_p[15]))
                    for ec in range(16):
                        TE.matmul(out=ps5[p][:], lhsT=Pq[ec][:, trel * 128:(trel + 1) * 128],
                                  rhs=wof[ec][:], start=(ec == 0), stop=(ec == 15))
                    pe_f = g.PE.n
                    V.after((g.PE, pe_f), evO[p], evOB, g.ST).tensor_tensor(
                        out=o1[p][:], in0=ps5[p][:], in1=o1[p][:], op=ALU.add)
                    V.tensor_tensor(out=o1[p][:], in0=o1[p][:], in1=obb[:], op=ALU.add)
                    V.tensor_tensor(out=o1[p][:], in0=o1[p][:], in1=x1t[p][:], op=ALU.add)
                    dv_o1[(fh, trel)] = g.DV.n
                    SY.to(g.ST).after((g.DV, dv_o1[(fh, trel)])).dma_start(
                        out=g.x2p[b, ts_, fh * 512:(fh + 1) * 512], in_=o1[p][:])
    g.barrier()


def phase_g(g):
    nc = g.nc
    SY, SD, GQ, V, A, P, TE = g.SY, g.SD, g.GQ, g.V, g.A, g.P, g.TE
    NSL = CAP // 128
    from contextlib import ExitStack
    with ExitStack() as ph:
        en = ph.enter_context
        w1b = [en(nc.sbuf_tensor(f"pg_w1{j}", [128, DFF], BF16)) for j in range(8)]
        w2b = [en(nc.sbuf_tensor(f"pg_w2{j}", [128, D], BF16)) for j in range(32)]
        cnts = en(nc.sbuf_tensor("pg_cnts", [8, 8], F32))
        cmv = en(nc.sbuf_tensor("pg_cmv", [8, 1], F32))
        orow = en(nc.sbuf_tensor("pg_orow", [1, 8], F32))
        crow = en(nc.sbuf_tensor("pg_crow", [1, 8], F32))
        obc = en(nc.sbuf_tensor("pg_obc", [128, 8], F32))
        mrow = en(nc.sbuf_tensor("pg_mrow", [1, 8], F32))
        mecol = en(nc.sbuf_tensor("pg_mecol", [128, 1], F32))
        colz = en(nc.sbuf_tensor("pg_colz", [128, 8], F32))
        t8 = en(nc.sbuf_tensor("pg_t8", [128, 8], F32))
        srcu = en(nc.sbuf_tensor("pg_srcu", [128, 1], U32))
        idu = en(nc.sbuf_tensor("pg_idu", [128, 1], U32))
        idsf = en(nc.sbuf_tensor("pg_idsf", [128, 1], F32))
        ein = [en(nc.sbuf_tensor(f"pg_ein{i}", [128, D], BF16)) for i in range(4)]
        einfm = en(nc.sbuf_tensor("pg_einfm", [128, 8 * 512], BF16))
        ps_t8 = en(nc.psum_tensor("pg_pst8", [128, 1024], BF16))
        hb = [en(nc.sbuf_tensor(f"pg_hb{fc}", [128, 512], BF16)) for fc in range(32)]
        eoutb = en(nc.sbuf_tensor("pg_eoutb", [128, D], BF16))
        ps_h = [en(nc.psum_tensor(f"pg_psh{i}", [128, 512], F32)) for i in range(2)]
        ps_e = [en(nc.psum_tensor(f"pg_pse{i}", [128, 512], F32)) for i in range(2)]
        ps_8 = en(nc.psum_tensor("pg_ps8", [1, 16], F32))
        ps_b = en(nc.psum_tensor("pg_psb", [128, 8], F32))

        for j in range(8):
            GQ.dma_start(out=w1b[j][:], in_=g.W1e[j * 128:(j + 1) * 128, :])
        for j in range(32):
            GQ.dma_start(out=w2b[j][:], in_=g.W2e[j * 128:(j + 1) * 128, :])
        SY.after(g.CC0).dma_start(out=cnts[:], in_=g.cc_cnt_out)
        V.after(g.LD).tensor_scalar(out=mrow[:], in0=g.c_iota8[0:1, :],
                                    scalar1=g.c_cvec[0:1, 2:3], scalar2=None, op0=ALU.is_equal)
        TE.after((g.DV, g.DV.n)).matmul(out=ps_b[0:8, :], lhsT=g.c_ones1f[0:1, 0:8], rhs=mrow[:],
                                        start=True, stop=True)
        V.after((g.PE, g.PE.n)).tensor_tensor(out=cnts[:], in0=cnts[:], in1=ps_b[0:8, :], op=ALU.mult)
        V.reduce_sum(cmv[:], cnts[:], axis=AX.X)
        TE.after((g.DV, g.DV.n))
        TE.matmul(out=ps_8[0:1, 0:8], lhsT=cmv[:], rhs=g.c_s8[:, 0:8], start=True, stop=True)
        TE.matmul(out=ps_8[0:1, 8:16], lhsT=cmv[:], rhs=g.c_s8[:, 8:16], start=True, stop=True)
        V.after((g.PE, g.PE.n)).tensor_copy(out=orow[:], in_=ps_8[0:1, 0:8])
        V.tensor_copy(out=crow[:], in_=ps_8[0:1, 8:16])
        TE.after((g.DV, g.DV.n)).matmul(out=ps_b[:], lhsT=g.c_ones1f[:], rhs=orow[:],
                                        start=True, stop=True)
        V.after((g.PE, g.PE.n)).tensor_copy(out=obc[:], in_=ps_b[:])
        TE.after((g.DV, g.DV.n)).matmul(out=ps_b[:, 0:1], lhsT=g.c_ones1f[:], rhs=g.c_cvec[0:1, 2:3],
                                        start=True, stop=True)
        V.after((g.PE, g.PE.n)).tensor_scalar(out=mecol[:], in0=ps_b[:, 0:1], scalar1=8192.0,
                                              scalar2=None, op0=ALU.mult)

        GQ.after(g.CC1, g.CC2)
        for st in range(NSL):
            q = st % 4
            V.after(g.GD, g.SC, (g.PE, g.PE.n))
            V.tensor_scalar(out=colz[:, 0:1], in0=g.c_iotap[:], scalar1=float(st * 128),
                            scalar2=None, op0=ALU.add)
            V.tensor_scalar(out=t8[:], in0=obc[:], scalar1=colz[:, 0:1], scalar2=None, op0=ALU.is_le)
            V.tensor_scalar(out=colz[:, 1:2], in0=colz[:, 0:1], scalar1=0.0, scalar2=None,
                            op0=ALU.mult)
            V.reduce_sum(colz[:, 1:2], t8[:], axis=AX.X)
            V.tensor_scalar(out=colz[:, 1:2], in0=colz[:, 1:2], scalar1=-1.0, scalar2=None,
                            op0=ALU.add)
            V.tensor_scalar(out=t8[:], in0=g.c_iota8[:], scalar1=colz[:, 1:2], scalar2=None,
                            op0=ALU.is_equal)
            V.tensor_tensor(out=t8[:], in0=t8[:], in1=obc[:], op=ALU.mult)
            V.reduce_sum(colz[:, 2:3], t8[:], axis=AX.X)
            V.tensor_tensor(out=colz[:, 3:4], in0=colz[:, 0:1], in1=colz[:, 2:3], op=ALU.subtract)
            V.tensor_scalar(out=colz[:, 4:5], in0=colz[:, 1:2], scalar1=65536.0, scalar2=None,
                            op0=ALU.mult)
            V.tensor_tensor(out=colz[:, 4:5], in0=colz[:, 4:5], in1=mecol[:], op=ALU.add)
            V.tensor_tensor(out=colz[:, 4:5], in0=colz[:, 4:5], in1=colz[:, 3:4], op=ALU.add)
            V.tensor_scalar(out=colz[:, 4:5], in0=colz[:, 4:5], scalar1=float(8 * E0 * 8192 - 1),
                            scalar2=None, op0=ALU.min)
            V.tensor_copy(out=srcu[:], in_=colz[:, 4:5])
            GQ.after(g.DV)
            GQ.indirect_dma_start(out=idu[:], out_offset=None, in_=g.lists_all,
                                  in_offset=IndirectOffsetOnAxis(ap=srcu[:], axis=0))
            V.after(g.GD).tensor_copy(out=idsf[:], in_=idu[:])
            V.tensor_scalar(out=idsf[:], in0=idsf[:], scalar1=float(NCORES * BPC * T - 1),
                            scalar2=None, op0=ALU.min)
            V.tensor_copy(out=idu[:], in_=idsf[:])
            GQ.after(g.DV)
            GQ.indirect_dma_start(out=ein[q][:], out_offset=None, in_=g.xn1_all,
                                  in_offset=IndirectOffsetOnAxis(ap=idu[:], axis=0))
            if q == 3:
                ch = st // 4
                for tt4 in range(4):
                    TE.after(g.GD, (g.DV, g.DV.n))
                    for j in range(8):
                        TE.transpose(out=ps_t8[:, j * 128:(j + 1) * 128],
                                     in_=ein[tt4][:, j * 128:(j + 1) * 128],
                                     identity=g.c_ident_bf[:])
                    V.after((g.PE, g.PE.n)).tensor_copy(
                        out=einfm.rearrange("p (j c) -> p j c", j=8)
                        [:, :, tt4 * 128:(tt4 + 1) * 128],
                        in_=ps_t8[:].rearrange("p (j c) -> p j c", j=8))
                ac_h = {}
                dv_tt = {}
                for fc in range(32):
                    p = fc % 2
                    if fc == 0:
                        TE.after((g.DV, g.DV.n))
                    if fc >= 2:
                        TE.after((g.AC, ac_h[fc - 2]))
                    for j in range(8):
                        TE.matmul(out=ps_h[p][:], lhsT=w1b[j][:, fc * 128:(fc + 1) * 128],
                                  rhs=einfm[:, j * 512:(j + 1) * 512], start=(j == 0), stop=(j == 7))
                    A.after((g.PE, g.PE.n)).activation(out=hb[fc][:], in_=ps_h[p][:], func=AF.Relu)
                    ac_h[fc] = g.AC.n
                for tt4 in range(4):
                    for fh in range(2):
                        i6 = tt4 * 2 + fh
                        p = i6 % 2
                        if i6 == 0:
                            TE.after((g.AC, ac_h[31]))
                        if i6 >= 2:
                            TE.after((g.DV, dv_tt[i6 - 2]))
                        for fc in range(32):
                            TE.matmul(out=ps_e[p][:],
                                      lhsT=hb[fc][:, tt4 * 128:(tt4 + 1) * 128],
                                      rhs=w2b[fc][:, fh * 512:(fh + 1) * 512],
                                      start=(fc == 0), stop=(fc == 31))
                        V.after((g.PE, g.PE.n), g.ST).tensor_copy(
                            out=eoutb[:, fh * 512:(fh + 1) * 512], in_=ps_e[p][:])
                        dv_tt[i6] = g.DV.n
                    SY.to(g.ST).after((g.DV, dv_tt[tt4 * 2 + 1])).dma_start(
                        out=g.eout_pub[(ch * 4 + tt4) * 128:(ch * 4 + tt4 + 1) * 128, :],
                        in_=eoutb[:])
        g.ST.wait(nc.gpsimd)
        g.CC3.inc(GQ.raw.collective_compute(
            "AllGather", ALU.bypass, replica_groups=g.rg,
            ins=[g.eout_pub.opt()], outs=[g.eout_all.opt()]))
    g.barrier()


def phase_h(g):
    nc = g.nc
    SY, GQ, V, A, P = g.SY, g.GQ, g.V, g.A, g.P
    from contextlib import ExitStack
    with ExitStack() as ph:
        en = ph.enter_context
        x2t = [en(nc.sbuf_tensor(f"ph_x2{i}", [128, D], F32)) for i in range(2)]
        xnt = [en(nc.sbuf_tensor(f"ph_xn{i}", [128, D], F32)) for i in range(2)]
        gtb = [en(nc.sbuf_tensor(f"ph_gt{i}", [128, D], BF16)) for i in range(2)]
        gf = en(nc.sbuf_tensor("ph_gf", [128, D], F32))
        ot = [en(nc.sbuf_tensor(f"ph_ot{i}", [128, D], F32)) for i in range(2)]
        slotu = en(nc.sbuf_tensor("ph_slotu", [128, BPC * NT], U32))
        fac = en(nc.sbuf_tensor("ph_fac", [128, 2 * BPC * NT], F32))
        evL = [g.ev(ph, f"ph_evL{i}") for i in range(2)]
        evG = [g.ev(ph, f"ph_evG{i}") for i in range(2)]
        NTILE = BPC * NT
        V.tensor_scalar(out=fac[:, 0:NTILE], in0=g.slotf[:], scalar1=float(8 * CAP - 1),
                        scalar2=None, op0=ALU.min)
        V.tensor_copy(out=slotu[:], in_=fac[:, 0:NTILE])
        V.tensor_tensor(out=fac[:, 0:NTILE], in0=g.route[:, 2::3], in1=g.keptf[:], op=ALU.mult)
        V.tensor_scalar(out=fac[:, NTILE:], in0=g.keptf[:], scalar1=-1.0, scalar2=1.0,
                        op0=ALU.mult, op1=ALU.add)
        dv_pre = g.DV.n
        dv_cmb = {}
        for gt in range(NTILE + 2):
            p = gt % 2
            if gt >= 2:
                go = gt - 2
                q = go % 2
                b2, tt2 = divmod(go, NT)
                ts2 = slice(tt2 * 128, (tt2 + 1) * 128)
                V.after(evG[q], evL[q], g.ST)
                V.tensor_copy(out=gf[:], in_=gtb[q][:])
                V.tensor_scalar(out=gf[:], in0=gf[:], scalar1=fac[:, go:go + 1], scalar2=None, op0=ALU.mult)
                V.tensor_scalar(out=ot[q][:], in0=xnt[q][:], scalar1=fac[:, NTILE + go:NTILE + go + 1],
                                scalar2=None, op0=ALU.mult)
                V.tensor_tensor(out=ot[q][:], in0=ot[q][:], in1=gf[:], op=ALU.add)
                V.tensor_tensor(out=ot[q][:], in0=ot[q][:], in1=x2t[q][:], op=ALU.add)
                dv_cmb[go] = g.DV.n
                SY.to(g.ST).after((g.DV, dv_cmb[go])).dma_start(out=g.out[b2, ts2, :], in_=ot[q][:])
            if gt < NTILE:
                b, tt = divmod(gt, NT)
                ts_ = slice(tt * 128, (tt + 1) * 128)
                if gt >= 2:
                    SY.after((g.DV, dv_cmb[gt - 2]))
                SY.to(evL[p]).dma_start(out=x2t[p][:], in_=g.x2p[b, ts_, :])
                SY.to(evL[p]).dma_start(out=xnt[p][:], in_=g.xn1_f32[b, ts_, :])
                if gt == 0:
                    GQ.after((g.DV, dv_pre), g.CC3)
                if gt >= 2:
                    GQ.after((g.DV, dv_cmb[gt - 2]))
                GQ.to(evG[p]).indirect_dma_start(
                    out=gtb[p][:], out_offset=None, in_=g.eout_all,
                    in_offset=IndirectOffsetOnAxis(ap=slotu[:, gt:gt + 1], axis=0))
    g.barrier()


def finish(g):
    nc = g.nc
    nc.all_engine_barrier()
    g.LD.wait(nc.sync)
    g.LD2.wait(nc.sync)
    g.GD.wait(nc.sync)
    g.SC.wait(nc.sync)
    g.ST.wait(nc.sync)



def kernel(**inputs):
    in_maps = _host_prep(inputs)
    nc = build_program()
    res = run_bass_kernel_spmd(nc, in_maps, core_ids=list(range(NCORES)))
    outs = [np.asarray(r["out"], np.float32).reshape(BPC, T, D) for r in res.results]
    full = np.concatenate(outs, axis=0)
    kernel._last_results = res
    return full


# version 07 (final)
# speedup vs baseline: 1.0119x; 1.0119x over previous
import numpy as np
import ml_dtypes
import concourse.bass as bass
import concourse.mybir as mybir
from concourse.bass_utils import run_bass_kernel_spmd
from concourse.bass import IndirectOffsetOnAxis

F32 = mybir.dt.float32
BF16 = mybir.dt.bfloat16
U32 = mybir.dt.uint32
U8 = mybir.dt.uint8
AF = mybir.ActivationFunctionType
ALU = mybir.AluOpType
AX = mybir.AxisListType
BF = ml_dtypes.bfloat16

NCORES = 8
BPC = 4
T = 2048
D = 1024
NT = T // 128
H0 = 16
CS = 16
K0 = 256
E0 = 8
HP = 128
E1 = 2048
S1 = 128
L1L = 512
DFF = 4096
CAP = 10240
LN_EPS = 1e-5

_LAST_PHASE = "ALL"
_DEBUG_TAPS = []



def _host_constants():
    c = {}
    c["ident_bf"] = np.eye(128, dtype=BF)
    c["ident_f"] = np.eye(128, dtype=np.float32)
    c["stril_bf"] = np.tril(np.ones((128, 128), dtype=np.float32), -1).astype(BF)
    c["ones1_f"] = np.ones((1, 128), dtype=np.float32)
    c["ones1_bf"] = np.ones((1, 128), dtype=BF)
    c["iota_p"] = np.arange(128, dtype=np.float32).reshape(128, 1)
    c["iota_r"] = np.broadcast_to(np.arange(K0, dtype=np.float32), (128, K0)).copy()
    c["iota8"] = np.broadcast_to(np.arange(8, dtype=np.float32), (128, 8)).copy()
    s8 = np.zeros((8, 24), dtype=np.float32)
    s8[:, 0:8] = np.triu(np.ones((8, 8)), 1)
    s8[:, 8:16] = np.eye(8)
    s8[:, 16] = 1.0
    c["stril8f"] = s8
    c["onecol_bf"] = np.ones((128, 1), dtype=BF)
    sm = np.zeros((128, 32), dtype=np.float32)
    for ch in range(16):
        sm[:, 2 * ch] = ch
        sm[:, 2 * ch + 1] = np.arange(128)
    c["smalls"] = sm.astype(BF)
    import jax
    import jax.numpy as jnp
    with jax.default_device(jax.devices("cpu")[0]):
        half = 32
        inv_freq = 1.0 / 10000.0 ** (jnp.arange(half, dtype=jnp.float32) * 2.0 / 64.0)
        pos = jnp.arange(T, dtype=jnp.float32)
        ang = np.asarray(pos[:, None] * inv_freq[None, :])
        half1 = 64
        inv1 = 10000.0 ** (jnp.arange(half1, dtype=jnp.float32) / half1)
        ang1 = np.asarray(pos[:, None] * inv1[None, :])
    c["postable"] = np.concatenate([
        np.sin(ang.astype(np.float64)),
        np.cos(ang.astype(np.float64)),
    ], axis=1).astype(np.float32)
    c["sinT1"] = np.ascontiguousarray(np.sin(ang1.astype(np.float64)).astype(np.float32).T)
    c["cosT1"] = np.ascontiguousarray(np.cos(ang1.astype(np.float64)).astype(np.float32).T)
    return c


def _host_prep(inputs):
    f32 = np.float32
    cst = _host_constants()
    x = np.ascontiguousarray(np.asarray(inputs["L0_attn_x"], f32))
    W_inT = np.ascontiguousarray(np.asarray(inputs["L0_attn_W_in"], f32).T)
    W_gateT = np.ascontiguousarray(np.asarray(inputs["L0_attn_W_gate"], f32).T)
    W_outT = np.ascontiguousarray(np.asarray(inputs["L0_attn_W_out"], f32).T)
    Wr0T = np.ascontiguousarray(np.asarray(inputs["L0_ffn_Wr"], f32).T)
    uv_wT = np.ascontiguousarray(np.asarray(inputs["L1_attn_uv_w"], f32).T)
    o_wT = np.ascontiguousarray(np.asarray(inputs["L1_attn_o_w"], f32).T)
    w1 = np.asarray(inputs["L1_attn_w"], f32)
    ii = np.arange(T)
    rel = ii[None, :] - ii[:, None] + (L1L - 1)
    eff = np.where(rel < 0, rel + (2 * L1L - 1), rel)
    eff = np.clip(eff, 0, 2 * L1L - 2)
    biasT = np.ascontiguousarray(w1[eff].T)

    shared = {
        "W_inT": W_inT, "W_gateT": W_gateT, "W_outT": W_outT, "Wr0T": Wr0T,
        "ln_g": np.asarray(inputs["L0_attn_ln_g"], f32).reshape(1, D),
        "ln_b": np.asarray(inputs["L0_attn_ln_b"], f32).reshape(1, D),
        "b_gate": np.asarray(inputs["L0_attn_b_gate"], f32).reshape(1, 32),
        "eig_raw": np.asarray(inputs["L0_attn_eig_raw"], f32).reshape(16, 1),
        "Wqkv": np.ascontiguousarray(np.asarray(inputs["L0_ffn_Wqkv"], f32)),
        "Wo0": np.ascontiguousarray(np.asarray(inputs["L0_ffn_Wo"], f32)),
        "ln1_w": np.asarray(inputs["L1_attn_ln_w"], f32).reshape(1, D),
        "ln1_b": np.asarray(inputs["L1_attn_ln_b"], f32).reshape(1, D),
        "uv_wT": uv_wT,
        "uv_b": np.asarray(inputs["L1_attn_uv_b"], f32).reshape(1, 4224),
        "gam": np.ascontiguousarray(np.asarray(inputs["L1_attn_gamma"], f32).reshape(2, S1)),
        "bet": np.ascontiguousarray(np.asarray(inputs["L1_attn_beta"], f32).reshape(2, S1)),
        "biasT": biasT, "o_wT": o_wT,
        "o_b": np.asarray(inputs["L1_attn_o_b"], f32).reshape(1, D),
        "Wr1": np.ascontiguousarray(np.asarray(inputs["L1_ffn_Wr"], f32)),
        **cst,
    }
    W1 = np.asarray(inputs["L1_ffn_W1"], f32)
    W2 = np.asarray(inputs["L1_ffn_W2"], f32)

    in_maps = []
    for c in range(NCORES):
        m = dict(shared)
        m["x4"] = np.ascontiguousarray(x[BPC * c:BPC * (c + 1)])
        m["W1e"] = np.ascontiguousarray(W1[c])
        m["W2e"] = np.ascontiguousarray(W2[c])
        cm = np.zeros((8, 1), f32)
        cm[:c] = 1.0
        m["cmask"] = cm
        m["cvec"] = np.array([[c * BPC * T, c * CAP, float(c)]], f32)
        in_maps.append(m)
    return in_maps



class Ev:

    def __init__(self, st, nc, name):
        self.sem = st.enter_context(nc.semaphore(name))
        self.n = 0

    def inc(self, instr, k=1):
        instr.then_inc(self.sem, k)
        self.n += k
        return instr

    def dma(self, instr):
        return self.inc(instr, 16)

    def wait(self, eng, val=None):
        eng.wait_ge(self.sem, self.n if val is None else val)


_DMA_OPS = {"dma_start", "dma_start_transpose", "indirect_dma_start"}


class Q:

    _CTRL = {"eng", "ev", "auto", "_deps", "after", "wait_ge", "raw"}

    def __init__(self, eng, ev, auto=True):
        self.eng = eng
        self.ev = ev
        self.auto = auto
        self._deps = []
        self._ev_override = None

    def after(self, *evs):
        self._deps.extend(evs)
        return self

    def to(self, ev):
        self._ev_override = ev
        return self

    def wait_ge(self, sem, val):
        self.eng.wait_ge(sem, val)

    @property
    def raw(self):
        return self.eng

    def __getattr__(self, name):
        f = getattr(object.__getattribute__(self, "eng"), name)
        if not callable(f):
            return f

        def call(*a, **kw):
            if self.auto:
                self.ev.wait(self.eng)
            for d in self._deps:
                if isinstance(d, tuple):
                    d[0].wait(self.eng, d[1])
                else:
                    d.wait(self.eng)
            del self._deps[:]
            ev = self._ev_override or self.ev
            self._ev_override = None
            ins = f(*a, **kw)
            return ev.inc(ins, 16 if name in _DMA_OPS else 1)
        return call


class Ctx:
    def ev(self, stack, name):
        return Ev(stack, self.nc, name)


def build_program():
    nc = bass.Bass("TRN2", target_bir_lowering=False)
    from contextlib import ExitStack
    st = ExitStack()

    g = Ctx()
    g.nc = nc
    g.st = st

    def din(name, shape, dt=F32):
        return nc.dram_tensor(name, shape, dt, kind="ExternalInput").ap()

    def dram(name, shape, dt=F32, out=False):
        kind = "ExternalOutput" if (out or name in _DEBUG_TAPS) else "Internal"
        return nc.dram_tensor(name, shape, dt, kind=kind).ap()

    g.x4 = din("x4", [BPC, T, D])
    g.W_inT = din("W_inT", [D, D]); g.ln_g = din("ln_g", [1, D]); g.ln_b = din("ln_b", [1, D])
    g.W_gateT = din("W_gateT", [D, 32]); g.b_gate = din("b_gate", [1, 32])
    g.eig_raw = din("eig_raw", [16, 1]); g.W_outT = din("W_outT", [D, D])
    g.Wr0T = din("Wr0T", [D, 8]); g.Wqkv = din("Wqkv", [E0, D, 3 * HP]); g.Wo0 = din("Wo0", [E0, HP, D])
    g.postable = din("postable", [T, 64])
    g.ln1_w = din("ln1_w", [1, D]); g.ln1_b = din("ln1_b", [1, D])
    g.uv_wT = din("uv_wT", [D, 4224]); g.uv_b = din("uv_b", [1, 4224])
    g.gam = din("gam", [2, S1]); g.bet = din("bet", [2, S1])
    g.biasT = din("biasT", [T, T]); g.o_wT = din("o_wT", [E1, D]); g.o_b = din("o_b", [1, D])
    g.sinT1 = din("sinT1", [64, T]); g.cosT1 = din("cosT1", [64, T])
    g.Wr1 = din("Wr1", [D, 8]); g.W1e = din("W1e", [D, DFF]); g.W2e = din("W2e", [DFF, D])
    g.ident_bf = din("ident_bf", [128, 128], BF16); g.ident_f = din("ident_f", [128, 128])
    g.stril_bf = din("stril_bf", [128, 128], BF16)
    g.ones1_f = din("ones1_f", [1, 128]); g.ones1_bf = din("ones1_bf", [1, 128], BF16)
    g.iota_p = din("iota_p", [128, 1]); g.iota_r = din("iota_r", [128, K0]); g.iota8 = din("iota8", [128, 8])
    g.smalls = din("smalls", [128, 32], BF16)
    g.stril8f = din("stril8f", [8, 24]); g.onecol_bf = din("onecol_bf", [128, 1], BF16)
    g.cmask = din("cmask", [8, 1]); g.cvec = din("cvec", [1, 3])

    g.xn0ext = dram("xn0ext", [BPC, T, 1096])
    g.xn0_tmb = dram("xn0_tmb", [BPC, T, D], BF16)
    g.x1p = dram("x1p", [BPC, T, D])
    g.stage16 = dram("stage16", [4, 16, T])
    g.idlist = dram("idlist", [32, K0])
    g.vstage2 = dram("vstage2", [32, T])
    g.uv_wTb = dram("uv_wTb", [D, 4224], BF16)
    g.o_wTb = dram("o_wTb", [E1, D], BF16)
    from concourse.replica_groups import maybe_share_collective_output_space
    g.rg = [list(range(NCORES))]
    _sp = maybe_share_collective_output_space("AllGather", g.rg, prefer_local=True)
    g.cc_cnt_in = nc.dram_tensor("cc_cnt_in", [1, 8], F32, kind="Internal").ap()
    g.cc_cnt_out = nc.dram_tensor("cc_cnt_out", [8, 8], F32, kind="Internal", addr_space=_sp).ap()
    g.lists_pub = nc.dram_tensor("lists_pub", [E0 * 8192, 1], U32, kind="Internal").ap()
    g.lists_all = nc.dram_tensor("lists_all", [8 * E0 * 8192, 1], U32, kind="Internal", addr_space=_sp).ap()
    g.xn1_all = nc.dram_tensor("xn1_all", [NCORES * BPC * T, D], BF16, kind="Internal", addr_space=_sp).ap()
    g.eout_pub = nc.dram_tensor("eout_pub", [CAP, D], BF16, kind="Internal").ap()
    g.eout_all = nc.dram_tensor("eout_all", [8 * CAP, D], BF16, kind="Internal", addr_space=_sp).ap()
    g.xn1_f32 = dram("xn1_f32", [BPC, T, D])
    g.xn1_bf = dram("xn1_bf", [BPC * T, D], BF16)
    g.xnattn_bf = dram("xnattn_bf", [BPC, T, D], BF16)
    g.x2p = dram("x2p", [BPC, T, D])
    g.out = dram("out", [BPC, T, D], out=True)

    g.LD = Ev(st, nc, "LD")
    g.LD2 = Ev(st, nc, "LD2")
    g.GD = Ev(st, nc, "GD")
    g.SC = Ev(st, nc, "SC")
    g.PE = Ev(st, nc, "PE")
    g.DV = Ev(st, nc, "DV")
    g.AC = Ev(st, nc, "AC")
    g.PL = Ev(st, nc, "PL")
    g.ST = Ev(st, nc, "ST")
    g.CC0 = Ev(st, nc, "CC0")
    g.CC1 = Ev(st, nc, "CC1")
    g.CC2 = Ev(st, nc, "CC2")
    g.CC3 = Ev(st, nc, "CC3")

    def barrier():
        for ev in (g.LD, g.LD2, g.GD, g.SC, g.ST):
            ev.wait(nc.sync)
        nc.all_engine_barrier()

    g.barrier = barrier
    g.SY = Q(nc.sync, g.LD, auto=False)
    g.SD = Q(nc.scalar, g.LD2, auto=False)
    g.GQ = Q(nc.gpsimd, g.GD, auto=False)
    g.V = Q(nc.vector, g.DV)
    g.A = Q(nc.scalar, g.AC)
    g.P = Q(nc.gpsimd, g.PL)
    g.TE = Q(nc.tensor, g.PE, auto=False)

    sb = lambda name, shape, dt=F32: st.enter_context(nc.sbuf_tensor(name, shape, dt))
    g.sb = sb
    g.c_ident_bf = sb("c_ident_bf", [128, 128], BF16)
    g.c_ident_f = sb("c_ident_f", [128, 128])
    g.c_stril = sb("c_stril", [128, 128], BF16)
    g.c_ones1f = sb("c_ones1f", [1, 128])
    g.c_ones1b = sb("c_ones1b", [1, 128], BF16)
    g.c_iotap = sb("c_iotap", [128, 1])
    g.c_iotar = sb("c_iotar", [128, K0])
    g.c_iota8 = sb("c_iota8", [128, 8])
    g.c_bgate = sb("c_bgate", [128, 32])
    g.c_eig = sb("c_eig", [16, 1])
    g.c_uvb = sb("c_uvb", [128, 33])
    g.c_gam = sb("c_gam", [128, 4])

    g.c_cmask = sb("c_cmask", [8, 1])
    g.c_cvec = sb("c_cvec", [1, 3])
    g.c_eps = sb("c_eps", [128, 1])
    g.c_smalls = sb("c_smalls", [128, 32], BF16)
    g.c_wr0 = sb("c_wr0", [128, 64], BF16)
    g.pmt = sb("pmt", [128, 16 * 32])
    g.c_wr1 = sb("c_wr1", [128, 64], BF16)
    g.c_s8 = sb("c_s8", [8, 24])
    g.c_onec = sb("c_onec", [128, 1], BF16)
    g.route = sb("route", [128, BPC * NT * 3])
    g.oh_all = sb("oh_all", [128, BPC * NT * 8], BF16)
    g.slotf = sb("slotf", [128, BPC * NT])
    g.keptf = sb("keptf", [128, BPC * NT])
    g.ttot = sb("ttot", [1, BPC * NT * 8])

    SY, SD, GQ, V, A, P, TE = g.SY, g.SD, g.GQ, g.V, g.A, g.P, g.TE
    SY.dma_start(out=g.c_ident_bf[:], in_=g.ident_bf)
    SY.dma_start(out=g.c_ident_f[:], in_=g.ident_f)
    SY.dma_start(out=g.c_stril[:], in_=g.stril_bf)
    SY.dma_start(out=g.c_ones1f[:], in_=g.ones1_f)
    SY.dma_start(out=g.c_ones1b[:], in_=g.ones1_bf)
    SY.dma_start(out=g.c_iotap[:], in_=g.iota_p)
    SY.dma_start(out=g.c_iotar[:], in_=g.iota_r)
    SY.dma_start(out=g.c_iota8[:], in_=g.iota8)
    SY.dma_start(out=g.c_bgate[:], in_=g.b_gate.broadcast_to([128, 32]))
    SY.dma_start(out=g.c_uvb[:], in_=g.uv_b.rearrange("o (c p) -> (o p) c", p=128), allow_slow_non_contiguous=True)

    SY.dma_start(out=g.c_cmask[:], in_=g.cmask)
    SY.dma_start(out=g.c_cvec[:], in_=g.cvec)
    SY.dma_start(out=g.c_smalls[:], in_=g.smalls)
    for j in range(8):
        GQ.dma_start(out=g.c_wr0[:, j * 8:(j + 1) * 8], in_=g.Wr0T[j * 128:(j + 1) * 128, :])
        GQ.dma_start(out=g.c_wr1[:, j * 8:(j + 1) * 8], in_=g.Wr1[j * 128:(j + 1) * 128, :])
    SY.dma_start(out=g.c_s8[:], in_=g.stril8f)
    SY.dma_start(out=g.c_onec[:], in_=g.onecol_bf)
    SY.dma_start(out=g.c_gam[:, 0:2], in_=g.gam.rearrange("c p -> p c"), allow_slow_non_contiguous=True)
    SY.dma_start(out=g.c_gam[:, 2:4], in_=g.bet.rearrange("c p -> p c"), allow_slow_non_contiguous=True)
    V.memset(g.c_eps[:], LN_EPS)
    SY.dma_start(out=g.c_eig[:], in_=g.eig_raw)
    A.after(g.LD).activation(out=g.c_eig[:], in_=g.c_eig[:], func=AF.Tanh)
    for b in range(BPC):
        GQ.dma_start(out=g.xn0ext[b, :, D:D + 64], in_=g.postable)
    with nc.sbuf_tensor("precast", [128, 4224], BF16) as pcast:
        for i in range(8):
            GQ.after(g.ST).dma_start(out=pcast[:], in_=g.uv_wT[i * 128:(i + 1) * 128, :])
            SY.to(g.ST).after(g.GD).dma_start(out=g.uv_wTb[i * 128:(i + 1) * 128, :], in_=pcast[:])
        for i in range(16):
            GQ.after(g.ST).dma_start(out=pcast[:, 0:D], in_=g.o_wT[i * 128:(i + 1) * 128, :])
            SY.to(g.ST).after(g.GD).dma_start(out=g.o_wTb[i * 128:(i + 1) * 128, :], in_=pcast[:, 0:D])

    barrier()

    def _scoped(name, fn, *a):
        with nc.named_scope(name):
            fn(*a)

    _scoped("A", phase_a, g)
    if _LAST_PHASE == "A":
        finish(g)
        return nc
    for b in range(BPC):
        _scoped(f"B{b}", phase_b, g, b)
        _scoped(f"C1{b}", phase_c1, g, b)
    if _LAST_PHASE == "B":
        finish(g)
        return nc
    _scoped("C2", phase_c2, g)
    _scoped("C3", phase_c3, g)
    if _LAST_PHASE == "C":
        finish(g)
        return nc
    _scoped("D", phase_d, g)
    if _LAST_PHASE == "D":
        finish(g)
        return nc
    for b in range(BPC):
        _scoped(f"E{b}", phase_e, g, b)
    _scoped("E2", phase_e2, g)
    if _LAST_PHASE == "F":
        for b in range(BPC):
            _scoped(f"F{b}", phase_f, g, b)
        finish(g)
        return nc
    for b in range(BPC):
        _scoped(f"F{b}", phase_f, g, b)
    _scoped("G", phase_g, g)
    if _LAST_PHASE == "G":
        finish(g)
        return nc
    _scoped("H", phase_h, g)
    finish(g)
    return nc


def phase_a(g):
    nc = g.nc
    SY, V, A, P = g.SY, g.V, g.A, g.P
    from contextlib import ExitStack
    with ExitStack() as ph:
        xt = [ph.enter_context(nc.sbuf_tensor(f"pa_x{i}", [128, D], F32)) for i in range(2)]
        xn = [ph.enter_context(nc.sbuf_tensor(f"pa_n{i}", [128, D], F32)) for i in range(2)]
        xnb = [ph.enter_context(nc.sbuf_tensor(f"pa_nb{i}", [128, D], BF16)) for i in range(2)]
        sq = ph.enter_context(nc.sbuf_tensor("pa_sq", [128, D], F32))
        col = ph.enter_context(nc.sbuf_tensor("pa_col", [128, 8], F32))

        ntile = BPC * NT
        evX = [g.ev(ph, f"pa_evX{i}") for i in range(2)]
        evS = [g.ev(ph, f"pa_evS{i}") for i in range(2)]
        hist = {}
        pl_h = {}
        for i in range(ntile):
            b, tt = divmod(i, NT)
            p = i % 2
            if i >= 2:
                SY.after((g.DV, hist[i - 2]))
            SY.to(evX[p]).dma_start(out=xt[p][:], in_=g.x4[b, tt * 128:(tt + 1) * 128, :])
            g.V.after(evX[p])
            if i >= 2:
                g.V.after(evS[p])
            ln_tile(g, xt[p], xn[p], sq, col,
                    adeps=(evX[p], (g.DV, getattr(g, "_ln_sqfree", 0))))
            hist[i] = g.DV.n
            if i >= 2:
                P.after((g.PL, pl_h[i - 2]))
            P.after((g.DV, hist[i])).tensor_copy(out=xnb[p][:], in_=xn[p][:])
            pl_h[i] = g.PL.n
            SY.to(evS[p]).after((g.DV, hist[i])).dma_start(
                out=g.xn0ext[b, tt * 128:(tt + 1) * 128, 0:D], in_=xn[p][:])
            SY.to(evS[p]).after((g.PL, pl_h[i])).dma_start(
                out=g.xn0_tmb[b, tt * 128:(tt + 1) * 128, :], in_=xnb[p][:])
        evS[0].wait(nc.sync)
        evS[1].wait(nc.sync)
    g.barrier()


def ln_tile(g, src, dst, sq, col, gamma=None, beta=None, adeps=()):
    V, A = g.V, g.A
    m = col[:, 0:1]; q = col[:, 1:2]; msq = col[:, 2:3]
    sd = col[:, 3:4]; rs = col[:, 4:5]; var = col[:, 5:6]
    if adeps:
        A.after(*adeps).activation(out=sq[:], in_=src[:], func=AF.Square)
    V.reduce_sum(m, src[:], axis=AX.X)
    if not adeps:
        A.after(g.DV).activation(out=sq[:], in_=src[:], func=AF.Square)
    V.after((g.AC, g.AC.n)).reduce_sum(q, sq[:], axis=AX.X)
    g._ln_sqfree = g.DV.n
    V.tensor_scalar(out=m, in0=m, scalar1=1.0 / D, scalar2=None, op0=ALU.mult)
    V.tensor_tensor(out=msq, in0=m, in1=m, op=ALU.mult)
    V.tensor_scalar(out=var, in0=q, scalar1=1.0 / D, scalar2=msq, op0=ALU.mult, op1=ALU.subtract)
    A.after((g.DV, g.DV.n)).activation(out=sd, in_=var, func=AF.Sqrt, bias=g.c_eps[:, 0:1])
    V.after((g.AC, g.AC.n)).reciprocal(rs, sd)
    V.tensor_scalar(out=dst[:], in0=src[:], scalar1=m, scalar2=rs,
                    op0=ALU.subtract, op1=ALU.mult)
    if gamma is not None:
        V.tensor_tensor(out=dst[:], in0=dst[:], in1=gamma[:], op=ALU.mult)
        V.tensor_tensor(out=dst[:], in0=dst[:], in1=beta[:], op=ALU.add)


def phase_b(g, b):
    nc = g.nc
    SY, SD, GQ, V, A, P, TE = g.SY, g.SD, g.GQ, g.V, g.A, g.P, g.TE
    from contextlib import ExitStack
    with ExitStack() as ph:
        en = ph.enter_context
        xpnfm = en(nc.sbuf_tensor(f"pb{b}_xpnfm", [128, 8 * T], F32))
        warena = [en(nc.sbuf_tensor(f"pb{b}_w{j}", [128, D], BF16)) for j in range(8)]
        hallb = en(nc.sbuf_tensor(f"pb{b}_hall", [128, 8 * T], BF16))
        s12 = ExitStack()
        al16 = s12.enter_context(nc.sbuf_tensor(f"pb{b}_al16", [16, T], F32))
        bet16 = s12.enter_context(nc.sbuf_tensor(f"pb{b}_bet16", [16, T], F32))

        def xpn_j(j, sl=slice(None)):
            return xpnfm[:, j * T:(j + 1) * T][:, sl]

        def hall_j(j, sl=slice(None)):
            return hallb[:, j * T:(j + 1) * T][:, sl]

        with ExitStack() as s1:
            en1 = s1.enter_context
            xn0fm = [en1(nc.sbuf_tensor(f"pb{b}_xn0fm{j}", [128, T], BF16)) for j in range(8)]
            wgate = en1(nc.sbuf_tensor(f"pb{b}_wg", [128, 8 * 32], F32))
            lngb = en1(nc.sbuf_tensor(f"pb{b}_lngb", [128, D], F32))
            lnbb = en1(nc.sbuf_tensor(f"pb{b}_lnbb", [128, D], F32))
            ps_xp = [en1(nc.psum_tensor(f"pb{b}_psxp{i}", [128, D], F32)) for i in range(2)]
            ps_tr = en1(nc.psum_tensor(f"pb{b}_pstr0", [128, 512], F32))
            ps_g = [en1(nc.psum_tensor(f"pb{b}_psg{i}", [128, 32], F32)) for i in range(2)]
            ps_gt = en1(nc.psum_tensor(f"pb{b}_psgt", [16, 256], F32))
            xpn_sb = [en1(nc.sbuf_tensor(f"pb{b}_xpn{i}", [128, D], F32)) for i in range(2)]
            sq = en1(nc.sbuf_tensor(f"pb{b}_sq", [128, D], F32))
            col = en1(nc.sbuf_tensor(f"pb{b}_col", [128, 8], F32))
            gsig = [en1(nc.sbuf_tensor(f"pb{b}_gsig{i}", [128, 32], F32)) for i in range(2)]
            SY.dma_start(out=lngb[:], in_=g.ln_g.broadcast_to([128, D]))
            SY.dma_start(out=lnbb[:], in_=g.ln_b.broadcast_to([128, D]))
            for j in range(8):
                SY.dma_start(out=xn0fm[j][:], in_=g.xn0_tmb[b, :, j * 128:(j + 1) * 128], transpose=True)
                GQ.dma_start(out=warena[j][:], in_=g.W_inT[j * 128:(j + 1) * 128, :])
                SD.dma_start(out=wgate[:, j * 32:(j + 1) * 32], in_=g.W_gateT[j * 128:(j + 1) * 128, :])
            ld0 = g.LD.n
            gd0 = g.GD.n
            ld20 = g.LD2.n

            dv_ln = {}
            dv_cp = {}
            dv_gs = {}
            for tt in range(NT):
                p = tt % 2
                ts_ = slice(tt * 128, (tt + 1) * 128)
                if tt == 0:
                    TE.after((g.LD, ld0), (g.GD, gd0), (g.LD2, ld20))
                if tt >= 2:
                    TE.after((g.DV, dv_ln[tt - 2]))
                for fh in range(2):
                    fs = slice(fh * 512, (fh + 1) * 512)
                    for j in range(8):
                        TE.matmul(out=ps_xp[p][:, fs], lhsT=xn0fm[j][:, ts_],
                                  rhs=warena[j][:, fs], start=(j == 0), stop=(j == 7))
                pe_xp = g.PE.n
                V.after((g.PE, pe_xp))
                ln_tile(g, ps_xp[p], xpn_sb[p], sq, col, gamma=lngb[:], beta=lnbb[:])
                dv_ln[tt] = g.DV.n
                TE.after((g.DV, dv_ln[tt]))
                for half in range(2):
                    if (tt, half) != (0, 0):
                        TE.after((g.DV, g.DV.n))
                    for jj in range(4):
                        j = half * 4 + jj
                        TE.transpose(out=ps_tr[:, jj * 128:(jj + 1) * 128],
                                     in_=xpn_sb[p][:, j * 128:(j + 1) * 128],
                                     identity=g.c_ident_f[:])
                    pe_tr = g.PE.n
                    V.after((g.PE, pe_tr)).tensor_copy(
                        out=xpnfm.rearrange("p (j t) -> p j t", j=8)[:, half * 4:half * 4 + 4, ts_],
                        in_=ps_tr[:].rearrange("p (j t) -> p j t", j=4))
                dv_cp[tt] = g.DV.n
                TE.after((g.DV, dv_cp[tt]))
                if tt >= 2:
                    TE.after((g.DV, dv_gs[tt - 2]))
                for j in range(8):
                    TE.matmul(out=ps_g[p][:], lhsT=xpn_j(j, ts_),
                              rhs=wgate[:, j * 32:(j + 1) * 32], start=(j == 0), stop=(j == 7))
                pe_g = g.PE.n
                V.after((g.PE, pe_g)).tensor_tensor(out=gsig[p][:], in0=ps_g[p][:],
                                                    in1=g.c_bgate[:, 0:32], op=ALU.add)
                dv_gg = g.DV.n
                A.after((g.DV, dv_gg)).activation(out=gsig[p][:], in_=gsig[p][:], func=AF.Sigmoid)
                ac_g = g.AC.n
                TE.after((g.AC, ac_g))
                if tt >= 1:
                    TE.after((g.DV, dv_gs[tt - 1]))
                TE.transpose(out=ps_gt[:, 0:128], in_=gsig[p][:, 0:16], identity=g.c_ident_f[:])
                TE.transpose(out=ps_gt[:, 128:256], in_=gsig[p][:, 16:32], identity=g.c_ident_f[:])
                pe_gt = g.PE.n
                V.after((g.PE, pe_gt)).tensor_copy(out=al16[:, ts_], in_=ps_gt[:, 0:128])
                V.tensor_copy(out=bet16[:, ts_], in_=ps_gt[:, 128:256])
                dv_gs[tt] = g.DV.n

        with ExitStack() as s2:
            en2 = s2.enter_context
            a16 = en2(nc.sbuf_tensor(f"pb{b}_a16", [16, T], F32))[:]
            atil = en2(nc.sbuf_tensor(f"pb{b}_atil", [16, T], F32))[:]
            btil = en2(nc.sbuf_tensor(f"pb{b}_btil", [16, T], F32))[:]
            cum = en2(nc.sbuf_tensor(f"pb{b}_cum", [16, T], F32))[:]
            z16 = en2(nc.sbuf_tensor(f"pb{b}_z16", [16, T], F32))[:]
            wp = en2(nc.sbuf_tensor(f"pb{b}_wp", [16, T], F32))[:]
            V.after(g.AC).tensor_scalar(out=a16, in0=al16[:], scalar1=g.c_eig[:], scalar2=None, op0=ALU.mult)
            V.tensor_copy(out=atil, in_=a16)
            V.memset(atil[:, 0::CS], 0.0)
            V.memset(btil, 0.0)
            V.tensor_copy(out=btil[:, 0::CS], in_=a16[:, 0::CS])
            V.tensor_tensor_scan(out=cum, data0=atil, data1=btil, initial=0.0,
                                 op0=ALU.mult, op1=ALU.add)
            A.after(g.DV).activation(out=z16[:, 1:], in_=cum[:, 0:T - 1], func=AF.Abs)
            V.after(g.AC).tensor_scalar(out=z16[:, 1:], in0=z16[:, 1:], scalar1=1e-8, scalar2=None,
                                        op0=ALU.is_gt)
            V.memset(z16[:, 0::CS], 1.0)
            V.tensor_tensor(out=wp, in0=a16, in1=z16, op=ALU.mult)
            V.tensor_tensor(out=wp, in0=wp, in1=bet16[:], op=ALU.mult)
            SY.to(g.ST).after(g.DV).dma_start(out=g.stage16[0], in_=atil)
            SY.to(g.ST).dma_start(out=g.stage16[1], in_=wp)
            SY.to(g.ST).dma_start(out=g.stage16[2], in_=bet16[:])
            SY.to(g.ST).dma_start(out=g.stage16[3], in_=cum)
            st_b2 = g.ST.n

        s12.close()
        with ExitStack() as s3:
            en3 = s3.enter_context
            exps = [[en3(nc.sbuf_tensor(f"pb{b}_e{i}{q}", [128, T], F32)) for q in range(4)]
                    for i in range(2)]
            s_p = en3(nc.sbuf_tensor(f"pb{b}_p", [128, T], F32))
            s_h = en3(nc.sbuf_tensor(f"pb{b}_h", [128, T], F32))
            hrun = en3(nc.sbuf_tensor(f"pb{b}_hrun", [128, T // CS], F32))
            dv_done = {}
            pl_done = {}
            evE = [g.ev(s3, f"pb{b}_evE{i}") for i in range(2)]
            SY.after((g.ST, st_b2), (g.DV, g.DV.n), (g.AC, g.AC.n))
            for c in range(2):
                for qi in range(4):
                    SY.to(evE[c]).dma_start(out=exps[c][qi][:], in_=g.stage16[qi, 2 * c:2 * c + 2, :]
                                            .unsqueeze(1).broadcast_to([2, 64, T]))
            ac_cp = {}
            for c in range(8):
                pp = c % 2
                ae, we, be, ce = exps[pp]
                V.after(evE[pp])
                if c >= 1:
                    V.after((g.AC, ac_cp[c - 1]))
                V.tensor_tensor(out=s_p[:], in0=we[:], in1=xpn_j(c), op=ALU.mult)
                V.tensor_tensor_scan(out=s_p[:], data0=ae[:], data1=s_p[:], initial=0.0,
                                     op0=ALU.mult, op1=ALU.add)
                P.after(evE[pp], (g.AC, g.AC.n)).tensor_tensor(
                    out=be[:], in0=be[:], in1=xpn_j(c), op=ALU.mult)
                pl_done[c] = g.PL.n
                V.memset(s_h[:, 0:1], 0.0)
                V.tensor_tensor(out=s_h[:, 1:], in0=ae[:, 1:], in1=s_p[:, 0:T - 1], op=ALU.mult)
                V.after((g.PL, pl_done[c])).tensor_tensor(out=s_h[:], in0=s_h[:], in1=be[:], op=ALU.add)
                V.tensor_tensor_scan(out=hrun[:], data0=ce[:, CS - 1::CS], data1=s_h[:, CS - 1::CS],
                                     initial=0.0, op0=ALU.mult, op1=ALU.add)
                nch = T // CS
                ce3 = ce[:, CS:].rearrange("p (c i) -> p c i", i=CS)
                hr3 = hrun[:, 0:nch - 1].unsqueeze(2).broadcast_to([128, nch - 1, CS])
                V.tensor_tensor(out=ae[:, CS:].rearrange("p (c i) -> p c i", i=CS),
                                in0=ce3, in1=hr3, op=ALU.mult)
                V.tensor_tensor(out=s_h[:, CS:], in0=s_h[:, CS:], in1=ae[:, CS:], op=ALU.add)
                dv_done[c] = g.DV.n
                A.after((g.DV, dv_done[c])).copy(out=hall_j(c), in_=s_h[:])
                ac_cp[c] = g.AC.n
                if c + 2 < 8:
                    SY.after((g.DV, dv_done[c]))
                    for qi in range(4):
                        SY.to(evE[pp]).dma_start(out=exps[pp][qi][:],
                                                 in_=g.stage16[qi, 2 * (c + 2):2 * (c + 2) + 2, :]
                                                 .unsqueeze(1).broadcast_to([2, 64, T]))
            ac_hall = g.AC.n

        with ExitStack() as s4:
            en4 = s4.enter_context
            ps_o = [en4(nc.psum_tensor(f"pb{b}_pso{i}", [128, D], F32)) for i in range(2)]
            xt = [en4(nc.sbuf_tensor(f"pb{b}_xt{i}", [128, D], F32)) for i in range(2)]
            x1t = [en4(nc.sbuf_tensor(f"pb{b}_x1t{i}", [128, D], F32)) for i in range(2)]
            GQ.after((g.PE, g.PE.n))
            for j in range(8):
                GQ.dma_start(out=warena[j][:], in_=g.W_outT[j * 128:(j + 1) * 128, :])
            gd_wo = g.GD.n
            histo = {}
            evX = [g.ev(s4, f"pb{b}_evX{i}") for i in range(2)]
            for tt in range(NT):
                p = tt % 2
                ts_ = slice(tt * 128, (tt + 1) * 128)
                if tt == 0:
                    SY.after((g.DV, g.DV.n), (g.AC, ac_hall))
                if tt >= 2:
                    SY.after((g.DV, histo[tt - 2]))
                SY.to(evX[p]).dma_start(out=xt[p][:], in_=g.x4[b, tt * 128:(tt + 1) * 128, :])
                if tt == 0:
                    TE.after((g.AC, ac_hall), (g.GD, gd_wo))
                if tt >= 2:
                    TE.after((g.DV, histo[tt - 2]))
                for fh in range(2):
                    fs = slice(fh * 512, (fh + 1) * 512)
                    for j in range(8):
                        TE.matmul(out=ps_o[p][:, fs], lhsT=hall_j(j, ts_),
                                  rhs=warena[j][:, fs], start=(j == 0), stop=(j == 7))
                pe_now = g.PE.n
                V.after((g.PE, pe_now), evX[p], g.ST).tensor_tensor(
                    out=x1t[p][:], in0=ps_o[p][:], in1=xt[p][:], op=ALU.add)
                histo[tt] = g.DV.n
                SY.to(g.ST).after((g.DV, histo[tt])).dma_start(
                    out=g.x1p[b, tt * 128:(tt + 1) * 128, :], in_=x1t[p][:])
    g.barrier()


def phase_c1(g, b):
    nc = g.nc
    SY, GQ, V, A, TE = g.SY, g.GQ, g.V, g.A, g.TE
    from contextlib import ExitStack
    with ExitStack() as ph:
        en = ph.enter_context
        xn0fm = [en(nc.sbuf_tensor(f"pc1{b}_xn0fm{j}", [128, T], BF16)) for j in range(8)]
        vsig = [en(nc.sbuf_tensor(f"pc1{b}_vs{i}", [128, 8], F32)) for i in range(2)]
        vfm_b = en(nc.sbuf_tensor(f"pc1{b}_vfm", [8, T], F32))
        ps_l = en(nc.psum_tensor(f"pc1{b}_psl", [128, 8], F32))
        ps_t = en(nc.psum_tensor(f"pc1{b}_pst", [8, 128], F32))
        for j in range(8):
            SY.dma_start(out=xn0fm[j][:], in_=g.xn0_tmb[b, :, j * 128:(j + 1) * 128], transpose=True)
        for tt in range(NT):
            p = tt % 2
            ts_ = slice(tt * 128, (tt + 1) * 128)
            if tt == 0:
                TE.after(g.LD, g.GD)
            if tt >= 2:
                TE.after((g.AC, g.AC.n))
                TE.after((g.DV, g.DV.n))
            for j in range(8):
                TE.matmul(out=ps_l[:], lhsT=xn0fm[j][:, ts_], rhs=g.c_wr0[:, j * 8:(j + 1) * 8],
                          start=(j == 0), stop=(j == 7))
            A.after((g.PE, g.PE.n), g.ST).activation(out=vsig[p][:], in_=ps_l[:], func=AF.Sigmoid)
            SY.to(g.ST).after(g.AC).dma_start(out=g.xn0ext[b, ts_, D + 64:D + 72], in_=vsig[p][:])
            TE.after((g.AC, g.AC.n), (g.DV, g.DV.n)).transpose(out=ps_t[:], in_=vsig[p][:], identity=g.c_ident_f[:])
            V.after((g.PE, g.PE.n)).tensor_copy(out=vfm_b[:, ts_], in_=ps_t[:])
        SY.to(g.ST).after(g.DV).dma_start(out=g.vstage2[8 * b:8 * b + 8, :], in_=vfm_b[:])
    g.barrier()


def phase_c2(g):
    nc = g.nc
    SY, V, A, TE = g.SY, g.V, g.A, g.TE
    from contextlib import ExitStack
    with ExitStack() as ph:
        en = ph.enter_context
        vfm = en(nc.sbuf_tensor("pc2_vfm", [32, T], F32))[:]
        junk = en(nc.sbuf_tensor("pc2_junk", [32, T], F32))[:]
        eq = en(nc.sbuf_tensor("pc2_eq", [32, T], F32))[:]
        pp0 = en(nc.sbuf_tensor("pc2_pp0", [32, T], F32))[:]
        pp1 = en(nc.sbuf_tensor("pc2_pp1", [32, T], F32))[:]
        sel = en(nc.sbuf_tensor("pc2_sel", [32, T], F32))[:]
        cfill = en(nc.sbuf_tensor("pc2_cf", [32, T], F32))[:]
        cols = en(nc.sbuf_tensor("pc2_cols", [32, 16], F32))
        ps_tr = en(nc.psum_tensor("pc2_pstr", [128, 32], F32))
        lo = cols[:, 0:1]; hi = cols[:, 1:2]; mid = cols[:, 2:3]
        cnt = cols[:, 3:4]; th = cols[:, 4:5]; m = cols[:, 5:6]; km = cols[:, 6:7]
        pf = cols[:, 7:8]; tv = cols[:, 8:9]

        SY.after(g.ST).dma_start(out=vfm, in_=g.vstage2)
        V.after(g.LD).memset(lo, 0.0)
        V.memset(hi, 1.0)
        for it in range(45):
            V.tensor_tensor(out=mid, in0=lo, in1=hi, op=ALU.add)
            V.tensor_scalar(out=mid, in0=mid, scalar1=0.5, scalar2=None, op0=ALU.mult)
            V.tensor_scalar(out=junk, in0=vfm, scalar1=mid, scalar2=0.0,
                            op0=ALU.is_gt, op1=ALU.add, accum_out=cnt)
            V.tensor_scalar(out=pf, in0=cnt, scalar1=float(K0), scalar2=None, op0=ALU.is_ge)
            V.tensor_tensor(out=tv, in0=pf, in1=mid, op=ALU.mult)
            V.tensor_tensor(out=lo, in0=lo, in1=tv, op=ALU.max)
            V.tensor_tensor(out=tv, in0=mid, in1=pf, op=ALU.add)
            V.tensor_tensor(out=hi, in0=hi, in1=tv, op=ALU.min)
        V.tensor_scalar(out=eq, in0=vfm, scalar1=hi, scalar2=None, op0=ALU.is_le)
        V.tensor_tensor(out=junk, in0=vfm, in1=eq, op=ALU.mult)
        V.tensor_tensor(out=junk, in0=junk, in1=eq, op=ALU.add)
        V.tensor_scalar(out=junk, in0=junk, scalar1=-1.0, scalar2=None, op0=ALU.add)
        V.reduce_max(th, junk, axis=AX.X)
        V.tensor_scalar(out=junk, in0=vfm, scalar1=th, scalar2=0.0,
                        op0=ALU.is_gt, op1=ALU.add, accum_out=m)
        V.tensor_scalar(out=eq, in0=vfm, scalar1=th, scalar2=None, op0=ALU.is_equal)
        _prefix(g, eq, pp0, pp1)
        V.tensor_tensor(out=pp0, in0=pp0, in1=eq, op=ALU.subtract)
        V.tensor_scalar(out=km, in0=m, scalar1=-1.0, scalar2=float(K0), op0=ALU.mult, op1=ALU.add)
        V.tensor_scalar(out=cfill, in0=pp0, scalar1=km, scalar2=None, op0=ALU.is_lt)
        V.tensor_tensor(out=cfill, in0=cfill, in1=eq, op=ALU.mult)
        V.tensor_scalar(out=sel, in0=vfm, scalar1=th, scalar2=None, op0=ALU.is_gt)
        V.tensor_tensor(out=sel, in0=sel, in1=cfill, op=ALU.add)
        _prefix(g, sel, pp0, pp1)
        V.tensor_tensor(out=pp0, in0=pp0, in1=sel, op=ALU.subtract)
        V.tensor_tensor(out=pp0, in0=pp0, in1=sel, op=ALU.mult)
        V.tensor_scalar(out=junk, in0=sel, scalar1=-1.0e6, scalar2=1.0e6, op0=ALU.mult, op1=ALU.add)
        V.tensor_tensor(out=pp0, in0=pp0, in1=junk, op=ALU.add)
        for ch in range(NT):
            TE.after((g.DV, g.DV.n)).transpose(out=ps_tr[:], in_=pp0[:, ch * 128:(ch + 1) * 128],
                                               identity=g.c_ident_f[0:32, 0:32])
            V.after((g.PE, g.PE.n)).tensor_copy(out=g.pmt[:, ch * 32:(ch + 1) * 32], in_=ps_tr[:])
    g.barrier()


def _prefix(g, srcv, out0, out1):
    V = g.V
    V.tensor_copy(out=out0, in_=srcv)
    cur, oth = out0, out1
    s = 1
    while s < T:
        V.tensor_copy(out=oth[:, 0:s], in_=cur[:, 0:s])
        V.tensor_tensor(out=oth[:, s:], in0=cur[:, s:], in1=cur[:, 0:T - s], op=ALU.add)
        cur, oth = oth, cur
        s *= 2
    if cur is not out0:
        V.tensor_copy(out=out0, in_=cur)


def phase_c3(g):
    nc = g.nc
    SY, SD, GQ, V, A, P, TE = g.SY, g.SD, g.GQ, g.V, g.A, g.P, g.TE
    from contextlib import ExitStack
    with ExitStack() as ph:
        en = ph.enter_context
        L = {}
        L["wq"] = [en(nc.sbuf_tensor(f"pc3_wq{j}", [128, 3 * HP], BF16)) for j in range(8)]
        L["woe"] = en(nc.sbuf_tensor("pc3_woe", [128, D], BF16))
        L["Gt16"] = en(nc.sbuf_tensor("pc3_G16", [128, NT * K0], BF16))
        L["idf"] = en(nc.sbuf_tensor("pc3_idf", [128, 4], F32))
        L["idu"] = en(nc.sbuf_tensor("pc3_idu", [128, 2], U32))
        L["vall"] = en(nc.sbuf_tensor("pc3_vall", [128, 2], F32))
        L["xg"] = [en(nc.sbuf_tensor(f"pc3_xg{i}", [128, 1096], F32)) for i in range(2)]
        L["xgb"] = en(nc.sbuf_tensor("pc3_xgb", [128, 1088], BF16))
        L["xgfm"] = [en(nc.sbuf_tensor(f"pc3_xgfm{j}", [128, K0], BF16)) for j in range(8)]
        L["ptab"] = en(nc.sbuf_tensor("pc3_ptab", [64, K0], F32))
        L["sn0"] = en(nc.sbuf_tensor("pc3_sn0", [32, K0], F32))
        L["cs0"] = en(nc.sbuf_tensor("pc3_cs0", [32, K0], F32))
        L["qf"] = en(nc.sbuf_tensor("pc3_qf", [128, K0], F32))
        L["kf"] = en(nc.sbuf_tensor("pc3_kf", [128, K0], F32))
        L["vf"] = en(nc.sbuf_tensor("pc3_vf", [128, K0], F32))
        L["r1"] = en(nc.sbuf_tensor("pc3_r1", [32, K0], F32))
        L["r2"] = en(nc.sbuf_tensor("pc3_r2", [32, K0], F32))
        L["r3"] = en(nc.sbuf_tensor("pc3_r3", [32, K0], F32))
        L["qfb"] = en(nc.sbuf_tensor("pc3_qfb", [128, K0], BF16))
        L["kfb"] = en(nc.sbuf_tensor("pc3_kfb", [128, K0], BF16))
        L["vfb"] = en(nc.sbuf_tensor("pc3_vfb", [128, K0], BF16))
        L["sct"] = en(nc.sbuf_tensor("pc3_sct", [128, K0], F32))
        L["et"] = en(nc.sbuf_tensor("pc3_et", [128, K0], F32))
        L["mkf"] = en(nc.sbuf_tensor("pc3_mkf", [128, K0], F32))
        L["idbc"] = en(nc.sbuf_tensor("pc3_idbc", [128, K0], F32))
        L["ab"] = en(nc.sbuf_tensor("pc3_ab", [128, K0], BF16))
        L["at"] = [en(nc.sbuf_tensor(f"pc3_at{i}", [128, K0], BF16)) for i in range(2)]
        L["vt"] = [en(nc.sbuf_tensor(f"pc3_vt{i}", [128, 128], BF16)) for i in range(2)]
        L["avb"] = en(nc.sbuf_tensor("pc3_avb", [128, K0], BF16))
        L["xps"] = [en(nc.sbuf_tensor(f"pc3_xps{i}", [128, D], F32)) for i in range(2)]
        L["cols"] = en(nc.sbuf_tensor("pc3_cols", [128, 8], F32))
        L["ps_id0"] = en(nc.psum_tensor("pc3_psid0", [128, 2], F32))
        L["ps_id1"] = en(nc.psum_tensor("pc3_psid1", [128, 2], F32))
        L["ps_trb"] = en(nc.psum_tensor("pc3_pstrb", [128, 128], BF16))
        L["ps_a"] = en(nc.psum_tensor("pc3_psa", [128, K0], F32))
        L["ps_b"] = en(nc.psum_tensor("pc3_psb", [128, K0], F32))
        L["ps_o"] = en(nc.psum_tensor("pc3_pso", [128, D], F32))

        for e in range(E0):
            GQ.after((g.PE, g.PE.n))
            for j in range(8):
                GQ.dma_start(out=L["wq"][j][:], in_=g.Wqkv[e, j * 128:(j + 1) * 128, :])
            GQ.dma_start(out=L["woe"][:], in_=g.Wo0[e])
            for b in range(BPC):
                _c3_pair(g, b, e, L)
    g.barrier()


def _c3_pair(g, b, e, L):
    nc = g.nc
    SY, SD, GQ, V, A, P, TE = g.SY, g.SD, g.GQ, g.V, g.A, g.P, g.TE
    wq = L["wq"]; woe = L["woe"]; Gt16 = L["Gt16"]; idf = L["idf"]; idu = L["idu"]
    vall = L["vall"]; xg = L["xg"]; xgb = L["xgb"]; xgfm = L["xgfm"]; ptab = L["ptab"]
    sn0 = L["sn0"]; cs0 = L["cs0"]; qf = L["qf"]; kf = L["kf"]; vf = L["vf"]
    r1 = L["r1"]; r2 = L["r2"]; r3 = L["r3"]; qfb = L["qfb"]; kfb = L["kfb"]; vfb = L["vfb"]
    sct = L["sct"]; et = L["et"]; mkf = L["mkf"]; idbc = L["idbc"]
    ab = L["ab"]; at = L["at"]; vt = L["vt"]; avb = L["avb"]; xps = L["xps"]; cols = L["cols"]
    ps_id0 = L["ps_id0"]; ps_id1 = L["ps_id1"]; ps_trb = L["ps_trb"]
    ps_a = L["ps_a"]; ps_b = L["ps_b"]; ps_o = L["ps_o"]
    row = b * 8 + e
    RT = 1.0 / float(np.sqrt(HP))

    V.after((g.PE, g.PE.n), g.GD, g.SC)
    for ch in range(NT):
        V.tensor_scalar(out=Gt16[:, ch * K0:(ch + 1) * K0], in0=g.c_iotar[:],
                        scalar1=g.pmt[:, ch * 32 + row:ch * 32 + row + 1],
                        scalar2=None, op0=ALU.is_equal)
    TE.after((g.DV, g.DV.n))
    for ch in range(NT):
        TE.matmul(out=ps_id0[:], lhsT=Gt16[:, ch * K0:ch * K0 + 128],
                  rhs=g.c_smalls[:, 2 * ch:2 * ch + 2], start=(ch == 0), stop=(ch == NT - 1))
    for ch in range(NT):
        TE.matmul(out=ps_id1[:], lhsT=Gt16[:, ch * K0 + 128:(ch + 1) * K0],
                  rhs=g.c_smalls[:, 2 * ch:2 * ch + 2], start=(ch == 0), stop=(ch == NT - 1))
    V.after((g.PE, g.PE.n)).tensor_copy(out=idf[:, 0:2], in_=ps_id0[:])
    V.tensor_copy(out=idf[:, 2:4], in_=ps_id1[:])
    V.scalar_tensor_tensor(out=cols[:, 0:1], in0=idf[:, 0:1], scalar=128.0, in1=idf[:, 1:2],
                           op0=ALU.mult, op1=ALU.add)
    V.scalar_tensor_tensor(out=cols[:, 1:2], in0=idf[:, 2:3], scalar=128.0, in1=idf[:, 3:4],
                           op0=ALU.mult, op1=ALU.add)
    V.tensor_copy(out=idu[:, 0:1], in_=cols[:, 0:1])
    V.tensor_copy(out=idu[:, 1:2], in_=cols[:, 1:2])
    SY.to(g.ST).after(g.DV).dma_start(out=g.idlist[row, 0:128], in_=cols[:, 0:1])
    SY.to(g.ST).dma_start(out=g.idlist[row, 128:256], in_=cols[:, 1:2])
    GQ.after(g.DV, g.AC)
    xnext_flat = g.xn0ext.rearrange("b t f -> (b t) f")
    GQ.indirect_dma_start(out=xg[0][:], out_offset=None, in_=xnext_flat,
                          in_offset=IndirectOffsetOnAxis(ap=idu[:, 0:1], axis=0),
                          element_offset=b * T * 1096)
    GQ.indirect_dma_start(out=xg[1][:], out_offset=None, in_=xnext_flat,
                          in_offset=IndirectOffsetOnAxis(ap=idu[:, 1:2], axis=0),
                          element_offset=b * T * 1096)
    SY.after(g.ST).dma_start(out=idbc[:], in_=g.idlist[row:row + 1, :].broadcast_to([128, K0]))
    for c in range(2):
        V.after(g.GD).tensor_copy(out=xgb[:], in_=xg[c][:, 0:1088])
        V.tensor_copy(out=vall[:, c:c + 1], in_=xg[c][:, 1088 + e:1089 + e])
        for j in range(9):
            w = 128 if j < 8 else 64
            TE.after((g.DV, g.DV.n)).transpose(out=ps_trb[0:w, :], in_=xgb[:, j * 128:j * 128 + w],
                                               identity=g.c_ident_bf[:])
            if j < 8:
                V.after((g.PE, g.PE.n)).tensor_copy(out=xgfm[j][:, c * 128:(c + 1) * 128],
                                                    in_=ps_trb[:])
            else:
                V.after((g.PE, g.PE.n)).tensor_copy(out=ptab[:, c * 128:(c + 1) * 128],
                                                    in_=ps_trb[0:64, :])
    outs = (qf, kf, vf)
    for jc in range(3):
        TE.after((g.DV, g.DV.n))
        for j in range(8):
            TE.matmul(out=ps_a[:], lhsT=wq[j][:, jc * 128:(jc + 1) * 128], rhs=xgfm[j][:],
                      start=(j == 0), stop=(j == 7))
        V.after((g.PE, g.PE.n)).tensor_copy(out=outs[jc][:], in_=ps_a[:])
    V.tensor_copy(out=sn0[:], in_=ptab[0:32, :])
    V.tensor_copy(out=cs0[:], in_=ptab[32:64, :])
    for X in (qf, kf):
        V.tensor_copy(out=r2[:], in_=X[32:64, :])
        V.tensor_tensor(out=r1[:], in0=X[0:32, :], in1=cs0[:], op=ALU.mult)
        V.tensor_tensor(out=r3[:], in0=r2[:], in1=sn0[:], op=ALU.mult)
        V.tensor_tensor(out=r1[:], in0=r1[:], in1=r3[:], op=ALU.subtract)
        V.tensor_tensor(out=r3[:], in0=r2[:], in1=cs0[:], op=ALU.mult)
        V.tensor_tensor(out=r2[:], in0=X[0:32, :], in1=sn0[:], op=ALU.mult)
        V.tensor_tensor(out=r3[:], in0=r3[:], in1=r2[:], op=ALU.add)
        V.tensor_copy(out=X[0:32, :], in_=r1[:])
        V.tensor_copy(out=X[32:64, :], in_=r3[:])
    A.after(g.DV).copy(out=qfb[:], in_=qf[:])
    A.copy(out=kfb[:], in_=kf[:])
    A.copy(out=vfb[:], in_=vf[:])
    for mc in range(2):
        TE.after(g.AC, (g.DV, g.DV.n)).transpose(out=ps_trb[:], in_=vfb[:, mc * 128:(mc + 1) * 128],
                                 identity=g.c_ident_bf[:])
        V.after((g.PE, g.PE.n)).tensor_copy(out=vt[mc][:], in_=ps_trb[:])
    for tc in range(2):
        TE.after(g.AC, (g.DV, g.DV.n), g.LD)
        TE.matmul(out=ps_b[:], lhsT=qfb[:, tc * 128:(tc + 1) * 128], rhs=kfb[:],
                  start=True, stop=True)
        V.after((g.PE, g.PE.n)).tensor_scalar(out=sct[:], in0=ps_b[:], scalar1=RT,
                                              scalar2=None, op0=ALU.mult)
        V.tensor_scalar(out=mkf[:], in0=idbc[:], scalar1=cols[:, tc:tc + 1],
                        scalar2=None, op0=ALU.is_le)
        V.tensor_tensor(out=sct[:], in0=sct[:], in1=mkf[:], op=ALU.mult)
        V.tensor_scalar(out=mkf[:], in0=mkf[:], scalar1=1.0e30, scalar2=-1.0e30,
                        op0=ALU.mult, op1=ALU.add)
        V.tensor_tensor(out=sct[:], in0=sct[:], in1=mkf[:], op=ALU.add)
        V.reduce_max(cols[:, 4:5], sct[:], axis=AX.X, negate=True)
        A.after(g.DV).activation(out=et[:], in_=sct[:], func=AF.Exp, bias=cols[:, 4:5])
        V.after(g.AC).reduce_sum(cols[:, 5:6], et[:], axis=AX.X)
        V.reciprocal(cols[:, 6:7], cols[:, 5:6])
        V.tensor_scalar(out=ab[:], in0=et[:], scalar1=cols[:, 6:7], scalar2=None, op0=ALU.mult)
        for mc in range(2):
            TE.after((g.DV, g.DV.n)).transpose(out=ps_trb[:], in_=ab[:, mc * 128:(mc + 1) * 128],
                                               identity=g.c_ident_bf[:])
            V.after((g.PE, g.PE.n)).tensor_copy(out=at[mc][:, tc * 128:(tc + 1) * 128],
                                                in_=ps_trb[:])
    TE.after((g.DV, g.DV.n))
    for mc in range(2):
        TE.matmul(out=ps_a[:], lhsT=vt[mc][:], rhs=at[mc][:], start=(mc == 0), stop=(mc == 1))
    V.after((g.PE, g.PE.n)).tensor_copy(out=avb[:], in_=ps_a[:])
    for tc in range(2):
        TE.after((g.DV, g.DV.n), g.SC)
        for fh in range(2):
            TE.matmul(out=ps_o[:, fh * 512:(fh + 1) * 512], lhsT=avb[:, tc * 128:(tc + 1) * 128],
                      rhs=woe[:, fh * 512:(fh + 1) * 512], start=True, stop=True)
        V.after((g.PE, g.PE.n), g.SC).tensor_scalar(out=xps[tc][:], in0=ps_o[:],
                                              scalar1=vall[:, tc:tc + 1], scalar2=None, op0=ALU.mult)
        GQ.to(g.SC).after(g.DV, g.SC).indirect_dma_start(
            out=g.x1p.rearrange("b t f -> (b t) f"),
            out_offset=IndirectOffsetOnAxis(ap=idu[:, tc:tc + 1], axis=0),
            in_=xps[tc][:], in_offset=None, compute_op=ALU.add, oob_is_err=True,
            element_offset=b * T * D)


def phase_d(g):
    nc = g.nc
    SY, V, A, P = g.SY, g.V, g.A, g.P
    from contextlib import ExitStack
    with ExitStack() as ph:
        en = ph.enter_context
        xt = [en(nc.sbuf_tensor(f"pd_x{i}", [128, D], F32)) for i in range(2)]
        xn = [en(nc.sbuf_tensor(f"pd_n{i}", [128, D], F32)) for i in range(2)]
        xa = [en(nc.sbuf_tensor(f"pd_a{i}", [128, D], F32)) for i in range(2)]
        xnb = [en(nc.sbuf_tensor(f"pd_nb{i}", [128, D], BF16)) for i in range(2)]
        xab = [en(nc.sbuf_tensor(f"pd_ab{i}", [128, D], BF16)) for i in range(2)]
        sq = en(nc.sbuf_tensor("pd_sq", [128, D], F32))
        col = en(nc.sbuf_tensor("pd_col", [128, 8], F32))
        ln1w = en(nc.sbuf_tensor("pd_ln1w", [128, D], F32))
        ln1b = en(nc.sbuf_tensor("pd_ln1b", [128, D], F32))
        SY.dma_start(out=ln1w[:], in_=g.ln1_w.broadcast_to([128, D]))
        SY.dma_start(out=ln1b[:], in_=g.ln1_b.broadcast_to([128, D]))
        evX = [g.ev(ph, f"pd_evX{i}") for i in range(2)]
        evS = [g.ev(ph, f"pd_evS{i}") for i in range(2)]
        hist = {}
        pl_h = {}
        for i in range(BPC * NT):
            b, tt = divmod(i, NT)
            p = i % 2
            if i >= 2:
                SY.after((g.DV, hist[i - 2]))
            SY.to(evX[p]).dma_start(out=xt[p][:], in_=g.x1p[b, tt * 128:(tt + 1) * 128, :])
            g.V.after(evX[p])
            if i >= 2:
                g.V.after(evS[p])
            ln_tile(g, xt[p], xn[p], sq, col,
                    adeps=(evX[p], (g.DV, getattr(g, "_ln_sqfree", 0))))
            ln_tile(g, xn[p], xa[p], sq, col, gamma=ln1w[:], beta=ln1b[:])
            hist[i] = g.DV.n
            if i >= 2:
                P.after((g.PL, pl_h[i - 2]))
            P.after((g.DV, hist[i])).tensor_copy(out=xnb[p][:], in_=xn[p][:])
            P.tensor_copy(out=xab[p][:], in_=xa[p][:])
            pl_h[i] = g.PL.n
            SY.to(evS[p]).after((g.DV, hist[i])).dma_start(
                out=g.xn1_f32[b, tt * 128:(tt + 1) * 128, :], in_=xn[p][:])
            SY.to(evS[p]).after((g.PL, pl_h[i])).dma_start(
                out=g.xn1_bf[(b * NT + tt) * 128:(b * NT + tt + 1) * 128, :], in_=xnb[p][:])
            SY.to(evS[p]).dma_start(out=g.xnattn_bf[b, tt * 128:(tt + 1) * 128, :], in_=xab[p][:])
        evS[0].wait(nc.sync)
        evS[1].wait(nc.sync)
    g.barrier()


def phase_e(g, b):
    nc = g.nc
    SY, GQ, V, A, TE = g.SY, g.GQ, g.V, g.A, g.TE
    from contextlib import ExitStack
    with ExitStack() as ph:
        en = ph.enter_context
        xn1fm = [en(nc.sbuf_tensor(f"pe{b}_fm{j}", [128, T], BF16)) for j in range(8)]
        prob = en(nc.sbuf_tensor(f"pe{b}_prob", [128, 8], F32))
        oh = en(nc.sbuf_tensor(f"pe{b}_oh", [128, 8], BF16))
        tmp8 = en(nc.sbuf_tensor(f"pe{b}_tmp8", [128, 8], F32))
        cols = en(nc.sbuf_tensor(f"pe{b}_cols", [128, 8], F32))
        ps_l = en(nc.psum_tensor(f"pe{b}_psl", [128, 8], F32))
        ps_c = en(nc.psum_tensor(f"pe{b}_psc", [128, 8], F32))
        ps_t = en(nc.psum_tensor(f"pe{b}_pst", [1, 8], F32))
        for j in range(8):
            SY.dma_start(out=xn1fm[j][:], in_=g.xn1_bf[b * T:(b + 1) * T, j * 128:(j + 1) * 128],
                         transpose=True)
        for tt in range(NT):
            gt = b * NT + tt
            ts_ = slice(tt * 128, (tt + 1) * 128)
            if tt == 0:
                TE.after(g.LD, g.GD)
            TE.after((g.DV, g.DV.n), (g.AC, g.AC.n))
            for j in range(8):
                TE.matmul(out=ps_l[:], lhsT=xn1fm[j][:, ts_], rhs=g.c_wr1[:, j * 8:(j + 1) * 8],
                          start=(j == 0), stop=(j == 7))
            pe_mm = g.PE.n
            V.after((g.PE, pe_mm)).reduce_max(cols[:, 0:1], ps_l[:], axis=AX.X, negate=True)
            A.after(g.DV).activation(out=prob[:], in_=ps_l[:], func=AF.Exp, bias=cols[:, 0:1])
            V.after(g.AC).reduce_sum(cols[:, 1:2], prob[:], axis=AX.X)
            V.reciprocal(cols[:, 2:3], cols[:, 1:2])
            V.tensor_scalar(out=prob[:], in0=prob[:], scalar1=cols[:, 2:3], scalar2=None, op0=ALU.mult)
            V.reduce_max(cols[:, 3:4], prob[:], axis=AX.X)
            V.tensor_scalar(out=tmp8[:], in0=prob[:], scalar1=cols[:, 3:4], scalar2=None, op0=ALU.is_equal)
            V.scalar_tensor_tensor(out=tmp8[:], in0=tmp8[:], scalar=-1000.0, in1=g.c_iota8[:],
                                   op0=ALU.mult, op1=ALU.add)
            V.tensor_reduce(cols[:, 4:5], tmp8[:], axis=AX.X, op=ALU.min)
            V.tensor_scalar(out=cols[:, 5:6], in0=cols[:, 4:5], scalar1=1000.0, scalar2=None,
                            op0=ALU.add)
            V.tensor_scalar(out=oh[:], in0=g.c_iota8[:], scalar1=cols[:, 5:6], scalar2=None, op0=ALU.is_equal)
            TE.after((g.DV, g.DV.n))
            TE.matmul(out=ps_c[:], lhsT=g.c_stril[:], rhs=oh[:], start=True, stop=True)
            TE.matmul(out=ps_t[:], lhsT=g.c_onec[:], rhs=oh[:], start=True, stop=True)
            pe_mm2 = g.PE.n
            V.after((g.PE, pe_mm2)).tensor_tensor(out=tmp8[:], in0=ps_c[:], in1=oh[:], op=ALU.mult)
            V.reduce_sum(g.route[:, gt * 3:gt * 3 + 1], tmp8[:], axis=AX.X)
            V.tensor_copy(out=g.route[:, gt * 3 + 1:gt * 3 + 2], in_=cols[:, 5:6])
            V.tensor_copy(out=g.route[:, gt * 3 + 2:gt * 3 + 3], in_=cols[:, 3:4])
            V.tensor_copy(out=g.oh_all[:, gt * 8:(gt + 1) * 8], in_=oh[:])
            V.tensor_copy(out=g.ttot[:, gt * 8:(gt + 1) * 8], in_=ps_t[:])
    g.barrier()


def phase_e2(g):
    nc = g.nc
    SY, GQ, V, A, TE = g.SY, g.GQ, g.V, g.A, g.TE
    NTILE = BPC * NT
    from contextlib import ExitStack
    with ExitStack() as ph:
        en = ph.enter_context
        pp0 = en(nc.sbuf_tensor("pe2_pp0", [1, NTILE * 8], F32))
        pp1 = en(nc.sbuf_tensor("pe2_pp1", [1, NTILE * 8], F32))
        cnts = en(nc.sbuf_tensor("pe2_cnts", [8, 8], F32))
        cmv = en(nc.sbuf_tensor("pe2_cmv", [8, 1], F32))
        offr = en(nc.sbuf_tensor("pe2_offr", [1, 8], F32))
        offb = en(nc.sbuf_tensor("pe2_offb", [128, 8], F32))
        tbb = en(nc.sbuf_tensor("pe2_tbb", [128, 8], F32))
        t8 = en(nc.sbuf_tensor("pe2_t8", [128, 8], F32))
        colz = en(nc.sbuf_tensor("pe2_colz", [128, 8], F32))
        gidu = en(nc.sbuf_tensor("pe2_gidu", [128, 1], U32))
        lidu = en(nc.sbuf_tensor("pe2_lidu", [128, 1], U32))
        cbcol = en(nc.sbuf_tensor("pe2_cbcol", [128, 1], F32))
        mk8 = en(nc.sbuf_tensor("pe2_mk8", [1, 8], F32))
        ps8 = en(nc.psum_tensor("pe2_ps8", [8, 8], F32))
        ps1 = en(nc.psum_tensor("pe2_ps1", [128, 8], F32))

        V.tensor_copy(out=pp0[:], in_=g.ttot[:])
        cur, oth = pp0, pp1
        s = 1
        while s < NTILE:
            V.tensor_copy(out=oth[:, 0:s * 8], in_=cur[:, 0:s * 8])
            V.tensor_tensor(out=oth[:, s * 8:], in0=cur[:, s * 8:],
                            in1=cur[:, 0:(NTILE - s) * 8], op=ALU.add)
            cur, oth = oth, cur
            s *= 2
        if cur is not pp0:
            V.tensor_copy(out=pp0[:], in_=cur[:])
        V.tensor_tensor(out=pp1[:], in0=pp0[:], in1=g.ttot[:], op=ALU.subtract)
        SY.to(g.ST).after(g.DV).dma_start(out=g.cc_cnt_in, in_=pp0[:, (NTILE - 1) * 8:NTILE * 8])
        g.ST.wait(nc.gpsimd)
        g.LD.wait(nc.gpsimd)
        g.CC0.inc(GQ.raw.collective_compute(
            "AllGather", ALU.bypass, replica_groups=g.rg,
            ins=[g.cc_cnt_in.opt()], outs=[g.cc_cnt_out.opt()]))
        g.CC1.inc(GQ.raw.collective_compute(
            "AllGather", ALU.bypass, replica_groups=g.rg,
            ins=[g.xn1_bf.opt()], outs=[g.xn1_all.opt()]))
        TE.after(g.LD).matmul(out=ps1[:, 0:1], lhsT=g.c_ones1f[:], rhs=g.c_cvec[0:1, 0:1],
                              start=True, stop=True)
        V.after((g.PE, g.PE.n)).tensor_copy(out=cbcol[:], in_=ps1[:, 0:1])
        for gt in range(NTILE):
            tbslice = pp1[:, gt * 8:(gt + 1) * 8]
            TE.after((g.DV, g.DV.n)).matmul(out=ps1[:], lhsT=g.c_ones1f[:], rhs=tbslice,
                                            start=True, stop=True)
            V.after((g.PE, g.PE.n), g.SC).tensor_copy(out=tbb[:], in_=ps1[:])
            ohsl = g.oh_all[:, gt * 8:(gt + 1) * 8]
            V.tensor_tensor(out=t8[:], in0=tbb[:], in1=ohsl, op=ALU.mult)
            V.reduce_sum(colz[:, 0:1], t8[:], axis=AX.X)
            V.tensor_tensor(out=colz[:, 1:2], in0=colz[:, 0:1],
                            in1=g.route[:, gt * 3:gt * 3 + 1], op=ALU.add)
            V.scalar_tensor_tensor(out=colz[:, 2:3], in0=g.route[:, gt * 3 + 1:gt * 3 + 2],
                                   scalar=8192.0, in1=colz[:, 1:2], op0=ALU.mult, op1=ALU.add)
            V.tensor_copy(out=lidu[:], in_=colz[:, 2:3])
            V.tensor_scalar(out=colz[:, 3:4], in0=g.c_iotap[:], scalar1=float(gt * 128),
                            scalar2=None, op0=ALU.add)
            V.tensor_tensor(out=colz[:, 3:4], in0=colz[:, 3:4], in1=cbcol[:], op=ALU.add)
            V.tensor_copy(out=gidu[:], in_=colz[:, 3:4])
            GQ.to(g.SC).after(g.DV, g.SC).indirect_dma_start(
                out=g.lists_pub, out_offset=IndirectOffsetOnAxis(ap=lidu[:], axis=0),
                in_=gidu[:], in_offset=None)
            V.tensor_copy(out=g.route[:, gt * 3:gt * 3 + 1], in_=colz[:, 1:2])
        g.SC.wait(nc.gpsimd)
        g.CC2.inc(GQ.raw.collective_compute(
            "AllGather", ALU.bypass, replica_groups=g.rg,
            ins=[g.lists_pub.opt()], outs=[g.lists_all.opt()]))
        SY.after(g.CC0)
        SY.dma_start(out=cnts[:], in_=g.cc_cnt_out)
        TE.after(g.LD).matmul(out=ps8[0:1, :], lhsT=g.c_cmask[:], rhs=cnts[:], start=True, stop=True)
        V.after((g.PE, g.PE.n)).tensor_copy(out=offr[:], in_=ps8[0:1, :])
        TE.after((g.DV, g.DV.n)).matmul(out=ps1[:], lhsT=g.c_ones1f[:], rhs=offr[:],
                                        start=True, stop=True)
        V.after((g.PE, g.PE.n)).tensor_copy(out=offb[:], in_=ps1[:])
        for gt in range(NTILE):
            ohsl = g.oh_all[:, gt * 8:(gt + 1) * 8]
            V.tensor_tensor(out=t8[:], in0=offb[:], in1=ohsl, op=ALU.mult)
            V.reduce_sum(colz[:, 0:1], t8[:], axis=AX.X)
            V.tensor_tensor(out=colz[:, 1:2], in0=colz[:, 0:1],
                            in1=g.route[:, gt * 3:gt * 3 + 1], op=ALU.add)
            V.tensor_scalar(out=g.keptf[:, gt:gt + 1], in0=colz[:, 1:2],
                            scalar1=float(CAP), scalar2=None, op0=ALU.is_lt)
            V.scalar_tensor_tensor(out=colz[:, 2:3], in0=g.route[:, gt * 3 + 1:gt * 3 + 2],
                                   scalar=float(CAP), in1=colz[:, 1:2], op0=ALU.mult, op1=ALU.add)
            V.tensor_tensor(out=colz[:, 2:3], in0=colz[:, 2:3], in1=g.keptf[:, gt:gt + 1], op=ALU.mult)
            V.tensor_scalar(out=colz[:, 3:4], in0=g.keptf[:, gt:gt + 1], scalar1=-1.0e6,
                            scalar2=1.0e6, op0=ALU.mult, op1=ALU.add)
            V.tensor_tensor(out=g.slotf[:, gt:gt + 1], in0=colz[:, 2:3], in1=colz[:, 3:4], op=ALU.add)
    g.barrier()


def phase_f(g, b):
    nc = g.nc
    SY, SD, GQ, V, A, P, TE = g.SY, g.SD, g.GQ, g.V, g.A, g.P, g.TE
    RT = 1.0 / float(np.sqrt(S1))
    NQ = 4
    QW = T // NQ
    from contextlib import ExitStack
    with ExitStack() as ph:
        en = ph.enter_context
        xafm = [en(nc.sbuf_tensor(f"pf{b}_xa{j}", [128, T], BF16)) for j in range(8)]
        vtm = [en(nc.sbuf_tensor(f"pf{b}_v{t}", [128, E1], BF16)) for t in range(NT)]
        kT = [en(nc.sbuf_tensor(f"pf{b}_kT{j}", [128, QW], BF16)) for j in range(NT)]
        Pq = [en(nc.sbuf_tensor(f"pf{b}_P{ec}", [128, QW], BF16)) for ec in range(16)]
        qb = en(nc.sbuf_tensor(f"pf{b}_qb", [128, T], BF16))
        kb = en(nc.sbuf_tensor(f"pf{b}_kb", [128, T], BF16))
        scr = en(nc.sbuf_tensor(f"pf{b}_scr", [128, QW], F32))
        scr2 = en(nc.sbuf_tensor(f"pf{b}_scr2", [128, QW], F32))
        uvbv = en(nc.sbuf_tensor(f"pf{b}_uvbv", [128, 2048], F32))
        ps1 = [en(nc.psum_tensor(f"pf{b}_ps1{i}", [128, 512], F32)) for i in range(2)]
        ps3 = [en(nc.psum_tensor(f"pf{b}_ps3{i}", [128, 512], F32)) for i in range(2)]
        ps5 = [en(nc.psum_tensor(f"pf{b}_ps5{i}", [128, 512], F32)) for i in range(2)]

        evXA = g.ev(ph, f"pf{b}_evXA")
        evUV = g.ev(ph, f"pf{b}_evUV")
        evSC = g.ev(ph, f"pf{b}_evSC")
        evWM = g.ev(ph, f"pf{b}_evWM")
        evWS = [g.ev(ph, f"pf{b}_evWS{i}") for i in range(2)]
        evB = [g.ev(ph, f"pf{b}_evB{i}") for i in range(2)]
        evU2 = [g.ev(ph, f"pf{b}_evU2{i}") for i in range(2)]
        evWF = g.ev(ph, f"pf{b}_evWF")
        evO = [g.ev(ph, f"pf{b}_evO{i}") for i in range(2)]
        evOB = g.ev(ph, f"pf{b}_evOB")
        for j in range(8):
            SY.to(evXA).dma_start(out=xafm[j][:], in_=g.xnattn_bf[b, :, j * 128:(j + 1) * 128], transpose=True)
        SY.to(evUV).dma_start(out=uvbv[:], in_=g.uv_b[0:1, E1:2 * E1].broadcast_to([128, 2048]))

        with ExitStack() as sb_:
            en2 = sb_.enter_context
            basef = en2(nc.sbuf_tensor(f"pf{b}_base", [128, T], F32))
            qkq = en2(nc.sbuf_tensor(f"pf{b}_qkq", [128, QW], F32))
            rr1 = en2(nc.sbuf_tensor(f"pf{b}_rr1", [64, QW], F32))
            rr2 = en2(nc.sbuf_tensor(f"pf{b}_rr2", [64, QW], F32))
            rr3 = en2(nc.sbuf_tensor(f"pf{b}_rr3", [64, QW], F32))
            sn = en2(nc.sbuf_tensor(f"pf{b}_sn", [64, T], F32))
            cs = en2(nc.sbuf_tensor(f"pf{b}_cs", [64, T], F32))
            wsm = [en2(nc.sbuf_tensor(f"pf{b}_wsm{j}", [128, 128], BF16)) for j in range(8)]
            SY.to(evSC).dma_start(out=sn[:], in_=g.sinT1)
            SY.to(evSC).dma_start(out=cs[:], in_=g.cosT1)
            for j in range(8):
                GQ.to(evWM).dma_start(out=wsm[j][:], in_=g.uv_wTb[j * 128:(j + 1) * 128, 2 * E1:2 * E1 + S1])
            ac_sil = {}
            for tq in range(NQ):
                qs = slice(tq * QW, (tq + 1) * QW)
                if tq == 0:
                    TE.after(evXA, evWM)
                if tq >= 2:
                    TE.after((g.AC, ac_sil[tq - 2]))
                for j in range(8):
                    TE.matmul(out=ps1[tq % 2][:], lhsT=wsm[j][:], rhs=xafm[j][:, qs],
                              start=(j == 0), stop=(j == 7))
                A.after((g.PE, g.PE.n)).activation(out=basef[:, qs], in_=ps1[tq % 2][:],
                                                   func=AF.Silu, bias=g.c_uvb[:, 32:33])
                ac_sil[tq] = g.AC.n
            for tq in range(NQ):
                qs = slice(tq * QW, (tq + 1) * QW)
                snq = sn[:, qs]
                csq = cs[:, qs]
                for gi, (dst, gcol, bcol) in enumerate(((qb, 0, 2), (kb, 1, 3))):
                    V.after((g.AC, ac_sil[tq]), evSC).tensor_scalar(
                        out=qkq[:], in0=basef[:, qs], scalar1=g.c_gam[:, gcol:gcol + 1],
                        scalar2=g.c_gam[:, bcol:bcol + 1], op0=ALU.mult, op1=ALU.add)
                    V.tensor_copy(out=rr2[:], in_=qkq[64:128, :])
                    V.tensor_tensor(out=rr1[:], in0=qkq[0:64, :], in1=csq, op=ALU.mult)
                    V.tensor_tensor(out=rr3[:], in0=rr2[:], in1=snq, op=ALU.mult)
                    V.tensor_tensor(out=rr1[:], in0=rr1[:], in1=rr3[:], op=ALU.subtract)
                    V.tensor_tensor(out=rr3[:], in0=rr2[:], in1=csq, op=ALU.mult)
                    V.tensor_tensor(out=rr2[:], in0=qkq[0:64, :], in1=snq, op=ALU.mult)
                    V.tensor_tensor(out=rr3[:], in0=rr3[:], in1=rr2[:], op=ALU.add)
                    V.tensor_copy(out=dst[0:64, qs], in_=rr1[:])
                    V.tensor_copy(out=dst[64:128, qs], in_=rr3[:])
            dv_rope = g.DV.n

        vstack = ExitStack()
        wsl = [[vstack.enter_context(nc.sbuf_tensor(f"pf{b}_wsl{i}_{j}", [128, 512], BF16)) for j in range(8)]
               for i in range(2)]
        gd_wsl = {}
        dv_v = {}
        GQ.after((g.DV, dv_rope))
        for ec4 in range(4):
            if ec4 < 2:
                for j in range(8):
                    GQ.to(evWS[ec4]).dma_start(out=wsl[ec4][j][:],
                                 in_=g.uv_wTb[j * 128:(j + 1) * 128, E1 + ec4 * 512:E1 + (ec4 + 1) * 512])
        for ec4 in range(4):
            es = slice(ec4 * 512, (ec4 + 1) * 512)
            wp = ec4 % 2
            if ec4 >= 2:
                pass
            for t in range(NT):
                i = ec4 * NT + t
                pp = i % 2
                if i == 0:
                    TE.after(evWS[ec4 % 2], evUV, (g.DV, dv_rope))
                elif t == 0:
                    TE.after(evWS[ec4 % 2])
                if i >= 2:
                    TE.after((g.DV, dv_v[i - 2]))
                for j in range(8):
                    TE.matmul(out=ps1[pp][:], lhsT=xafm[j][:, t * 128:(t + 1) * 128],
                              rhs=wsl[wp][j][:], start=(j == 0), stop=(j == 7))
                pe_now = g.PE.n
                V.after((g.PE, pe_now), evUV).tensor_tensor(out=(scr if pp == 0 else scr2)[:],
                                                      in0=ps1[pp][:], in1=uvbv[:, es], op=ALU.add)
                dv_v[i] = g.DV.n
                A.after((g.DV, dv_v[i])).activation(out=vtm[t][:, es],
                                                    in_=(scr if pp == 0 else scr2)[:], func=AF.Silu)
            if ec4 < 2:
                GQ.after((g.PE, g.PE.n))
                if ec4 + 2 < 4:
                    for j in range(8):
                        GQ.to(evWS[wp]).dma_start(out=wsl[wp][j][:],
                                     in_=g.uv_wTb[j * 128:(j + 1) * 128, E1 + (ec4 + 2) * 512:E1 + (ec4 + 3) * 512])
        ac_vlast = g.AC.n
        pe_vlast = g.PE.n
        vstack.close()

        bias2 = [en(nc.sbuf_tensor(f"pf{b}_bias{i}", [128, QW], F32)) for i in range(2)]
        wsm2 = [[en(nc.sbuf_tensor(f"pf{b}_wsm2{i}_{j}", [128, 128], BF16)) for j in range(8)]
                for i in range(2)]
        wof = [en(nc.sbuf_tensor(f"pf{b}_wof{ec}", [128, 512], BF16)) for ec in range(16)]
        obb = en(nc.sbuf_tensor(f"pf{b}_obb", [128, 512], F32))
        ub = [en(nc.sbuf_tensor(f"pf{b}_ub{i}", [128, QW], BF16)) for i in range(2)]
        x1t = [en(nc.sbuf_tensor(f"pf{b}_x1t{i}", [128, 512], F32)) for i in range(2)]
        o1 = [en(nc.sbuf_tensor(f"pf{b}_o1{i}", [128, 512], F32)) for i in range(2)]
        TE.after((g.AC, ac_vlast))
        dv_kt_last = None
        for tq in range(NQ):
            qs = slice(tq * QW, (tq + 1) * QW)
            pl_prev = g.PL.n
            ld_bias = {}
            dv_kt = {}
            pl_kt = {}
            if dv_kt_last is not None:
                SY.after((g.DV, dv_kt_last))
            else:
                SY.after((g.PE, pe_vlast))
            for j in range(2):
                SY.to(evB[j]).dma_start(out=bias2[j][:], in_=g.biasT[j * 128:(j + 1) * 128, qs])
            for j in range(NT):
                p = j % 2
                if j >= 2:
                    TE.after((g.DV, dv_kt[j - 2]))
                TE.matmul(out=ps3[p][:], lhsT=kb[:, j * 128:(j + 1) * 128], rhs=qb[:, qs],
                          start=True, stop=True)
                pe_s = g.PE.n
                if j < 2:
                    V.after((g.AC, ac_vlast), (g.PL, pl_prev))
                else:
                    V.after((g.PL, pl_kt[j - 2]))
                V.after((g.PE, pe_s), evB[p]).scalar_tensor_tensor(
                    out=(scr if p == 0 else scr2)[:], in0=ps3[p][:], scalar=RT, in1=bias2[p][:],
                    op0=ALU.mult, op1=ALU.add)
                dv_kt[j] = g.DV.n
                if j + 2 < NT:
                    SY.after((g.DV, dv_kt[j]))
                    SY.to(evB[p]).dma_start(out=bias2[p][:], in_=g.biasT[(j + 2) * 128:(j + 3) * 128, qs])
                P.after((g.DV, dv_kt[j])).tensor_scalar_max(
                    out=kT[j][:], in0=(scr if p == 0 else scr2)[:], scalar1=0.0)
                P.tensor_tensor(out=kT[j][:], in0=kT[j][:], in1=(scr if p == 0 else scr2)[:],
                                op=ALU.mult)
                pl_kt[j] = g.PL.n
                if j == NT - 1:
                    dv_kt_last = dv_kt[j]
            for ec in range(2):
                for j in range(8):
                    GQ.to(evU2[ec]).dma_start(out=wsm2[ec][j][:],
                                 in_=g.uv_wTb[j * 128:(j + 1) * 128, ec * 128:(ec + 1) * 128])
            ac_u = {}
            dv_p = {}
            for ec in range(16):
                p = ec % 2
                TE.after(evU2[p])
                if ec >= 2:
                    TE.after((g.AC, ac_u[ec - 2]))
                if ec == 0:
                    TE.after((g.PL, pl_kt[NT - 1]))
                for j in range(8):
                    TE.matmul(out=ps1[p][:], lhsT=wsm2[p][j][:], rhs=xafm[j][:, qs],
                              start=(j == 0), stop=(j == 7))
                pe_u = g.PE.n
                A.after((g.PE, pe_u)).activation(out=ub[p][:], in_=ps1[p][:],
                                                 func=AF.Silu, bias=g.c_uvb[:, ec:ec + 1])
                ac_u[ec] = g.AC.n
                if ec >= 2:
                    TE.after((g.DV, dv_p[ec - 2]))
                for j in range(NT):
                    TE.matmul(out=ps3[p][:], lhsT=vtm[j][:, ec * 128:(ec + 1) * 128], rhs=kT[j][:],
                              start=(j == 0), stop=(j == NT - 1))
                pe_o = g.PE.n
                V.after((g.PE, pe_o), (g.AC, ac_u[ec])).tensor_tensor(
                    out=Pq[ec][:], in0=ps3[p][:], in1=ub[p][:], op=ALU.mult)
                dv_p[ec] = g.DV.n
                if ec + 2 < 16:
                    GQ.after((g.PE, pe_u))
                    for j in range(8):
                        GQ.to(evU2[p]).dma_start(out=wsm2[p][j][:],
                                     in_=g.uv_wTb[j * 128:(j + 1) * 128, (ec + 2) * 128:(ec + 3) * 128])
            dv_o1 = {}
            for fh in range(2):
                GQ.after((g.PE, g.PE.n))
                for ec in range(16):
                    GQ.to(evWF).dma_start(out=wof[ec][:], in_=g.o_wTb[ec * 128:(ec + 1) * 128, fh * 512:(fh + 1) * 512])
                SY.after((g.DV, g.DV.n))
                SY.to(evOB).dma_start(out=obb[:], in_=g.o_b[0:1, fh * 512:(fh + 1) * 512].broadcast_to([128, 512]))
                for trel in range(4):
                    tt = tq * 4 + trel
                    p = trel % 2
                    ts_ = slice(tt * 128, (tt + 1) * 128)
                    prevkey = (fh, trel - 2) if trel >= 2 else ((0, trel + 2) if fh == 1 else None)
                    if prevkey is not None and prevkey in dv_o1:
                        SY.after((g.DV, dv_o1[prevkey]))
                    SY.to(evO[p]).dma_start(out=x1t[p][:], in_=g.x1p[b, ts_, fh * 512:(fh + 1) * 512])
                    SY.to(evO[p]).dma_start(out=o1[p][:], in_=g.xn1_f32[b, ts_, fh * 512:(fh + 1) * 512])
                    TE.after(evWF)
                    if prevkey is not None and prevkey in dv_o1:
                        TE.after((g.DV, dv_o1[prevkey]))
                    if fh == 0 and trel == 0:
                        TE.after((g.DV, dv_p[15]))
                    for ec in range(16):
                        TE.matmul(out=ps5[p][:], lhsT=Pq[ec][:, trel * 128:(trel + 1) * 128],
                                  rhs=wof[ec][:], start=(ec == 0), stop=(ec == 15))
                    pe_f = g.PE.n
                    V.after((g.PE, pe_f), evO[p], evOB, g.ST).tensor_tensor(
                        out=o1[p][:], in0=ps5[p][:], in1=o1[p][:], op=ALU.add)
                    V.tensor_tensor(out=o1[p][:], in0=o1[p][:], in1=obb[:], op=ALU.add)
                    V.tensor_tensor(out=o1[p][:], in0=o1[p][:], in1=x1t[p][:], op=ALU.add)
                    dv_o1[(fh, trel)] = g.DV.n
                    SY.to(g.ST).after((g.DV, dv_o1[(fh, trel)])).dma_start(
                        out=g.x2p[b, ts_, fh * 512:(fh + 1) * 512], in_=o1[p][:])
    g.barrier()


def phase_g(g):
    nc = g.nc
    SY, SD, GQ, V, A, P, TE = g.SY, g.SD, g.GQ, g.V, g.A, g.P, g.TE
    NSL = CAP // 128
    from contextlib import ExitStack
    with ExitStack() as ph:
        en = ph.enter_context
        w1b = [en(nc.sbuf_tensor(f"pg_w1{j}", [128, DFF], BF16)) for j in range(8)]
        w2b = [en(nc.sbuf_tensor(f"pg_w2{j}", [128, D], BF16)) for j in range(32)]
        cnts = en(nc.sbuf_tensor("pg_cnts", [8, 8], F32))
        cmv = en(nc.sbuf_tensor("pg_cmv", [8, 1], F32))
        orow = en(nc.sbuf_tensor("pg_orow", [1, 8], F32))
        crow = en(nc.sbuf_tensor("pg_crow", [1, 8], F32))
        obc = en(nc.sbuf_tensor("pg_obc", [128, 8], F32))
        mrow = en(nc.sbuf_tensor("pg_mrow", [1, 8], F32))
        mecol = en(nc.sbuf_tensor("pg_mecol", [128, 1], F32))
        colz = en(nc.sbuf_tensor("pg_colz", [128, 8], F32))
        t8 = en(nc.sbuf_tensor("pg_t8", [128, 8], F32))
        srcu = en(nc.sbuf_tensor("pg_srcu", [128, 1], U32))
        idu = en(nc.sbuf_tensor("pg_idu", [128, 1], U32))
        idsf = en(nc.sbuf_tensor("pg_idsf", [128, 1], F32))
        ein = [en(nc.sbuf_tensor(f"pg_ein{i}", [128, D], BF16)) for i in range(4)]
        einfm = en(nc.sbuf_tensor("pg_einfm", [128, 8 * 512], BF16))
        ps_t8 = en(nc.psum_tensor("pg_pst8", [128, 1024], BF16))
        hb = [en(nc.sbuf_tensor(f"pg_hb{fc}", [128, 512], BF16)) for fc in range(32)]
        eoutb = en(nc.sbuf_tensor("pg_eoutb", [128, D], BF16))
        ps_h = [en(nc.psum_tensor(f"pg_psh{i}", [128, 512], F32)) for i in range(2)]
        ps_e = [en(nc.psum_tensor(f"pg_pse{i}", [128, 512], F32)) for i in range(2)]
        ps_8 = en(nc.psum_tensor("pg_ps8", [1, 16], F32))
        ps_b = en(nc.psum_tensor("pg_psb", [128, 8], F32))

        for j in range(8):
            GQ.dma_start(out=w1b[j][:], in_=g.W1e[j * 128:(j + 1) * 128, :])
        for j in range(32):
            GQ.dma_start(out=w2b[j][:], in_=g.W2e[j * 128:(j + 1) * 128, :])
        SY.after(g.CC0).dma_start(out=cnts[:], in_=g.cc_cnt_out)
        V.after(g.LD).tensor_scalar(out=mrow[:], in0=g.c_iota8[0:1, :],
                                    scalar1=g.c_cvec[0:1, 2:3], scalar2=None, op0=ALU.is_equal)
        TE.after((g.DV, g.DV.n)).matmul(out=ps_b[0:8, :], lhsT=g.c_ones1f[0:1, 0:8], rhs=mrow[:],
                                        start=True, stop=True)
        V.after((g.PE, g.PE.n)).tensor_tensor(out=cnts[:], in0=cnts[:], in1=ps_b[0:8, :], op=ALU.mult)
        V.reduce_sum(cmv[:], cnts[:], axis=AX.X)
        TE.after((g.DV, g.DV.n))
        TE.matmul(out=ps_8[0:1, 0:8], lhsT=cmv[:], rhs=g.c_s8[:, 0:8], start=True, stop=True)
        TE.matmul(out=ps_8[0:1, 8:16], lhsT=cmv[:], rhs=g.c_s8[:, 8:16], start=True, stop=True)
        V.after((g.PE, g.PE.n)).tensor_copy(out=orow[:], in_=ps_8[0:1, 0:8])
        V.tensor_copy(out=crow[:], in_=ps_8[0:1, 8:16])
        TE.after((g.DV, g.DV.n)).matmul(out=ps_b[:], lhsT=g.c_ones1f[:], rhs=orow[:],
                                        start=True, stop=True)
        V.after((g.PE, g.PE.n)).tensor_copy(out=obc[:], in_=ps_b[:])
        TE.after((g.DV, g.DV.n)).matmul(out=ps_b[:, 0:1], lhsT=g.c_ones1f[:], rhs=g.c_cvec[0:1, 2:3],
                                        start=True, stop=True)
        V.after((g.PE, g.PE.n)).tensor_scalar(out=mecol[:], in0=ps_b[:, 0:1], scalar1=8192.0,
                                              scalar2=None, op0=ALU.mult)

        GQ.after(g.CC1, g.CC2)
        for st in range(NSL):
            q = st % 4
            V.after(g.GD, g.SC, (g.PE, g.PE.n))
            V.tensor_scalar(out=colz[:, 0:1], in0=g.c_iotap[:], scalar1=float(st * 128),
                            scalar2=None, op0=ALU.add)
            V.tensor_scalar(out=t8[:], in0=obc[:], scalar1=colz[:, 0:1], scalar2=None, op0=ALU.is_le)
            V.tensor_scalar(out=colz[:, 1:2], in0=colz[:, 0:1], scalar1=0.0, scalar2=None,
                            op0=ALU.mult)
            V.reduce_sum(colz[:, 1:2], t8[:], axis=AX.X)
            V.tensor_scalar(out=colz[:, 1:2], in0=colz[:, 1:2], scalar1=-1.0, scalar2=None,
                            op0=ALU.add)
            V.tensor_scalar(out=t8[:], in0=g.c_iota8[:], scalar1=colz[:, 1:2], scalar2=None,
                            op0=ALU.is_equal)
            V.tensor_tensor(out=t8[:], in0=t8[:], in1=obc[:], op=ALU.mult)
            V.reduce_sum(colz[:, 2:3], t8[:], axis=AX.X)
            V.tensor_tensor(out=colz[:, 3:4], in0=colz[:, 0:1], in1=colz[:, 2:3], op=ALU.subtract)
            V.tensor_scalar(out=colz[:, 4:5], in0=colz[:, 1:2], scalar1=65536.0, scalar2=None,
                            op0=ALU.mult)
            V.tensor_tensor(out=colz[:, 4:5], in0=colz[:, 4:5], in1=mecol[:], op=ALU.add)
            V.tensor_tensor(out=colz[:, 4:5], in0=colz[:, 4:5], in1=colz[:, 3:4], op=ALU.add)
            V.tensor_scalar(out=colz[:, 4:5], in0=colz[:, 4:5], scalar1=float(8 * E0 * 8192 - 1),
                            scalar2=None, op0=ALU.min)
            V.tensor_copy(out=srcu[:], in_=colz[:, 4:5])
            GQ.after(g.DV)
            GQ.indirect_dma_start(out=idu[:], out_offset=None, in_=g.lists_all,
                                  in_offset=IndirectOffsetOnAxis(ap=srcu[:], axis=0))
            V.after(g.GD).tensor_copy(out=idsf[:], in_=idu[:])
            V.tensor_scalar(out=idsf[:], in0=idsf[:], scalar1=float(NCORES * BPC * T - 1),
                            scalar2=None, op0=ALU.min)
            V.tensor_copy(out=idu[:], in_=idsf[:])
            GQ.after(g.DV)
            GQ.indirect_dma_start(out=ein[q][:], out_offset=None, in_=g.xn1_all,
                                  in_offset=IndirectOffsetOnAxis(ap=idu[:], axis=0))
            if q == 3:
                ch = st // 4
                for tt4 in range(4):
                    TE.after(g.GD, (g.DV, g.DV.n))
                    for j in range(8):
                        TE.transpose(out=ps_t8[:, j * 128:(j + 1) * 128],
                                     in_=ein[tt4][:, j * 128:(j + 1) * 128],
                                     identity=g.c_ident_bf[:])
                    V.after((g.PE, g.PE.n)).tensor_copy(
                        out=einfm.rearrange("p (j c) -> p j c", j=8)
                        [:, :, tt4 * 128:(tt4 + 1) * 128],
                        in_=ps_t8[:].rearrange("p (j c) -> p j c", j=8))
                ac_h = {}
                dv_tt = {}
                for fc in range(32):
                    p = fc % 2
                    if fc == 0:
                        TE.after((g.DV, g.DV.n))
                    if fc >= 2:
                        TE.after((g.AC, ac_h[fc - 2]))
                    for j in range(8):
                        TE.matmul(out=ps_h[p][:], lhsT=w1b[j][:, fc * 128:(fc + 1) * 128],
                                  rhs=einfm[:, j * 512:(j + 1) * 512], start=(j == 0), stop=(j == 7))
                    A.after((g.PE, g.PE.n)).activation(out=hb[fc][:], in_=ps_h[p][:], func=AF.Relu)
                    ac_h[fc] = g.AC.n
                for tt4 in range(4):
                    for fh in range(2):
                        i6 = tt4 * 2 + fh
                        p = i6 % 2
                        if i6 == 0:
                            TE.after((g.AC, ac_h[31]))
                        if i6 >= 2:
                            TE.after((g.DV, dv_tt[i6 - 2]))
                        for fc in range(32):
                            TE.matmul(out=ps_e[p][:],
                                      lhsT=hb[fc][:, tt4 * 128:(tt4 + 1) * 128],
                                      rhs=w2b[fc][:, fh * 512:(fh + 1) * 512],
                                      start=(fc == 0), stop=(fc == 31))
                        V.after((g.PE, g.PE.n), g.ST).tensor_copy(
                            out=eoutb[:, fh * 512:(fh + 1) * 512], in_=ps_e[p][:])
                        dv_tt[i6] = g.DV.n
                    SY.to(g.ST).after((g.DV, dv_tt[tt4 * 2 + 1])).dma_start(
                        out=g.eout_pub[(ch * 4 + tt4) * 128:(ch * 4 + tt4 + 1) * 128, :],
                        in_=eoutb[:])
        g.ST.wait(nc.gpsimd)
        g.CC3.inc(GQ.raw.collective_compute(
            "AllGather", ALU.bypass, replica_groups=g.rg,
            ins=[g.eout_pub.opt()], outs=[g.eout_all.opt()]))
    g.barrier()


def phase_h(g):
    nc = g.nc
    SY, GQ, V, A, P = g.SY, g.GQ, g.V, g.A, g.P
    from contextlib import ExitStack
    with ExitStack() as ph:
        en = ph.enter_context
        x2t = [en(nc.sbuf_tensor(f"ph_x2{i}", [128, D], F32)) for i in range(2)]
        xnt = [en(nc.sbuf_tensor(f"ph_xn{i}", [128, D], F32)) for i in range(2)]
        gtb = [en(nc.sbuf_tensor(f"ph_gt{i}", [128, D], BF16)) for i in range(2)]
        gf = en(nc.sbuf_tensor("ph_gf", [128, D], F32))
        ot = [en(nc.sbuf_tensor(f"ph_ot{i}", [128, D], F32)) for i in range(2)]
        slotu = en(nc.sbuf_tensor("ph_slotu", [128, BPC * NT], U32))
        fac = en(nc.sbuf_tensor("ph_fac", [128, 2 * BPC * NT], F32))
        evL = [g.ev(ph, f"ph_evL{i}") for i in range(2)]
        evG = [g.ev(ph, f"ph_evG{i}") for i in range(2)]
        NTILE = BPC * NT
        V.tensor_scalar(out=fac[:, 0:NTILE], in0=g.slotf[:], scalar1=float(8 * CAP - 1),
                        scalar2=None, op0=ALU.min)
        V.tensor_copy(out=slotu[:], in_=fac[:, 0:NTILE])
        V.tensor_tensor(out=fac[:, 0:NTILE], in0=g.route[:, 2::3], in1=g.keptf[:], op=ALU.mult)
        V.tensor_scalar(out=fac[:, NTILE:], in0=g.keptf[:], scalar1=-1.0, scalar2=1.0,
                        op0=ALU.mult, op1=ALU.add)
        dv_pre = g.DV.n
        dv_cmb = {}
        for gt in range(NTILE + 2):
            p = gt % 2
            if gt >= 2:
                go = gt - 2
                q = go % 2
                b2, tt2 = divmod(go, NT)
                ts2 = slice(tt2 * 128, (tt2 + 1) * 128)
                V.after(evG[q], evL[q], g.ST)
                V.tensor_copy(out=gf[:], in_=gtb[q][:])
                V.tensor_scalar(out=gf[:], in0=gf[:], scalar1=fac[:, go:go + 1], scalar2=None, op0=ALU.mult)
                V.tensor_scalar(out=ot[q][:], in0=xnt[q][:], scalar1=fac[:, NTILE + go:NTILE + go + 1],
                                scalar2=None, op0=ALU.mult)
                V.tensor_tensor(out=ot[q][:], in0=ot[q][:], in1=gf[:], op=ALU.add)
                V.tensor_tensor(out=ot[q][:], in0=ot[q][:], in1=x2t[q][:], op=ALU.add)
                dv_cmb[go] = g.DV.n
                SY.to(g.ST).after((g.DV, dv_cmb[go])).dma_start(out=g.out[b2, ts2, :], in_=ot[q][:])
            if gt < NTILE:
                b, tt = divmod(gt, NT)
                ts_ = slice(tt * 128, (tt + 1) * 128)
                if gt >= 2:
                    SY.after((g.DV, dv_cmb[gt - 2]))
                SY.to(evL[p]).dma_start(out=x2t[p][:], in_=g.x2p[b, ts_, :])
                SY.to(evL[p]).dma_start(out=xnt[p][:], in_=g.xn1_f32[b, ts_, :])
                if gt == 0:
                    GQ.after((g.DV, dv_pre), g.CC3)
                if gt >= 2:
                    GQ.after((g.DV, dv_cmb[gt - 2]))
                GQ.to(evG[p]).indirect_dma_start(
                    out=gtb[p][:], out_offset=None, in_=g.eout_all,
                    in_offset=IndirectOffsetOnAxis(ap=slotu[:, gt:gt + 1], axis=0))
    g.barrier()


def finish(g):
    nc = g.nc
    nc.all_engine_barrier()
    g.LD.wait(nc.sync)
    g.LD2.wait(nc.sync)
    g.GD.wait(nc.sync)
    g.SC.wait(nc.sync)
    g.ST.wait(nc.sync)



def kernel(**inputs):
    in_maps = _host_prep(inputs)
    nc = build_program()
    res = run_bass_kernel_spmd(nc, in_maps, core_ids=list(range(NCORES)))
    outs = [np.asarray(r["out"], np.float32).reshape(BPC, T, D) for r in res.results]
    full = np.concatenate(outs, axis=0)
    kernel._last_results = res
    return full
